# version 13
# baseline (speedup 1.0000x reference)
import numpy as np
import ml_dtypes
from contextlib import ExitStack

import concourse.bass as bass
import concourse.mybir as mybir
from concourse.bass_utils import run_bass_kernel_spmd

F32 = mybir.dt.float32
BF16 = mybir.dt.bfloat16
AF = mybir.ActivationFunctionType
ALU = mybir.AluOpType
AX = mybir.AxisListType

S, DM, NH, DK = 4096, 1024, 16, 64
NB = S // 128
NSB = NB // 4
NC_ = DM // 128
NE, HID = 8, 4096
DEPTH_SCALE = float(1.0 / np.sqrt(12.0))
NEG = -1.0e9
EPS = 1e-5

TG = 1024
NTG = S // TG
NQ = 4
TSB = TG // 512


class Buf:
    __slots__ = ("ap", "w", "r")

    def __init__(self, ap):
        self.ap = ap
        self.w = None
        self.r = {}


class Prog:
    ENGS = ("sync", "gpsimd", "vector", "scalar", "tensor")

    def __init__(self, nc, es):
        self.nc = nc
        self.es = es
        self.q = {e: [] for e in self.ENGS}
        self.sems = {}
        self.cnt = {}
        for name in ("pe", "ve", "se", "po", "dx", "dg", "do", "pb"):
            self.mksem(name)

    def mksem(self, name):
        if name not in self.sems:
            self.sems[name] = self.es.enter_context(self.nc.semaphore(name))
            self.cnt[name] = 0
        return name

    def emit(self, eng, fn, *a, waits=(), inc=None, inc_n=1, **kw):
        ws = [w for w in waits if w is not None]
        incs = []
        ms = None
        if inc is not None:
            self.cnt[inc] += inc_n
            ms = (inc, self.cnt[inc])
            incs.append((inc, inc_n))
        self.q[eng].append([fn, a, kw, ws, incs])
        return ms

    def wait_only(self, eng, waits):
        ws = [w for w in waits if w is not None]
        if ws:
            self.q[eng].append([None, (), {}, ws, []])

    def add_inc_to_last(self, eng, sem, n):
        assert self.q[eng], eng
        self.q[eng][-1][4].append((sem, n))
        self.cnt[sem] += n
        return (sem, self.cnt[sem])

    def replay(self, eng_name, eng):
        seen = {}
        for fn, a, kw, ws, incs in self.q[eng_name]:
            for (s, v) in ws:
                if seen.get(s, -1) >= v:
                    continue
                seen[s] = v
                eng.wait_ge(self.sems[s], v)
            if fn is None:
                continue
            inst = getattr(eng, fn)(*a, **kw)
            for (s, n) in incs:
                inst.then_inc(self.sems[s], n)


def RW(reads=(), writes=()):
    ws = []
    for b in reads:
        if b.w:
            ws.append(b.w)
    for b in writes:
        if b.w:
            ws.append(b.w)
        ws.extend(b.r.items())
    return ws


def setw(bufs, ms):
    for b in bufs:
        b.w = ms
        b.r = {}


def setr(bufs, ms):
    if ms is None:
        return
    for b in bufs:
        old = b.r.get(ms[0], -1)
        if ms[1] > old:
            b.r[ms[0]] = ms[1]


def build_nc(debug=False):
    nc = bass.Bass("TRN2", target_bir_lowering=False)

    x_h = nc.dram_tensor("x", [S, DM], F32, kind="ExternalInput").ap()
    out_h = nc.dram_tensor("out", [S, DM], F32, kind="ExternalOutput").ap()
    wq_h = nc.dram_tensor("wq", [DM, DM], F32, kind="ExternalInput").ap()
    wk_h = nc.dram_tensor("wk", [DM, DM], F32, kind="ExternalInput").ap()
    wv_h = nc.dram_tensor("wv", [DM, DM], F32, kind="ExternalInput").ap()
    wo_h = nc.dram_tensor("wo", [DM, DM], F32, kind="ExternalInput").ap()
    sl_h = nc.dram_tensor("slopes", [NH], F32, kind="ExternalInput").ap()
    wr_h = nc.dram_tensor("wr", [DM, NE], F32, kind="ExternalInput").ap()
    wg_h = nc.dram_tensor("wg", [NE, DM, HID], F32, kind="ExternalInput").ap()
    wu_h = nc.dram_tensor("wu", [NE, DM, HID], F32, kind="ExternalInput").ap()
    wd_h = nc.dram_tensor("wd", [NE, HID, DM], F32, kind="ExternalInput").ap()
    relm_h = nc.dram_tensor("relm", [128, 384], F32, kind="ExternalInput").ap()
    bmf_h = nc.dram_tensor("bm_f", [128, 384], F32, kind="ExternalInput").ap()
    bml_h = nc.dram_tensor("bm_l", [128, 384], F32, kind="ExternalInput").ap()
    iden_h = nc.dram_tensor("iden", [128, 128], BF16, kind="ExternalInput").ap()

    scr_kind = "ExternalOutput" if debug else "Internal"
    x1_h = nc.dram_tensor("x1_scr", [S, DM], F32, kind=scr_kind).ap()
    xn2_h = nc.dram_tensor("xn2_scr", [NB, NC_, 128, 128], BF16, kind=scr_kind).ap()
    comb_h = (nc.dram_tensor("comb_scr", [128, NB * NE], F32, kind="ExternalOutput").ap()
              if debug else None)
    if debug:
        dbg_xn1 = nc.dram_tensor("dbg_xn1", [128, 4096], BF16, kind="ExternalOutput").ap()
        dbg_k = nc.dram_tensor("dbg_k", [128, 4096], BF16, kind="ExternalOutput").ap()
        dbg_q = nc.dram_tensor("dbg_q", [128, 4096], BF16, kind="ExternalOutput").ap()
        dbg_v = nc.dram_tensor("dbg_v", [128, 2048], BF16, kind="ExternalOutput").ap()
        dbg_s2 = nc.dram_tensor("dbg_s2", [128, 384], F32, kind="ExternalOutput").ap()
        dbg_pA = nc.dram_tensor("dbg_pA", [128, 384], BF16, kind="ExternalOutput").ap()
        dbg_A = nc.dram_tensor("dbg_A", [128, 384], BF16, kind="ExternalOutput").ap()
        dbg_st = nc.dram_tensor("dbg_st", [128, 8], F32, kind="ExternalOutput").ap()

    es = ExitStack()
    p = Prog(nc, es)

    def sb(name, shape, dt=F32, stack=es):
        return Buf(stack.enter_context(nc.sbuf_tensor(name, shape, dt)))

    def psum(name, shape, dt=F32, stack=es):
        return Buf(stack.enter_context(nc.psum_tensor(name, shape, dt)))

    iden = sb("s_iden", [128, 128], BF16)
    bm_ = {"f": sb("s_bmf", [128, 384]), "l": sb("s_bml", [128, 384])}
    relmb = sb("s_relm", [128, 384])
    hbias = sb("s_hbias", [128, NH * 384])
    slbc = sb("slbc", [128, NH])
    comb = sb("comb", [128, NB * NE])
    wrhi = sb("wrhi", [128, NC_ * NE], BF16)
    wrlo = sb("wrlo", [128, NC_ * NE], BF16)

    esA = ExitStack()
    swq = sb("swq", [128, NC_ * DM], BF16, esA)
    swk = sb("swk", [128, NC_ * DM], BF16, esA)
    swv = sb("swv", [128, NC_ * DM], BF16, esA)
    swo = sb("swo", [128, NC_ * DM], BF16, esA)
    swrf = sb("swrf", [128, NC_ * NE], F32, esA)
    NXR = 5
    x_ring = [sb(f"xr{i}", [128, DM], F32, esA) for i in range(NXR)]
    xnst = [sb("xnst0", [128, DM], BF16, esA)] * 2
    sqdmp = xnst
    xn1r = [sb("xn1r0", [128, NC_ * 512], BF16, esA)] * 2
    k_ring = [sb(f"kr{i}", [128, NC_ * 512], BF16, esA) for i in range(2)]
    q_ring = [sb(f"qr{i}", [128, NC_ * 512], BF16, esA) for i in range(2)]
    v_ring = [sb(f"vr{i}", [128, DM], BF16, esA) for i in range(6)]
    spA = [sb(f"pA{i}", [128, 768], BF16, esA) for i in range(2)]
    s2m = [sb(f"s2m{i}", [128, 768], F32, esA) for i in range(2)]
    sA = [sb("A0", [128, 768], BF16, esA)] * 2
    sAT = [sb("AT0", [128, 768], BF16, esA)] * 2
    attnf = [sb(f"attnf{i}", [128, DM], BF16, esA) for i in range(2)]
    x1b = [sb(f"x1b{i}", [128, DM], F32, esA) for i in range(2)]
    xn2f32 = [sb("xn2f0", [128, DM], F32, esA)] * 2
    xn2hit = [sb("xn2ht0", [128, DM], BF16, esA)] * 2
    xn2lot = [sb("xn2lt0", [128, DM], BF16, esA)] * 2
    xn2hif = [sb("xn2hf0", [128, DM], BF16, esA)] * 2
    xn2lof = [sb("xn2lf0", [128, DM], BF16, esA)] * 2

    def statset(tag):
        return {k: sb(f"st_{tag}_{k}", [128, 1], F32, esA)
                for k in ("sum", "sq", "mean", "mm2", "veps", "var", "lnv", "rstd", "nmr")}

    st1 = [statset(f"a{i}") for i in range(2)]
    st2 = [statset(f"b{i}") for i in range(2)]
    slog = [sb(f"rlog{i}", [128, NE], F32, esA) for i in range(2)]
    sind1 = [sb(f"rind1{i}", [128, NE], F32, esA) for i in range(2)]
    slk = [sb(f"rlk{i}", [128, NE], F32, esA) for i in range(2)]
    sind2 = [sb(f"rind2{i}", [128, NE], F32, esA) for i in range(2)]
    sm1 = [sb(f"sxm1{i}", [128, 1], F32, esA) for i in range(2)]
    sm2 = [sb(f"sxm2{i}", [128, 2], F32, esA) for i in range(2)]
    sd_ = [sb(f"sxd{i}", [128, 2], F32, esA) for i in range(2)]
    rm1 = [sb(f"rm1{i}", [128, 1], F32, esA) for i in range(2)]
    rm2 = [sb(f"rm2{i}", [128, 1], F32, esA) for i in range(2)]
    rdd = [sb(f"rd{i}", [128, 1], F32, esA) for i in range(2)]
    se_ = [sb(f"re{i}", [128, 1], F32, esA) for i in range(2)]
    st_ = [sb(f"rt{i}", [128, 1], F32, esA) for i in range(2)]
    sw1 = [sb(f"rw1{i}", [128, 1], F32, esA) for i in range(2)]
    sw1s = [sb(f"rw1s{i}", [128, 1], F32, esA) for i in range(2)]
    sw2s = [sb(f"rw2s{i}", [128, 1], F32, esA) for i in range(2)]
    sc1_ = [sb(f"rc1{i}", [128, NE], F32, esA) for i in range(2)]

    psA = ExitStack()
    pp = [psum(f"pp{i}", [128, 512], F32, psA) for i in range(2)]
    ps_sc = [psum(f"psc{i}", [128, 1024], F32, psA) for i in range(2)]
    ps_tr = psum("ptr", [128, 1024], BF16, psA)
    pav_t = psA.enter_context(nc.psum_tensor("pav", [128, 128], F32))
    pav_lo = Buf(pav_t)
    pav_hi = Buf(pav_t)
    pav_all = (pav_lo, pav_hi)

    ms = p.emit("sync", "dma_start", out=iden.ap[:], in_=iden_h[:], inc=p.mksem("did"), inc_n=16)
    setw([iden], ms)
    for key, hbm in (("f", bmf_h), ("l", bml_h)):
        ms = p.emit("sync", "dma_start", out=bm_[key].ap[:], in_=hbm[:],
                    inc=p.mksem(f"dbm{key}"), inc_n=16)
        setw([bm_[key]], ms)
    ms = p.emit("sync", "dma_start", out=relmb.ap[:], in_=relm_h[:],
                inc=p.mksem("dre"), inc_n=16)
    setw([relmb], ms)
    ms = p.emit("sync", "dma_start", out=slbc.ap[:],
                in_=sl_h.unsqueeze(0).broadcast_to([128, NH]), inc=p.mksem("dsl"), inc_n=16)
    setw([slbc], ms)
    wstage = k_ring[0]
    for h in range(NH):
        ms = p.emit("scalar", "activation",
                    hbias.ap[:, h * 384:(h + 1) * 384], relmb.ap[:],
                    AF.Identity, scale=slbc.ap[:, h:h + 1],
                    waits=RW([relmb, slbc], [hbias]), inc="se")
        setw([hbias], ms)
    setr([relmb, slbc], ms)
    for w_hbm, w_sb in ((wq_h, swq), (wk_h, swk), (wv_h, swv), (wo_h, swo)):
        for half in range(2):
            ms = p.emit("gpsimd", "dma_start",
                        out=wstage.ap[:].rearrange("p (oc c) -> p oc c", c=DM),
                        in_=w_hbm[half * 512:(half + 1) * 512, :].rearrange(
                            "(oc p) c -> p oc c", p=128),
                        waits=RW((), [wstage]), inc=p.mksem("dwst"), inc_n=16)
            setw([wstage], ms)
            for dc in range(NC_):
                m = None
                for ocl in range(4):
                    w = RW([wstage, iden], [ps_tr]) if ocl == 0 else ()
                    m = p.emit("tensor", "transpose",
                               ps_tr.ap[:, ocl * 128:(ocl + 1) * 128],
                               wstage.ap[:, ocl * DM + dc * 128:ocl * DM + (dc + 1) * 128],
                               iden.ap[:], waits=w, inc="pe" if ocl == 3 else None)
                setw([ps_tr], m)
                setr([wstage, iden], m)
                m2 = p.emit("vector", "tensor_copy",
                            w_sb.ap[:, dc * DM + half * 512:dc * DM + (half + 1) * 512],
                            ps_tr.ap[:, 0:512],
                            waits=RW([ps_tr], [w_sb]), inc="ve")
                setw([w_sb], m2)
                setr([ps_tr], m2)
    ms = p.emit("sync", "dma_start",
                out=swrf.ap[:].rearrange("p (dc c) -> p dc c", c=NE),
                in_=wr_h.rearrange("(dc p) c -> p dc c", p=128),
                inc=p.mksem("dwr"), inc_n=16)
    setw([swrf], ms)
    ms = p.emit("vector", "tensor_copy", wrhi.ap[:], swrf.ap[:],
                waits=RW([swrf], [wrhi]), inc="ve")
    setw([wrhi], ms)
    setr([swrf], ms)
    ms = p.emit("vector", "tensor_sub", wrlo.ap[:], swrf.ap[:], wrhi.ap[:],
                waits=RW([swrf, wrhi], [wrlo]), inc="ve")
    setw([wrlo], ms)
    setr([swrf, wrhi], ms)
    for t in range(4):
        ms = p.emit("sync", "dma_start", out=x_ring[t].ap[:],
                    in_=x_h[t * 128:(t + 1) * 128, :],
                    inc=p.mksem(f"dxx{t % NXR}"), inc_n=16)
        setw([x_ring[t]], ms)

    def emit_ln(xbuf, stats, out_bf, sqd, out_f32=None):
        m = p.emit("vector", "reduce_sum", stats["sum"].ap[:], xbuf.ap[:], axis=AX.X,
                   waits=RW([xbuf], [stats["sum"]]), inc="ve")
        setw([stats["sum"]], m); setr([xbuf], m)
        m = p.emit("scalar", "activation", sqd.ap[:], xbuf.ap[:], AF.Square,
                   accum_out=stats["sq"].ap[:],
                   waits=RW([xbuf], [sqd, stats["sq"]]), inc="se")
        setw([sqd, stats["sq"]], m); setr([xbuf], m)
        m = p.emit("vector", "tensor_scalar_mul", stats["mean"].ap[:],
                   stats["sum"].ap[:], 1.0 / DM,
                   waits=RW([stats["sum"]], [stats["mean"]]), inc="ve")
        setw([stats["mean"]], m); setr([stats["sum"]], m)
        m = p.emit("vector", "tensor_mul", stats["mm2"].ap[:], stats["mean"].ap[:],
                   stats["mean"].ap[:], waits=RW([stats["mean"]], [stats["mm2"]]), inc="ve")
        setw([stats["mm2"]], m); setr([stats["mean"]], m)
        m = p.emit("vector", "tensor_scalar", stats["veps"].ap[:],
                   stats["sq"].ap[:], 1.0 / DM, EPS, ALU.mult, ALU.add,
                   waits=RW([stats["sq"]], [stats["veps"]]), inc="ve")
        setw([stats["veps"]], m); setr([stats["sq"]], m)
        m = p.emit("vector", "tensor_sub", stats["var"].ap[:],
                   stats["veps"].ap[:], stats["mm2"].ap[:],
                   waits=RW([stats["veps"], stats["mm2"]], [stats["var"]]), inc="ve")
        setw([stats["var"]], m); setr([stats["veps"], stats["mm2"]], m)
        m = p.emit("scalar", "activation", stats["lnv"].ap[:], stats["var"].ap[:],
                   AF.Ln, waits=RW([stats["var"]], [stats["lnv"]]), inc="se")
        setw([stats["lnv"]], m); setr([stats["var"]], m)
        m = p.emit("scalar", "activation", stats["rstd"].ap[:], stats["lnv"].ap[:],
                   AF.Exp, scale=-0.5, waits=RW([stats["lnv"]], [stats["rstd"]]), inc="se")
        setw([stats["rstd"]], m); setr([stats["lnv"]], m)
        m = p.emit("vector", "scalar_tensor_tensor", stats["nmr"].ap[:],
                   stats["mean"].ap[:], -1.0, stats["rstd"].ap[:], ALU.mult, ALU.mult,
                   waits=RW([stats["mean"], stats["rstd"]], [stats["nmr"]]), inc="ve")
        setw([stats["nmr"]], m); setr([stats["mean"], stats["rstd"]], m)
        m = p.emit("scalar", "activation", out_bf.ap[:], xbuf.ap[:], AF.Identity,
                   bias=stats["nmr"].ap[:], scale=stats["rstd"].ap[:],
                   waits=RW([xbuf, stats["nmr"], stats["rstd"]], [out_bf]), inc="se")
        setw([out_bf], m); setr([xbuf, stats["nmr"], stats["rstd"]], m)
        if out_f32 is not None:
            m = p.emit("scalar", "activation", out_f32.ap[:], xbuf.ap[:], AF.Identity,
                       bias=stats["nmr"].ap[:], scale=stats["rstd"].ap[:],
                       waits=RW([xbuf, stats["nmr"], stats["rstd"]], [out_f32]), inc="se")
            setw([out_f32], m); setr([xbuf, stats["nmr"], stats["rstd"]], m)

    def emit_transpose8(src_tm, dst_fm, dst_ap3, evac_eng):
        m = None
        for dc in range(NC_):
            w = RW([src_tm, iden], [ps_tr]) if dc == 0 else ()
            m = p.emit("tensor", "transpose", ps_tr.ap[:, dc * 128:(dc + 1) * 128],
                       src_tm.ap[:, dc * 128:(dc + 1) * 128], iden.ap[:],
                       waits=w, inc="pe" if dc == NC_ - 1 else None)
        setw([ps_tr], m)
        setr([src_tm, iden], m)
        sem = "ve" if evac_eng == "vector" else "se"
        fn = "tensor_copy" if evac_eng == "vector" else "copy"
        m2 = p.emit(evac_eng, fn, dst_ap3,
                    ps_tr.ap[:].rearrange("p (dc c) -> p dc c", c=128),
                    waits=RW([ps_tr], [dst_fm]), inc=sem)
        setw([dst_fm], m2)
        setr([ps_tr], m2)

    def emit_proj_fm(w_sb, xn_slot, dst_slot, scale=None):
        for dout_c in range(NC_):
            pb = pp[dout_c % 2]
            m = None
            for dci in range(NC_):
                w = RW([w_sb, xn_slot], [pb]) if dci == 0 else ()
                m = p.emit("tensor", "matmul", pb.ap[:, 0:512],
                           w_sb.ap[:, dci * DM + dout_c * 128:dci * DM + (dout_c + 1) * 128],
                           xn_slot.ap[:, dci * 512:(dci + 1) * 512],
                           start=(dci == 0), stop=(dci == NC_ - 1),
                           waits=w, inc="pe" if dci == NC_ - 1 else None)
            setw([pb], m)
            setr([w_sb, xn_slot], m)
            dst = dst_slot.ap[:, dout_c * 512:(dout_c + 1) * 512]
            if scale is None:
                m2 = p.emit("vector", "tensor_copy", dst, pb.ap[:, 0:512],
                            waits=RW([pb], [dst_slot]), inc="ve")
            else:
                m2 = p.emit("scalar", "activation", dst, pb.ap[:, 0:512], AF.Copy,
                            scale=scale, waits=RW([pb], [dst_slot]), inc="se")
            setw([dst_slot], m2)
            setr([pb], m2)

    def emit_v_tm(xn_slot, sbk, blocks):
        for t in range(4):
            b = blocks[t]
            vslot = v_ring[b % 6]
            for half in range(2):
                pb = pp[half]
                m = None
                for dci in range(NC_):
                    w = RW([swv, xn_slot], [pb]) if dci == 0 else ()
                    m = p.emit("tensor", "matmul", pb.ap[:, 0:512],
                               xn_slot.ap[:, dci * 512 + t * 128:dci * 512 + (t + 1) * 128],
                               swv.ap[:, dci * DM + half * 512:dci * DM + (half + 1) * 512],
                               start=(dci == 0), stop=(dci == NC_ - 1),
                               waits=w, inc="pe" if dci == NC_ - 1 else None)
                setw([pb], m)
                setr([swv, xn_slot], m)
                m2 = p.emit("scalar", "copy", vslot.ap[:, half * 512:(half + 1) * 512],
                            pb.ap[:, 0:512], waits=RW([pb], [vslot]), inc="se")
                setw([vslot], m2)
                setr([pb], m2)

    def emit_attention(qb):
        par_q = (qb // 4) % 2
        kbs = [kb for kb in (qb - 1, qb, qb + 1) if 0 <= kb < NB]
        j0 = kbs[0] - (qb - 1)
        nk = len(kbs)
        edge = None if 0 < qb < NB - 1 else ("f" if qb == 0 else "l")
        qslot = q_ring[par_q]
        kslots = [k_ring[(kb // 4) % 2] for kb in kbs]
        for pr in range(NH // 2):
            par = pr % 2
            dc = pr
            psb = ps_sc[par]
            s2 = s2m[par]
            mz = None
            if edge is not None:
                e0, e1 = (0, 128) if edge == "f" else (256, 384)
                mz = p.emit("vector", "memset", psb.ap[:, e0:e1], 0.0,
                            waits=RW((), [psb]), inc="ve")
                mz = p.emit("vector", "memset", psb.ap[:, 512 + e0:512 + e1], 0.0,
                            inc="ve")
                setw([psb], mz)
            m = None
            first = True
            for hh in range(2):
                h = 2 * pr + hh
                row = (h % 2) * 64
                for j, kb in enumerate(kbs):
                    ksl = kslots[j]
                    w = RW([qslot, ksl], [psb]) if first else RW([ksl])
                    first = False
                    m = p.emit("tensor", "matmul",
                               psb.ap[:, hh * 512 + (j0 + j) * 128:hh * 512 + (j0 + j + 1) * 128],
                               qslot.ap[row:row + 64, dc * 512 + (qb % 4) * 128:dc * 512 + (qb % 4 + 1) * 128],
                               ksl.ap[row:row + 64, dc * 512 + (kb % 4) * 128:dc * 512 + (kb % 4 + 1) * 128],
                               start=True, stop=True,
                               inc="pe" if (hh == 1 and j == nk - 1) else None,
                               waits=w)
            setw([psb], m)
            setr([qslot], m)
            for ksl in kslots:
                setr([ksl], m)
            hb3 = hbias.ap[:, 2 * pr * 384:(2 * pr + 2) * 384].rearrange(
                "p (hh c) -> p hh c", c=384)
            ps3 = psb.ap[:].rearrange("p (hh c) -> p hh c", c=512)[:, :, 0:384]
            mma = p.emit("vector", "scalar_tensor_tensor",
                         s2.ap[:].rearrange("p (hh c) -> p hh c", c=384),
                         ps3, 1.0, hb3, ALU.mult, ALU.add,
                         waits=RW([psb, hbias], [s2]), inc="ve")
            setw([s2], mma)
            setr([psb, hbias], mma)
            if edge is not None:
                bm3 = bm_[edge].ap[:].unsqueeze(1).broadcast_to([128, 2, 384])
                mma = p.emit("vector", "tensor_add",
                             s2.ap[:].rearrange("p (hh c) -> p hh c", c=384),
                             s2.ap[:].rearrange("p (hh c) -> p hh c", c=384), bm3,
                             waits=RW([bm_[edge]], [s2]), inc="ve")
                setw([s2], mma)
                setr([bm_[edge]], mma)
            mrm = p.emit("vector", "reduce_max", sm1[par].ap[:], s2.ap[:],
                         axis=AX.X, negate=True,
                         waits=RW([s2], [sm1[par]]), inc="ve")
            setw([sm1[par]], mrm)
            mex = p.emit("scalar", "activation", spA[par].ap[:], s2.ap[:], AF.Exp,
                         bias=sm1[par].ap[:],
                         waits=RW([s2, sm1[par]], [spA[par]]), inc="se")
            setw([spA[par]], mex)
            setr([s2, sm1[par]], mex)
            mrs = p.emit("vector", "reduce_sum", sm2[par].ap[:],
                         spA[par].ap[:].rearrange("p (hh c) -> p hh c", c=384),
                         axis=AX.X, waits=RW([spA[par]], [sm2[par]]), inc="ve")
            setw([sm2[par]], mrs)
            mri = p.emit("vector", "reciprocal", sd_[par].ap[:], sm2[par].ap[:],
                         waits=RW([sm2[par]], [sd_[par]]), inc="ve")
            setw([sd_[par]], mri)
            setr([sm2[par]], mri)
            rinv3 = sd_[par].ap[:].unsqueeze(2).broadcast_to([128, 2, 384])
            mA = p.emit("vector", "tensor_mul",
                        sA[par].ap[:].rearrange("p (hh c) -> p hh c", c=384),
                        spA[par].ap[:].rearrange("p (hh c) -> p hh c", c=384), rinv3,
                        waits=RW([spA[par], sd_[par]], [sA[par]]), inc="ve")
            setw([sA[par]], mA)
            setr([spA[par], sd_[par]], mA)
            m = None
            ntr = 0
            for hh in range(2):
                for j in range(nk):
                    ntr += 1
                    w = RW([sA[par], iden], [ps_tr]) if ntr == 1 else ()
                    m = p.emit("tensor", "transpose",
                               ps_tr.ap[:, hh * 384 + j * 128:hh * 384 + (j + 1) * 128],
                               sA[par].ap[:, hh * 384 + (j0 + j) * 128:hh * 384 + (j0 + j + 1) * 128],
                               iden.ap[:], waits=w,
                               inc="pe" if ntr == 2 * nk else None)
            setw([ps_tr], m)
            setr([sA[par], iden], m)
            mat = p.emit("scalar", "copy", sAT[par].ap[:, 0:768], ps_tr.ap[:, 0:768],
                         waits=RW([ps_tr], [sAT[par]]), inc="se")
            setw([sAT[par]], mat)
            setr([ps_tr], mat)
            m = None
            for hh in range(2):
                h = 2 * pr + hh
                row = (h % 2) * 64
                pav_half = pav_all[hh]
                for j, kb in enumerate(kbs):
                    vsl = v_ring[kb % 6]
                    w = RW([sAT[par], vsl], [pav_half]) if j == 0 else RW([vsl])
                    m = p.emit("tensor", "matmul", pav_t[row:row + 64, 0:128],
                               vsl.ap[:, h * 64:(h + 1) * 64],
                               sAT[par].ap[:, hh * 384 + j * 128:hh * 384 + (j + 1) * 128],
                               start=(j == 0), stop=(j == nk - 1),
                               waits=w, inc="pe" if j == nk - 1 else None)
                setw([pav_all[hh]], m)
                for kb in kbs:
                    setr([v_ring[kb % 6]], m)
            setr([sAT[par]], m)
            mev = p.emit("vector", "tensor_copy",
                         attnf[qb % 2].ap[:, dc * 128:(dc + 1) * 128],
                         pav_t[:, 0:128],
                         waits=RW([pav_lo, pav_hi], [attnf[qb % 2]]), inc="ve")
            setw([attnf[qb % 2]], mev)
            setr([pav_lo, pav_hi], mev)

    def emit_post(qb):
        af = attnf[qb % 2]
        x1 = x1b[qb % 2]
        xs = x_ring[qb % NXR]
        for half in range(2):
            pb = pp[half]
            m = None
            for dc in range(NC_):
                w = RW([af, swo], [pb]) if dc == 0 else ()
                m = p.emit("tensor", "matmul", pb.ap[:, 0:512],
                           af.ap[:, dc * 128:(dc + 1) * 128],
                           swo.ap[:, dc * DM + half * 512:dc * DM + (half + 1) * 512],
                           start=(dc == 0), stop=(dc == NC_ - 1),
                           waits=w, inc="pe" if dc == NC_ - 1 else None)
            setw([pb], m)
            setr([af, swo], m)
            m2 = p.emit("vector", "tensor_add", x1.ap[:, half * 512:(half + 1) * 512],
                        pb.ap[:, 0:512], xs.ap[:, half * 512:(half + 1) * 512],
                        waits=RW([pb, xs], [x1]), inc="ve")
            setw([x1], m2)
            setr([pb, xs], m2)
        mdx = p.emit("sync", "dma_start", out=x1_h[qb * 128:(qb + 1) * 128, :],
                     in_=x1.ap[:], waits=RW([x1]), inc=p.mksem(f"dx1_{qb % 2}"),
                     inc_n=16)
        setr([x1], mdx)
        emit_ln(x1, st2[qb % 2], xn2hit[qb % 2], sqdmp[qb % 2], out_f32=xn2f32[qb % 2])
        mlo = p.emit("vector", "tensor_sub", xn2lot[qb % 2].ap[:],
                     xn2f32[qb % 2].ap[:], xn2hit[qb % 2].ap[:],
                     waits=RW([xn2f32[qb % 2], xn2hit[qb % 2]], [xn2lot[qb % 2]]),
                     inc="ve")
        setw([xn2lot[qb % 2]], mlo)
        setr([xn2f32[qb % 2], xn2hit[qb % 2]], mlo)
        hi = xn2hif[qb % 2]
        lo = xn2lof[qb % 2]
        emit_transpose8(xn2hit[qb % 2], hi,
                        hi.ap[:].rearrange("p (dc c) -> p dc c", c=128), "vector")
        emit_transpose8(xn2lot[qb % 2], lo,
                        lo.ap[:].rearrange("p (dc c) -> p dc c", c=128), "scalar")
        mdn = p.emit("sync", "dma_start",
                     out=xn2_h[qb].rearrange("dc p c -> p dc c"),
                     in_=hi.ap[:].rearrange("p (dc c) -> p dc c", c=128),
                     waits=RW([hi]), inc=p.mksem("dxn"), inc_n=16)
        setr([hi], mdn)
        m = None
        first = True
        trips = ([(hi, wrhi)] * NC_ + [(hi, wrlo)] * NC_ + [(lo, wrhi)] * NC_)
        for i, (xf, wr_s) in enumerate(trips):
            dc = i % NC_
            w = RW([hi, lo, wrhi, wrlo], [pav_lo, pav_hi]) if first else ()
            m = p.emit("tensor", "matmul", pav_t[:, 0:NE],
                       xf.ap[:, dc * 128:(dc + 1) * 128],
                       wr_s.ap[:, dc * NE:(dc + 1) * NE],
                       start=first, stop=(i == len(trips) - 1),
                       waits=w, inc="pe" if i == len(trips) - 1 else None)
            first = False
        setw([pav_lo, pav_hi], m)
        setr([hi, lo, wrhi, wrlo], m)
        par = qb % 2
        mlg = p.emit("vector", "tensor_copy", slog[par].ap[:], pav_t[:, 0:NE],
                     waits=RW([pav_lo, pav_hi], [slog[par]]), inc="ve")
        setw([slog[par]], mlg)
        setr([pav_lo, pav_hi], mlg)
        mm1 = p.emit("vector", "reduce_max", rm1[par].ap[:], slog[par].ap[:], axis=AX.X,
                     waits=RW([slog[par]], [rm1[par]]), inc="ve")
        setw([rm1[par]], mm1)
        mi1 = p.emit("vector", "tensor_single_scalar", sind1[par].ap[:], slog[par].ap[:],
                     rm1[par].ap[:], ALU.is_ge,
                     waits=RW([slog[par], rm1[par]], [sind1[par]]), inc="ve")
        setw([sind1[par]], mi1)
        setr([rm1[par]], mi1)
        mlk = p.emit("vector", "scalar_tensor_tensor", slk[par].ap[:], sind1[par].ap[:],
                     -1.0e30, slog[par].ap[:], ALU.mult, ALU.add,
                     waits=RW([sind1[par], slog[par]], [slk[par]]), inc="ve")
        setw([slk[par]], mlk)
        setr([sind1[par], slog[par]], mlk)
        mm2_ = p.emit("vector", "reduce_max", rm2[par].ap[:], slk[par].ap[:], axis=AX.X,
                      waits=RW([slk[par]], [rm2[par]]), inc="ve")
        setw([rm2[par]], mm2_)
        mi2 = p.emit("vector", "tensor_single_scalar", sind2[par].ap[:], slk[par].ap[:],
                     rm2[par].ap[:], ALU.is_ge,
                     waits=RW([slk[par], rm2[par]], [sind2[par]]), inc="ve")
        setw([sind2[par]], mi2)
        setr([slk[par], rm2[par]], mi2)
        md = p.emit("vector", "tensor_sub", rdd[par].ap[:], rm2[par].ap[:], rm1[par].ap[:],
                    waits=RW([rm2[par], rm1[par]], [rdd[par]]), inc="ve")
        setw([rdd[par]], md)
        setr([rm2[par], rm1[par]], md)
        me = p.emit("scalar", "activation", se_[par].ap[:], rdd[par].ap[:], AF.Exp,
                    waits=RW([rdd[par]], [se_[par]]), inc="se")
        setw([se_[par]], me)
        setr([rdd[par]], me)
        mt = p.emit("vector", "tensor_scalar_add", st_[par].ap[:], se_[par].ap[:], 1.0,
                    waits=RW([se_[par]], [st_[par]]), inc="ve")
        setw([st_[par]], mt)
        mw1 = p.emit("vector", "reciprocal", sw1[par].ap[:], st_[par].ap[:],
                     waits=RW([st_[par]], [sw1[par]]), inc="ve")
        setw([sw1[par]], mw1)
        setr([st_[par]], mw1)
        mw1s = p.emit("vector", "tensor_scalar_mul", sw1s[par].ap[:], sw1[par].ap[:],
                      DEPTH_SCALE, waits=RW([sw1[par]], [sw1s[par]]), inc="ve")
        setw([sw1s[par]], mw1s)
        setr([sw1[par]], mw1s)
        mw2s = p.emit("vector", "tensor_mul", sw2s[par].ap[:], se_[par].ap[:],
                      sw1s[par].ap[:], waits=RW([se_[par], sw1s[par]], [sw2s[par]]),
                      inc="ve")
        setw([sw2s[par]], mw2s)
        setr([se_[par]], mw2s)
        mc1 = p.emit("vector", "tensor_single_scalar", sc1_[par].ap[:], sind1[par].ap[:],
                     sw1s[par].ap[:], ALU.mult,
                     waits=RW([sind1[par], sw1s[par]], [sc1_[par]]), inc="ve")
        setw([sc1_[par]], mc1)
        setr([sind1[par], sw1s[par]], mc1)
        mcb = p.emit("vector", "scalar_tensor_tensor",
                     comb.ap[:, qb * NE:(qb + 1) * NE], sind2[par].ap[:],
                     sw2s[par].ap[:], sc1_[par].ap[:], ALU.mult, ALU.add,
                     waits=RW([sind2[par], sw2s[par], sc1_[par]], [comb]), inc="ve")
        setw([comb], mcb)
        setr([sind2[par], sw2s[par], sc1_[par]], mcb)

    for sbk in range(NSB):
        blocks = [4 * sbk + t for t in range(4)]
        xn1 = xn1r[sbk % 2]
        for t in range(4):
            b = blocks[t]
            emit_ln(x_ring[b % NXR], st1[t % 2], xnst[t % 2], sqdmp[t % 2])
            dst3 = xn1.ap[:].rearrange("p (dc t c) -> p dc t c", dc=NC_, t=4, c=128)[:, :, t, :]
            emit_transpose8(xnst[t % 2], xn1, dst3, "vector")
        emit_proj_fm(swk, xn1, k_ring[sbk % 2])
        emit_v_tm(xn1, sbk, blocks)
        emit_proj_fm(swq, xn1, q_ring[sbk % 2], scale=0.125)
        if debug and sbk == 0:
            for src_b, dst in ((xn1r[0], dbg_xn1), (k_ring[0], dbg_k),
                               (q_ring[0], dbg_q)):
                mdbg = p.emit("sync", "dma_start", out=dst[:], in_=src_b.ap[:],
                              waits=RW([src_b]), inc="dx", inc_n=16)
                setr([src_b], mdbg)
            for i in range(2):
                mdbg = p.emit("sync", "dma_start", out=dbg_v[:, i*1024:(i+1)*1024],
                              in_=v_ring[i].ap[:], waits=RW([v_ring[i]]),
                              inc="dx", inc_n=16)
                setr([v_ring[i]], mdbg)
        qbs = [0, 1, 2] if sbk == 0 else [4 * sbk - 1, 4 * sbk, 4 * sbk + 1, 4 * sbk + 2]
        for qb in qbs:
            emit_attention(qb)
            emit_post(qb)
        if sbk + 1 < NSB:
            for t in range(4):
                b = 4 * (sbk + 1) + t
                xr = x_ring[b % NXR]
                m = p.emit("sync", "dma_start", out=xr.ap[:],
                           in_=x_h[b * 128:(b + 1) * 128, :],
                           waits=RW((), [xr]), inc=p.mksem(f"dxx{b % NXR}"), inc_n=16)
                setw([xr], m)
    emit_attention(NB - 1)
    emit_post(NB - 1)
    if debug:
        m = p.emit("sync", "dma_start", out=comb_h[:], in_=comb.ap[:],
                   waits=RW([comb]), inc="dx", inc_n=16)
        setr([comb], m)

    sync_sems = [s for s in p.cnt if s.startswith(("dxx", "dx1")) or
                 s in ("dx", "dxn", "did", "dre", "dsl", "dwr")]
    own = {"sync": sync_sems, "gpsimd": ["dg", "dwst", "po"], "vector": ["ve"],
           "scalar": ["se"], "tensor": ["pe"]}
    for eng in Prog.ENGS:
        p.wait_only(eng, [(s, p.cnt[s]) for s in own[eng] if s in p.cnt])
        p.emit(eng, "sem_inc", p.sems["pb"], 1)
        p.cnt["pb"] += 1
    barrier = ("pb", p.cnt["pb"])
    for eng in Prog.ENGS:
        p.wait_only(eng, [barrier])

    psA.close()
    esA.close()

    esB = ExitStack()
    sxn2 = [sb(f"sxn2_{dc}", [128, TG], BF16, esB) for dc in range(NC_)]
    oacc = [sb(f"oacc{t}", [128, DM], F32, esB) for t in range(TG // 128)]
    swgb = [sb(f"swgb{i}", [128, NC_ * 1024], BF16, esB) for i in range(2)]
    swub = [sb(f"swub{i}", [128, NC_ * 1024], BF16, esB) for i in range(2)]
    swdb = [sb(f"swdb{i}", [128, NC_ * 1024], BF16, esB) for i in range(2)]
    hbuf = [[sb(f"h{s}_{i}", [128, 512], BF16, esB) for i in range(8)] for s in range(2)]
    ssil = [sb(f"ssil{i}", [128, 512], F32, esB) for i in range(2)]

    psB = ExitStack()
    pg = [psum(f"pg{i}", [128, 512], F32, psB) for i in range(2)]
    pu = [psum(f"pu{i}", [128, 512], F32, psB) for i in range(2)]
    peo = [psum(f"peo{i}", [128, 1024], F32, psB) for i in range(2)]

    chunk_idx = 0
    for g in range(NTG):
        for dc in range(NC_):
            m = p.emit("sync", "dma_start",
                       out=sxn2[dc].ap[:].rearrange("p (t c) -> p t c", c=128),
                       in_=xn2_h[g * (TG // 128):(g + 1) * (TG // 128), dc].rearrange("t p c -> p t c"),
                       waits=RW((), [sxn2[dc]]), inc=p.mksem(f"dsx{dc}"), inc_n=16)
            setw([sxn2[dc]], m)
        for tt in range(TG // 128):
            row0 = (g * (TG // 128) + tt) * 128
            m = p.emit("sync", "dma_start", out=oacc[tt].ap[:],
                       in_=x1_h[row0:row0 + 128, :],
                       waits=RW((), [oacc[tt]]), inc=p.mksem(f"doa{tt}"), inc_n=16)
            setw([oacc[tt]], m)
        for e in range(NE):
            for q in range(NQ):
                par = chunk_idx % 2
                wgb, wub, wdb = swgb[par], swub[par], swdb[par]
                mwg = p.emit("gpsimd", "dma_start",
                             out=wgb.ap[:].rearrange("p (dc c) -> p dc c", c=1024),
                             in_=wg_h[e][:, q * 1024:(q + 1) * 1024].rearrange(
                                 "(dc p) c -> p dc c", p=128),
                             waits=RW((), [wgb]), inc=p.mksem(f"dwg{par}"), inc_n=16)
                setw([wgb], mwg)
                mwu = p.emit("gpsimd", "dma_start",
                             out=wub.ap[:].rearrange("p (dc c) -> p dc c", c=1024),
                             in_=wu_h[e][:, q * 1024:(q + 1) * 1024].rearrange(
                                 "(dc p) c -> p dc c", p=128),
                             waits=RW((), [wub]), inc=p.mksem(f"dwu{par}"), inc_n=16)
                setw([wub], mwu)
                mwd = p.emit("gpsimd", "dma_start",
                             out=wdb.ap[:].rearrange("p (hc c) -> p hc c", c=1024),
                             in_=wd_h[e][q * 1024:(q + 1) * 1024, :].rearrange(
                                 "(hc p) c -> p hc c", p=128),
                             waits=RW((), [wdb]), inc=p.mksem(f"dwd{par}"), inc_n=16)
                setw([wdb], mwd)
                for ts in range(TSB):
                    hset = hbuf[ts % 2]
                    for hc in range(8):
                        pgb, pub = pg[hc % 2], pu[hc % 2]
                        m = None
                        for dc in range(NC_):
                            w = RW([wgb, sxn2[dc]], [pgb]) if dc == 0 else RW([sxn2[dc]])
                            m = p.emit("tensor", "matmul", pgb.ap[:, 0:512],
                                       wgb.ap[:, dc * 1024 + hc * 128:dc * 1024 + (hc + 1) * 128],
                                       sxn2[dc].ap[:, ts * 512:(ts + 1) * 512],
                                       start=(dc == 0), stop=(dc == NC_ - 1),
                                       waits=w, inc="pe" if dc == NC_ - 1 else None)
                        setw([pgb], m)
                        setr([wgb], m)
                        for dc in range(NC_):
                            setr([sxn2[dc]], m)
                        m = None
                        for dc in range(NC_):
                            w = RW([wub], [pub]) if dc == 0 else ()
                            m = p.emit("tensor", "matmul", pub.ap[:, 0:512],
                                       wub.ap[:, dc * 1024 + hc * 128:dc * 1024 + (hc + 1) * 128],
                                       sxn2[dc].ap[:, ts * 512:(ts + 1) * 512],
                                       start=(dc == 0), stop=(dc == NC_ - 1),
                                       waits=w, inc="pe" if dc == NC_ - 1 else None)
                        setw([pub], m)
                        setr([wub], m)
                        msil = p.emit("scalar", "activation", ssil[hc % 2].ap[:],
                                      pgb.ap[:, 0:512], AF.Silu,
                                      waits=RW([pgb], [ssil[hc % 2]]), inc="se")
                        setw([ssil[hc % 2]], msil)
                        setr([pgb], msil)
                        mh = p.emit("vector", "tensor_mul", hset[hc].ap[:],
                                    ssil[hc % 2].ap[:], pub.ap[:, 0:512],
                                    waits=RW([ssil[hc % 2], pub], [hset[hc]]), inc="ve")
                        setw([hset[hc]], mh)
                        setr([ssil[hc % 2], pub], mh)
                    for tl in range(4):
                        tt = ts * 4 + tl
                        pe_ = peo[tt % 2]
                        m = None
                        nmm = 0
                        for hc in range(8):
                            for half in range(2):
                                nmm += 1
                                w = RW([hset[hc], wdb], [pe_]) if hc == 0 else RW([hset[hc]])
                                m = p.emit("tensor", "matmul",
                                           pe_.ap[:, half * 512:(half + 1) * 512],
                                           hset[hc].ap[:, tl * 128:(tl + 1) * 128],
                                           wdb.ap[:, hc * 1024 + half * 512:hc * 1024 + (half + 1) * 512],
                                           start=(hc == 0), stop=(hc == 7),
                                           waits=w, inc="pe" if nmm == 16 else None)
                        setw([pe_], m)
                        setr([wdb], m)
                        for hc in range(8):
                            setr([hset[hc]], m)
                        cap = comb.ap[:, (g * (TG // 128) + tt) * NE + e:
                                      (g * (TG // 128) + tt) * NE + e + 1]
                        mev = p.emit("vector", "scalar_tensor_tensor",
                                     oacc[tt].ap[:], pe_.ap[:, 0:1024], cap,
                                     oacc[tt].ap[:], ALU.mult, ALU.add,
                                     waits=RW([pe_, comb, oacc[tt]], [oacc[tt]]),
                                     inc="ve")
                        setw([oacc[tt]], mev)
                        setr([pe_, comb], mev)
                chunk_idx += 1
        for tt in range(TG // 128):
            row0 = (g * (TG // 128) + tt) * 128
            m = p.emit("sync", "dma_start", out=out_h[row0:row0 + 128, :],
                       in_=oacc[tt].ap[:], waits=RW([oacc[tt]]),
                       inc=p.mksem(f"doa{tt}"), inc_n=16)
            setr([oacc[tt]], m)

    p.wait_only("sync", [(f"doa{tt}", p.cnt[f"doa{tt}"]) for tt in range(TG // 128)])

    psB.close()
    esB.close()

    with nc.Block() as block:
        @block.sync
        def _(eng):
            p.replay("sync", eng)

        @block.gpsimd
        def _(eng):
            p.replay("gpsimd", eng)

        @block.vector
        def _(eng):
            p.replay("vector", eng)

        @block.scalar
        def _(eng):
            p.replay("scalar", eng)

        @block.tensor
        def _(eng):
            p.replay("tensor", eng)

    es.close()
    return nc


def _host_constants():
    ql = np.arange(128)[:, None]
    kl = np.arange(384)[None, :]
    band = (kl >= ql) & (kl <= ql + 256)
    mk = lambda b: np.where(b, np.float32(0.0), np.float32(NEG)).astype(np.float32)
    bm_f = mk(band & (kl >= 128))
    bm_l = mk(band & (kl < 256))
    rel = (128 + ql - kl).astype(np.float32)
    relm = np.where(band, rel, np.float32(NEG)).astype(np.float32)
    iden = np.eye(128, dtype=ml_dtypes.bfloat16)
    return relm, bm_f, bm_l, iden


def kernel(L0_attn_x, L0_attn_Wq, L0_attn_Wk, L0_attn_Wv, L0_attn_Wo,
           L0_attn_slopes, L0_ffn_Wr, L0_ffn_Wg, L0_ffn_Wu, L0_ffn_Wd):
    x = np.ascontiguousarray(np.asarray(L0_attn_x, dtype=np.float32))
    relm, bm_f, bm_l, iden = _host_constants()
    common = {
        "wq": np.ascontiguousarray(np.asarray(L0_attn_Wq, dtype=np.float32)),
        "wk": np.ascontiguousarray(np.asarray(L0_attn_Wk, dtype=np.float32)),
        "wv": np.ascontiguousarray(np.asarray(L0_attn_Wv, dtype=np.float32)),
        "wo": np.ascontiguousarray(np.asarray(L0_attn_Wo, dtype=np.float32)),
        "slopes": np.ascontiguousarray(np.asarray(L0_attn_slopes, dtype=np.float32)),
        "wr": np.ascontiguousarray(np.asarray(L0_ffn_Wr, dtype=np.float32)),
        "wg": np.ascontiguousarray(np.asarray(L0_ffn_Wg, dtype=np.float32)),
        "wu": np.ascontiguousarray(np.asarray(L0_ffn_Wu, dtype=np.float32)),
        "wd": np.ascontiguousarray(np.asarray(L0_ffn_Wd, dtype=np.float32)),
        "relm": relm,
        "bm_f": bm_f,
        "bm_l": bm_l,
        "iden": iden,
    }
    nc = build_nc()
    in_maps = [dict(common, x=np.ascontiguousarray(x[i])) for i in range(8)]
    res = run_bass_kernel_spmd(nc, in_maps, core_ids=list(range(8)))
    return np.stack([res.results[i]["out"] for i in range(8)], axis=0)


# version 14
# speedup vs baseline: 1.1073x; 1.1073x over previous
import numpy as np
import ml_dtypes
from contextlib import ExitStack

import concourse.bass as bass
import concourse.mybir as mybir
from concourse.bass_utils import run_bass_kernel_spmd

F32 = mybir.dt.float32
BF16 = mybir.dt.bfloat16
AF = mybir.ActivationFunctionType
ALU = mybir.AluOpType
AX = mybir.AxisListType

S, DM, NH, DK = 4096, 1024, 16, 64
NB = S // 128
NSB = NB // 4
NC_ = DM // 128
NE, HID = 8, 4096
DEPTH_SCALE = float(1.0 / np.sqrt(12.0))
NEG = -1.0e9
EPS = 1e-5

TG = 1024
NTG = S // TG
NQ = 4
TSB = TG // 512


class Buf:
    __slots__ = ("ap", "w", "r")

    def __init__(self, ap):
        self.ap = ap
        self.w = None
        self.r = {}


class Prog:
    ENGS = ("sync", "gpsimd", "vector", "scalar", "tensor")

    def __init__(self, nc, es):
        self.nc = nc
        self.es = es
        self.q = {e: [] for e in self.ENGS}
        self.sems = {}
        self.cnt = {}
        for name in ("pe", "ve", "se", "po", "dx", "dg", "do", "pb"):
            self.mksem(name)

    def mksem(self, name):
        if name not in self.sems:
            self.sems[name] = self.es.enter_context(self.nc.semaphore(name))
            self.cnt[name] = 0
        return name

    def emit(self, eng, fn, *a, waits=(), inc=None, inc_n=1, **kw):
        ws = [w for w in waits if w is not None]
        incs = []
        ms = None
        if inc is not None:
            self.cnt[inc] += inc_n
            ms = (inc, self.cnt[inc])
            incs.append((inc, inc_n))
        self.q[eng].append([fn, a, kw, ws, incs])
        return ms

    def wait_only(self, eng, waits):
        ws = [w for w in waits if w is not None]
        if ws:
            self.q[eng].append([None, (), {}, ws, []])

    def add_inc_to_last(self, eng, sem, n):
        assert self.q[eng], eng
        self.q[eng][-1][4].append((sem, n))
        self.cnt[sem] += n
        return (sem, self.cnt[sem])

    def replay(self, eng_name, eng):
        seen = {}
        for fn, a, kw, ws, incs in self.q[eng_name]:
            for (s, v) in ws:
                if seen.get(s, -1) >= v:
                    continue
                seen[s] = v
                eng.wait_ge(self.sems[s], v)
            if fn is None:
                continue
            inst = getattr(eng, fn)(*a, **kw)
            for (s, n) in incs:
                inst.then_inc(self.sems[s], n)


def RW(reads=(), writes=()):
    ws = []
    for b in reads:
        if b.w:
            ws.append(b.w)
    for b in writes:
        if b.w:
            ws.append(b.w)
        ws.extend(b.r.items())
    return ws


def setw(bufs, ms):
    for b in bufs:
        b.w = ms
        b.r = {}


def setr(bufs, ms):
    if ms is None:
        return
    for b in bufs:
        old = b.r.get(ms[0], -1)
        if ms[1] > old:
            b.r[ms[0]] = ms[1]


def build_nc(debug=False):
    nc = bass.Bass("TRN2", target_bir_lowering=False)

    x_h = nc.dram_tensor("x", [S, DM], F32, kind="ExternalInput").ap()
    out_h = nc.dram_tensor("out", [S, DM], F32, kind="ExternalOutput").ap()
    wq_h = nc.dram_tensor("wq", [DM, DM], F32, kind="ExternalInput").ap()
    wk_h = nc.dram_tensor("wk", [DM, DM], F32, kind="ExternalInput").ap()
    wv_h = nc.dram_tensor("wv", [DM, DM], F32, kind="ExternalInput").ap()
    wo_h = nc.dram_tensor("wo", [DM, DM], F32, kind="ExternalInput").ap()
    sl_h = nc.dram_tensor("slopes", [NH], F32, kind="ExternalInput").ap()
    wr_h = nc.dram_tensor("wr", [DM, NE], F32, kind="ExternalInput").ap()
    wg_h = nc.dram_tensor("wg", [NE, DM, HID], F32, kind="ExternalInput").ap()
    wu_h = nc.dram_tensor("wu", [NE, DM, HID], F32, kind="ExternalInput").ap()
    wd_h = nc.dram_tensor("wd", [NE, HID, DM], F32, kind="ExternalInput").ap()
    relm_h = nc.dram_tensor("relm", [128, 384], F32, kind="ExternalInput").ap()
    bmf_h = nc.dram_tensor("bm_f", [128, 384], F32, kind="ExternalInput").ap()
    bml_h = nc.dram_tensor("bm_l", [128, 384], F32, kind="ExternalInput").ap()
    iden_h = nc.dram_tensor("iden", [128, 128], BF16, kind="ExternalInput").ap()

    scr_kind = "ExternalOutput" if debug else "Internal"
    x1_h = nc.dram_tensor("x1_scr", [S, DM], F32, kind=scr_kind).ap()
    xn2_h = nc.dram_tensor("xn2_scr", [NB, NC_, 128, 128], BF16, kind=scr_kind).ap()
    comb_h = (nc.dram_tensor("comb_scr", [128, NB * NE], F32, kind="ExternalOutput").ap()
              if debug else None)
    if debug:
        dbg_xn1 = nc.dram_tensor("dbg_xn1", [128, 4096], BF16, kind="ExternalOutput").ap()
        dbg_k = nc.dram_tensor("dbg_k", [128, 4096], BF16, kind="ExternalOutput").ap()
        dbg_q = nc.dram_tensor("dbg_q", [128, 4096], BF16, kind="ExternalOutput").ap()
        dbg_v = nc.dram_tensor("dbg_v", [128, 2048], BF16, kind="ExternalOutput").ap()
        dbg_s2 = nc.dram_tensor("dbg_s2", [128, 384], F32, kind="ExternalOutput").ap()
        dbg_pA = nc.dram_tensor("dbg_pA", [128, 384], BF16, kind="ExternalOutput").ap()
        dbg_A = nc.dram_tensor("dbg_A", [128, 384], BF16, kind="ExternalOutput").ap()
        dbg_st = nc.dram_tensor("dbg_st", [128, 8], F32, kind="ExternalOutput").ap()

    es = ExitStack()
    p = Prog(nc, es)

    def sb(name, shape, dt=F32, stack=es):
        return Buf(stack.enter_context(nc.sbuf_tensor(name, shape, dt)))

    def psum(name, shape, dt=F32, stack=es):
        return Buf(stack.enter_context(nc.psum_tensor(name, shape, dt)))

    iden = sb("s_iden", [128, 128], BF16)
    bm_ = {"f": sb("s_bmf", [128, 384]), "l": sb("s_bml", [128, 384])}
    relmb = sb("s_relm", [128, 384])
    hbias = sb("s_hbias", [128, NH * 384])
    slbc = sb("slbc", [128, NH])
    comb = sb("comb", [128, NB * NE])
    wrhi = sb("wrhi", [128, NC_ * NE], BF16)
    wrlo = sb("wrlo", [128, NC_ * NE], BF16)

    esA = ExitStack()
    swq = sb("swq", [128, NC_ * DM], BF16, esA)
    swk = sb("swk", [128, NC_ * DM], BF16, esA)
    swv = sb("swv", [128, NC_ * DM], BF16, esA)
    swo = sb("swo", [128, NC_ * DM], BF16, esA)
    swrf = sb("swrf", [128, NC_ * NE], F32, esA)
    NXR = 5
    x_ring = [sb(f"xr{i}", [128, DM], F32, esA) for i in range(NXR)]
    xnst = [sb("xnst0", [128, DM], BF16, esA)] * 2
    sqdmp = xnst
    xn1r = [sb("xn1r0", [128, NC_ * 512], BF16, esA)] * 2
    k_ring = [sb(f"kr{i}", [128, NC_ * 512], BF16, esA) for i in range(2)]
    q_ring = [sb(f"qr{i}", [128, NC_ * 512], BF16, esA) for i in range(2)]
    v_ring = [sb(f"vr{i}", [128, DM], BF16, esA) for i in range(6)]
    spA = [sb(f"pA{i}", [128, 768], BF16, esA) for i in range(2)]
    sA = [sb(f"A{i}", [128, 768], BF16, esA) for i in range(2)]
    sAT = [sb(f"AT{i}", [128, 768], BF16, esA) for i in range(2)]
    attnf = [sb(f"attnf{i}", [128, DM], BF16, esA) for i in range(2)]
    x1b = [sb(f"x1b{i}", [128, DM], F32, esA) for i in range(2)]
    xn2f32 = [sb("xn2f0", [128, DM], F32, esA)] * 2
    xn2hit = [sb("xn2ht0", [128, DM], BF16, esA)] * 2
    xn2lot = [sb("xn2lt0", [128, DM], BF16, esA)] * 2
    xn2hif = [sb("xn2hf0", [128, DM], BF16, esA)] * 2
    xn2lof = [sb("xn2lf0", [128, DM], BF16, esA)] * 2

    def statset(tag):
        return {k: sb(f"st_{tag}_{k}", [128, 1], F32, esA)
                for k in ("sum", "sq", "mean", "mm2", "veps", "var", "lnv", "rstd", "nmr")}

    st1 = [statset(f"a{i}") for i in range(2)]
    st2 = [statset(f"b{i}") for i in range(2)]
    slog = [sb(f"rlog{i}", [128, NE], F32, esA) for i in range(2)]
    sind1 = [sb(f"rind1{i}", [128, NE], F32, esA) for i in range(2)]
    slk = [sb(f"rlk{i}", [128, NE], F32, esA) for i in range(2)]
    sind2 = [sb(f"rind2{i}", [128, NE], F32, esA) for i in range(2)]
    sm1 = [sb(f"sxm1{i}", [128, 1], F32, esA) for i in range(2)]
    sm2 = [sb(f"sxm2{i}", [128, 2], F32, esA) for i in range(2)]
    sd_ = [sb(f"sxd{i}", [128, 2], F32, esA) for i in range(2)]
    rm1 = [sb(f"rm1{i}", [128, 1], F32, esA) for i in range(2)]
    rm2 = [sb(f"rm2{i}", [128, 1], F32, esA) for i in range(2)]
    rdd = [sb(f"rd{i}", [128, 1], F32, esA) for i in range(2)]
    se_ = [sb(f"re{i}", [128, 1], F32, esA) for i in range(2)]
    st_ = [sb(f"rt{i}", [128, 1], F32, esA) for i in range(2)]
    sw1 = [sb(f"rw1{i}", [128, 1], F32, esA) for i in range(2)]
    sw1s = [sb(f"rw1s{i}", [128, 1], F32, esA) for i in range(2)]
    sw2s = [sb(f"rw2s{i}", [128, 1], F32, esA) for i in range(2)]
    sc1_ = [sb(f"rc1{i}", [128, NE], F32, esA) for i in range(2)]

    psA = ExitStack()
    pp = [psum("pp0", [128, 512], F32, psA)] * 2
    ps_sc = [psum(f"psc{i}", [128, 1024], F32, psA) for i in range(2)]
    ps_trs = [psum(f"ptr{i}", [128, 1024], BF16, psA) for i in range(2)]
    ps_tr = ps_trs[0]
    pav_t = psA.enter_context(nc.psum_tensor("pav", [128, 256], F32))
    pav_slots = [(Buf(pav_t), Buf(pav_t)) for _ in range(2)]

    ms = p.emit("sync", "dma_start", out=iden.ap[:], in_=iden_h[:], inc=p.mksem("did"), inc_n=16)
    setw([iden], ms)
    for key, hbm in (("f", bmf_h), ("l", bml_h)):
        ms = p.emit("sync", "dma_start", out=bm_[key].ap[:], in_=hbm[:],
                    inc=p.mksem(f"dbm{key}"), inc_n=16)
        setw([bm_[key]], ms)
    ms = p.emit("sync", "dma_start", out=relmb.ap[:], in_=relm_h[:],
                inc=p.mksem("dre"), inc_n=16)
    setw([relmb], ms)
    ms = p.emit("sync", "dma_start", out=slbc.ap[:],
                in_=sl_h.unsqueeze(0).broadcast_to([128, NH]), inc=p.mksem("dsl"), inc_n=16)
    setw([slbc], ms)
    wstage = k_ring[0]
    for h in range(NH):
        ms = p.emit("scalar", "activation",
                    hbias.ap[:, h * 384:(h + 1) * 384], relmb.ap[:],
                    AF.Identity, scale=slbc.ap[:, h:h + 1],
                    waits=RW([relmb, slbc], [hbias]), inc="se")
        setw([hbias], ms)
    setr([relmb, slbc], ms)
    for w_hbm, w_sb in ((wq_h, swq), (wk_h, swk), (wv_h, swv), (wo_h, swo)):
        for half in range(2):
            ms = p.emit("gpsimd", "dma_start",
                        out=wstage.ap[:].rearrange("p (oc c) -> p oc c", c=DM),
                        in_=w_hbm[half * 512:(half + 1) * 512, :].rearrange(
                            "(oc p) c -> p oc c", p=128),
                        waits=RW((), [wstage]), inc=p.mksem("dwst"), inc_n=16)
            setw([wstage], ms)
            for dc in range(NC_):
                m = None
                for ocl in range(4):
                    w = RW([wstage, iden], [ps_tr]) if ocl == 0 else ()
                    m = p.emit("tensor", "transpose",
                               ps_tr.ap[:, ocl * 128:(ocl + 1) * 128],
                               wstage.ap[:, ocl * DM + dc * 128:ocl * DM + (dc + 1) * 128],
                               iden.ap[:], waits=w, inc="pe" if ocl == 3 else None)
                setw([ps_tr], m)
                setr([wstage, iden], m)
                m2 = p.emit("vector", "tensor_copy",
                            w_sb.ap[:, dc * DM + half * 512:dc * DM + (half + 1) * 512],
                            ps_tr.ap[:, 0:512],
                            waits=RW([ps_tr], [w_sb]), inc="ve")
                setw([w_sb], m2)
                setr([ps_tr], m2)
    ms = p.emit("sync", "dma_start",
                out=swrf.ap[:].rearrange("p (dc c) -> p dc c", c=NE),
                in_=wr_h.rearrange("(dc p) c -> p dc c", p=128),
                inc=p.mksem("dwr"), inc_n=16)
    setw([swrf], ms)
    ms = p.emit("vector", "tensor_copy", wrhi.ap[:], swrf.ap[:],
                waits=RW([swrf], [wrhi]), inc="ve")
    setw([wrhi], ms)
    setr([swrf], ms)
    ms = p.emit("vector", "tensor_sub", wrlo.ap[:], swrf.ap[:], wrhi.ap[:],
                waits=RW([swrf, wrhi], [wrlo]), inc="ve")
    setw([wrlo], ms)
    setr([swrf, wrhi], ms)
    for t in range(4):
        ms = p.emit("sync", "dma_start", out=x_ring[t].ap[:],
                    in_=x_h[t * 128:(t + 1) * 128, :],
                    inc=p.mksem(f"dxx{t % NXR}"), inc_n=16)
        setw([x_ring[t]], ms)

    def emit_ln(xbuf, stats, out_bf, sqd, out_f32=None):
        m = p.emit("vector", "reduce_sum", stats["sum"].ap[:], xbuf.ap[:], axis=AX.X,
                   waits=RW([xbuf], [stats["sum"]]), inc="ve")
        setw([stats["sum"]], m); setr([xbuf], m)
        m = p.emit("scalar", "activation", sqd.ap[:], xbuf.ap[:], AF.Square,
                   accum_out=stats["sq"].ap[:],
                   waits=RW([xbuf], [sqd, stats["sq"]]), inc="se")
        setw([sqd, stats["sq"]], m); setr([xbuf], m)
        m = p.emit("vector", "tensor_scalar_mul", stats["mean"].ap[:],
                   stats["sum"].ap[:], 1.0 / DM,
                   waits=RW([stats["sum"]], [stats["mean"]]), inc="ve")
        setw([stats["mean"]], m); setr([stats["sum"]], m)
        m = p.emit("vector", "tensor_mul", stats["mm2"].ap[:], stats["mean"].ap[:],
                   stats["mean"].ap[:], waits=RW([stats["mean"]], [stats["mm2"]]), inc="ve")
        setw([stats["mm2"]], m); setr([stats["mean"]], m)
        m = p.emit("vector", "tensor_scalar", stats["veps"].ap[:],
                   stats["sq"].ap[:], 1.0 / DM, EPS, ALU.mult, ALU.add,
                   waits=RW([stats["sq"]], [stats["veps"]]), inc="ve")
        setw([stats["veps"]], m); setr([stats["sq"]], m)
        m = p.emit("vector", "tensor_sub", stats["var"].ap[:],
                   stats["veps"].ap[:], stats["mm2"].ap[:],
                   waits=RW([stats["veps"], stats["mm2"]], [stats["var"]]), inc="ve")
        setw([stats["var"]], m); setr([stats["veps"], stats["mm2"]], m)
        m = p.emit("scalar", "activation", stats["lnv"].ap[:], stats["var"].ap[:],
                   AF.Ln, waits=RW([stats["var"]], [stats["lnv"]]), inc="se")
        setw([stats["lnv"]], m); setr([stats["var"]], m)
        m = p.emit("scalar", "activation", stats["rstd"].ap[:], stats["lnv"].ap[:],
                   AF.Exp, scale=-0.5, waits=RW([stats["lnv"]], [stats["rstd"]]), inc="se")
        setw([stats["rstd"]], m); setr([stats["lnv"]], m)
        m = p.emit("vector", "scalar_tensor_tensor", stats["nmr"].ap[:],
                   stats["mean"].ap[:], -1.0, stats["rstd"].ap[:], ALU.mult, ALU.mult,
                   waits=RW([stats["mean"], stats["rstd"]], [stats["nmr"]]), inc="ve")
        setw([stats["nmr"]], m); setr([stats["mean"], stats["rstd"]], m)
        m = p.emit("scalar", "activation", out_bf.ap[:], xbuf.ap[:], AF.Identity,
                   bias=stats["nmr"].ap[:], scale=stats["rstd"].ap[:],
                   waits=RW([xbuf, stats["nmr"], stats["rstd"]], [out_bf]), inc="se")
        setw([out_bf], m); setr([xbuf, stats["nmr"], stats["rstd"]], m)
        if out_f32 is not None:
            m = p.emit("scalar", "activation", out_f32.ap[:], xbuf.ap[:], AF.Identity,
                       bias=stats["nmr"].ap[:], scale=stats["rstd"].ap[:],
                       waits=RW([xbuf, stats["nmr"], stats["rstd"]], [out_f32]), inc="se")
            setw([out_f32], m); setr([xbuf, stats["nmr"], stats["rstd"]], m)

    def emit_transpose8(src_tm, dst_fm, dst_ap3, evac_eng):
        m = None
        for dc in range(NC_):
            w = RW([src_tm, iden], [ps_tr]) if dc == 0 else ()
            m = p.emit("tensor", "transpose", ps_tr.ap[:, dc * 128:(dc + 1) * 128],
                       src_tm.ap[:, dc * 128:(dc + 1) * 128], iden.ap[:],
                       waits=w, inc="pe" if dc == NC_ - 1 else None)
        setw([ps_tr], m)
        setr([src_tm, iden], m)
        sem = "ve" if evac_eng == "vector" else "se"
        fn = "tensor_copy" if evac_eng == "vector" else "copy"
        m2 = p.emit(evac_eng, fn, dst_ap3,
                    ps_tr.ap[:].rearrange("p (dc c) -> p dc c", c=128),
                    waits=RW([ps_tr], [dst_fm]), inc=sem)
        setw([dst_fm], m2)
        setr([ps_tr], m2)

    def emit_proj_fm(w_sb, xn_slot, dst_slot, scale=None):
        for dout_c in range(NC_):
            pb = pp[dout_c % 2]
            m = None
            for dci in range(NC_):
                w = RW([w_sb, xn_slot], [pb]) if dci == 0 else ()
                m = p.emit("tensor", "matmul", pb.ap[:, 0:512],
                           w_sb.ap[:, dci * DM + dout_c * 128:dci * DM + (dout_c + 1) * 128],
                           xn_slot.ap[:, dci * 512:(dci + 1) * 512],
                           start=(dci == 0), stop=(dci == NC_ - 1),
                           waits=w, inc="pe" if dci == NC_ - 1 else None)
            setw([pb], m)
            setr([w_sb, xn_slot], m)
            dst = dst_slot.ap[:, dout_c * 512:(dout_c + 1) * 512]
            if scale is None:
                m2 = p.emit("vector", "tensor_copy", dst, pb.ap[:, 0:512],
                            waits=RW([pb], [dst_slot]), inc="ve")
            else:
                m2 = p.emit("scalar", "activation", dst, pb.ap[:, 0:512], AF.Copy,
                            scale=scale, waits=RW([pb], [dst_slot]), inc="se")
            setw([dst_slot], m2)
            setr([pb], m2)

    def emit_v_tm(xn_slot, sbk, blocks):
        for t in range(4):
            b = blocks[t]
            vslot = v_ring[b % 6]
            for half in range(2):
                pb = pp[half]
                m = None
                for dci in range(NC_):
                    w = RW([swv, xn_slot], [pb]) if dci == 0 else ()
                    m = p.emit("tensor", "matmul", pb.ap[:, 0:512],
                               xn_slot.ap[:, dci * 512 + t * 128:dci * 512 + (t + 1) * 128],
                               swv.ap[:, dci * DM + half * 512:dci * DM + (half + 1) * 512],
                               start=(dci == 0), stop=(dci == NC_ - 1),
                               waits=w, inc="pe" if dci == NC_ - 1 else None)
                setw([pb], m)
                setr([swv, xn_slot], m)
                m2 = p.emit("scalar", "copy", vslot.ap[:, half * 512:(half + 1) * 512],
                            pb.ap[:, 0:512], waits=RW([pb], [vslot]), inc="se")
                setw([vslot], m2)
                setr([pb], m2)

    def emit_attention(qb):
        par_q = (qb // 4) % 2
        kbs = [kb for kb in (qb - 1, qb, qb + 1) if 0 <= kb < NB]
        j0 = kbs[0] - (qb - 1)
        nk = len(kbs)
        edge = None if 0 < qb < NB - 1 else ("f" if qb == 0 else "l")
        qslot = q_ring[par_q]
        kslots = [k_ring[(kb // 4) % 2] for kb in kbs]
        for pr in range(NH // 2):
            par = pr % 2
            dc = pr
            psb = ps_sc[par]
            mz = None
            if edge is not None:
                e0, e1 = (0, 128) if edge == "f" else (256, 384)
                mz = p.emit("vector", "memset", psb.ap[:, e0:e1], 0.0,
                            waits=RW((), [psb]), inc="ve")
                mz = p.emit("vector", "memset", psb.ap[:, 512 + e0:512 + e1], 0.0,
                            inc="ve")
                setw([psb], mz)
            m = None
            first = True
            for hh in range(2):
                h = 2 * pr + hh
                row = (h % 2) * 64
                for j, kb in enumerate(kbs):
                    ksl = kslots[j]
                    w = RW([qslot, ksl], [psb]) if first else RW([ksl])
                    first = False
                    m = p.emit("tensor", "matmul",
                               psb.ap[:, hh * 512 + (j0 + j) * 128:hh * 512 + (j0 + j + 1) * 128],
                               qslot.ap[row:row + 64, dc * 512 + (qb % 4) * 128:dc * 512 + (qb % 4 + 1) * 128],
                               ksl.ap[row:row + 64, dc * 512 + (kb % 4) * 128:dc * 512 + (kb % 4 + 1) * 128],
                               start=True, stop=True,
                               inc="pe" if (hh == 1 and j == nk - 1) else None,
                               waits=w)
            setw([psb], m)
            setr([qslot], m)
            for ksl in kslots:
                setr([ksl], m)
            hb3 = hbias.ap[:, 2 * pr * 384:(2 * pr + 2) * 384].rearrange(
                "p (hh c) -> p hh c", c=384)
            ps3 = psb.ap[:].rearrange("p (hh c) -> p hh c", c=512)[:, :, 0:384]
            mma = p.emit("vector", "scalar_tensor_tensor",
                         ps3, ps3, 1.0, hb3, ALU.mult, ALU.add,
                         waits=RW([hbias], [psb]), inc="ve")
            setw([psb], mma)
            setr([hbias], mma)
            if edge is not None:
                bm3 = bm_[edge].ap[:].unsqueeze(1).broadcast_to([128, 2, 384])
                mma = p.emit("vector", "tensor_add", ps3, ps3, bm3,
                             waits=RW([bm_[edge]], [psb]), inc="ve")
                setw([psb], mma)
                setr([bm_[edge]], mma)
            mrm = p.emit("vector", "reduce_max", sm1[par].ap[:], ps3,
                         axis=AX.XY, negate=True,
                         waits=RW([psb], [sm1[par]]), inc="ve")
            setw([sm1[par]], mrm)
            setr([psb], mrm)
            mex = p.emit("scalar", "activation",
                         spA[par].ap[:].rearrange("p (hh c) -> p hh c", c=384),
                         ps3, AF.Exp, bias=sm1[par].ap[:],
                         waits=RW([psb, sm1[par]], [spA[par]]), inc="se")
            setw([spA[par]], mex)
            setr([psb, sm1[par]], mex)
            mrs = p.emit("vector", "reduce_sum", sm2[par].ap[:],
                         spA[par].ap[:].rearrange("p (hh c) -> p hh c", c=384),
                         axis=AX.X, waits=RW([spA[par]], [sm2[par]]), inc="ve")
            setw([sm2[par]], mrs)
            mri = p.emit("vector", "reciprocal", sd_[par].ap[:], sm2[par].ap[:],
                         waits=RW([sm2[par]], [sd_[par]]), inc="ve")
            setw([sd_[par]], mri)
            setr([sm2[par]], mri)
            rinv3 = sd_[par].ap[:].unsqueeze(2).broadcast_to([128, 2, 384])
            mA = p.emit("vector", "tensor_mul",
                        sA[par].ap[:].rearrange("p (hh c) -> p hh c", c=384),
                        spA[par].ap[:].rearrange("p (hh c) -> p hh c", c=384), rinv3,
                        waits=RW([spA[par], sd_[par]], [sA[par]]), inc="ve")
            setw([sA[par]], mA)
            setr([spA[par], sd_[par]], mA)
            trb = ps_trs[par]
            m = None
            ntr = 0
            for hh in range(2):
                for j in range(nk):
                    ntr += 1
                    w = RW([sA[par], iden], [trb]) if ntr == 1 else ()
                    m = p.emit("tensor", "transpose",
                               trb.ap[:, hh * 384 + j * 128:hh * 384 + (j + 1) * 128],
                               sA[par].ap[:, hh * 384 + (j0 + j) * 128:hh * 384 + (j0 + j + 1) * 128],
                               iden.ap[:], waits=w,
                               inc="pe" if ntr == 2 * nk else None)
            setw([trb], m)
            setr([sA[par], iden], m)
            mat = p.emit("scalar", "copy", sAT[par].ap[:, 0:768], trb.ap[:, 0:768],
                         waits=RW([trb], [sAT[par]]), inc="se")
            setw([sAT[par]], mat)
            setr([trb], mat)
            pav_lo2, pav_hi2 = pav_slots[par]
            pc = par * 128
            m = None
            for hh in range(2):
                h = 2 * pr + hh
                row = (h % 2) * 64
                pav_half = (pav_lo2, pav_hi2)[hh]
                for j, kb in enumerate(kbs):
                    vsl = v_ring[kb % 6]
                    w = RW([sAT[par], vsl], [pav_half]) if j == 0 else RW([vsl])
                    m = p.emit("tensor", "matmul", pav_t[row:row + 64, pc:pc + 128],
                               vsl.ap[:, h * 64:(h + 1) * 64],
                               sAT[par].ap[:, hh * 384 + j * 128:hh * 384 + (j + 1) * 128],
                               start=(j == 0), stop=(j == nk - 1),
                               waits=w, inc="pe" if j == nk - 1 else None)
                setw([pav_half], m)
                for kb in kbs:
                    setr([v_ring[kb % 6]], m)
            setr([sAT[par]], m)
            mev = p.emit("vector", "tensor_copy",
                         attnf[qb % 2].ap[:, dc * 128:(dc + 1) * 128],
                         pav_t[:, pc:pc + 128],
                         waits=RW([pav_lo2, pav_hi2], [attnf[qb % 2]]), inc="ve")
            setw([attnf[qb % 2]], mev)
            setr([pav_lo2, pav_hi2], mev)

    def emit_post(qb):
        af = attnf[qb % 2]
        x1 = x1b[qb % 2]
        xs = x_ring[qb % NXR]
        for half in range(2):
            pb = pp[half]
            m = None
            for dc in range(NC_):
                w = RW([af, swo], [pb]) if dc == 0 else ()
                m = p.emit("tensor", "matmul", pb.ap[:, 0:512],
                           af.ap[:, dc * 128:(dc + 1) * 128],
                           swo.ap[:, dc * DM + half * 512:dc * DM + (half + 1) * 512],
                           start=(dc == 0), stop=(dc == NC_ - 1),
                           waits=w, inc="pe" if dc == NC_ - 1 else None)
            setw([pb], m)
            setr([af, swo], m)
            m2 = p.emit("vector", "tensor_add", x1.ap[:, half * 512:(half + 1) * 512],
                        pb.ap[:, 0:512], xs.ap[:, half * 512:(half + 1) * 512],
                        waits=RW([pb, xs], [x1]), inc="ve")
            setw([x1], m2)
            setr([pb, xs], m2)
        mdx = p.emit("sync", "dma_start", out=x1_h[qb * 128:(qb + 1) * 128, :],
                     in_=x1.ap[:], waits=RW([x1]), inc=p.mksem(f"dx1_{qb % 2}"),
                     inc_n=16)
        setr([x1], mdx)
        emit_ln(x1, st2[qb % 2], xn2hit[qb % 2], sqdmp[qb % 2], out_f32=xn2f32[qb % 2])
        mlo = p.emit("vector", "tensor_sub", xn2lot[qb % 2].ap[:],
                     xn2f32[qb % 2].ap[:], xn2hit[qb % 2].ap[:],
                     waits=RW([xn2f32[qb % 2], xn2hit[qb % 2]], [xn2lot[qb % 2]]),
                     inc="ve")
        setw([xn2lot[qb % 2]], mlo)
        setr([xn2f32[qb % 2], xn2hit[qb % 2]], mlo)
        hi = xn2hif[qb % 2]
        lo = xn2lof[qb % 2]
        emit_transpose8(xn2hit[qb % 2], hi,
                        hi.ap[:].rearrange("p (dc c) -> p dc c", c=128), "vector")
        emit_transpose8(xn2lot[qb % 2], lo,
                        lo.ap[:].rearrange("p (dc c) -> p dc c", c=128), "scalar")
        mdn = p.emit("sync", "dma_start",
                     out=xn2_h[qb].rearrange("dc p c -> p dc c"),
                     in_=hi.ap[:].rearrange("p (dc c) -> p dc c", c=128),
                     waits=RW([hi]), inc=p.mksem("dxn"), inc_n=16)
        setr([hi], mdn)
        m = None
        first = True
        trips = ([(hi, wrhi)] * NC_ + [(hi, wrlo)] * NC_ + [(lo, wrhi)] * NC_)
        for i, (xf, wr_s) in enumerate(trips):
            dc = i % NC_
            w = RW([hi, lo, wrhi, wrlo], [pav_slots[0][0], pav_slots[0][1]]) if first else ()
            m = p.emit("tensor", "matmul", pav_t[:, 0:NE],
                       xf.ap[:, dc * 128:(dc + 1) * 128],
                       wr_s.ap[:, dc * NE:(dc + 1) * NE],
                       start=first, stop=(i == len(trips) - 1),
                       waits=w, inc="pe" if i == len(trips) - 1 else None)
            first = False
        setw([pav_slots[0][0], pav_slots[0][1]], m)
        setr([hi, lo, wrhi, wrlo], m)
        par = qb % 2
        mlg = p.emit("vector", "tensor_copy", slog[par].ap[:], pav_t[:, 0:NE],
                     waits=RW([pav_slots[0][0], pav_slots[0][1]], [slog[par]]), inc="ve")
        setw([slog[par]], mlg)
        setr([pav_slots[0][0], pav_slots[0][1]], mlg)
        mm1 = p.emit("vector", "reduce_max", rm1[par].ap[:], slog[par].ap[:], axis=AX.X,
                     waits=RW([slog[par]], [rm1[par]]), inc="ve")
        setw([rm1[par]], mm1)
        mi1 = p.emit("vector", "tensor_single_scalar", sind1[par].ap[:], slog[par].ap[:],
                     rm1[par].ap[:], ALU.is_ge,
                     waits=RW([slog[par], rm1[par]], [sind1[par]]), inc="ve")
        setw([sind1[par]], mi1)
        setr([rm1[par]], mi1)
        mlk = p.emit("vector", "scalar_tensor_tensor", slk[par].ap[:], sind1[par].ap[:],
                     -1.0e30, slog[par].ap[:], ALU.mult, ALU.add,
                     waits=RW([sind1[par], slog[par]], [slk[par]]), inc="ve")
        setw([slk[par]], mlk)
        setr([sind1[par], slog[par]], mlk)
        mm2_ = p.emit("vector", "reduce_max", rm2[par].ap[:], slk[par].ap[:], axis=AX.X,
                      waits=RW([slk[par]], [rm2[par]]), inc="ve")
        setw([rm2[par]], mm2_)
        mi2 = p.emit("vector", "tensor_single_scalar", sind2[par].ap[:], slk[par].ap[:],
                     rm2[par].ap[:], ALU.is_ge,
                     waits=RW([slk[par], rm2[par]], [sind2[par]]), inc="ve")
        setw([sind2[par]], mi2)
        setr([slk[par], rm2[par]], mi2)
        md = p.emit("vector", "tensor_sub", rdd[par].ap[:], rm2[par].ap[:], rm1[par].ap[:],
                    waits=RW([rm2[par], rm1[par]], [rdd[par]]), inc="ve")
        setw([rdd[par]], md)
        setr([rm2[par], rm1[par]], md)
        me = p.emit("scalar", "activation", se_[par].ap[:], rdd[par].ap[:], AF.Exp,
                    waits=RW([rdd[par]], [se_[par]]), inc="se")
        setw([se_[par]], me)
        setr([rdd[par]], me)
        mt = p.emit("vector", "tensor_scalar_add", st_[par].ap[:], se_[par].ap[:], 1.0,
                    waits=RW([se_[par]], [st_[par]]), inc="ve")
        setw([st_[par]], mt)
        mw1 = p.emit("vector", "reciprocal", sw1[par].ap[:], st_[par].ap[:],
                     waits=RW([st_[par]], [sw1[par]]), inc="ve")
        setw([sw1[par]], mw1)
        setr([st_[par]], mw1)
        mw1s = p.emit("vector", "tensor_scalar_mul", sw1s[par].ap[:], sw1[par].ap[:],
                      DEPTH_SCALE, waits=RW([sw1[par]], [sw1s[par]]), inc="ve")
        setw([sw1s[par]], mw1s)
        setr([sw1[par]], mw1s)
        mw2s = p.emit("vector", "tensor_mul", sw2s[par].ap[:], se_[par].ap[:],
                      sw1s[par].ap[:], waits=RW([se_[par], sw1s[par]], [sw2s[par]]),
                      inc="ve")
        setw([sw2s[par]], mw2s)
        setr([se_[par]], mw2s)
        mc1 = p.emit("vector", "tensor_single_scalar", sc1_[par].ap[:], sind1[par].ap[:],
                     sw1s[par].ap[:], ALU.mult,
                     waits=RW([sind1[par], sw1s[par]], [sc1_[par]]), inc="ve")
        setw([sc1_[par]], mc1)
        setr([sind1[par], sw1s[par]], mc1)
        mcb = p.emit("vector", "scalar_tensor_tensor",
                     comb.ap[:, qb * NE:(qb + 1) * NE], sind2[par].ap[:],
                     sw2s[par].ap[:], sc1_[par].ap[:], ALU.mult, ALU.add,
                     waits=RW([sind2[par], sw2s[par], sc1_[par]], [comb]), inc="ve")
        setw([comb], mcb)
        setr([sind2[par], sw2s[par], sc1_[par]], mcb)

    for sbk in range(NSB):
        blocks = [4 * sbk + t for t in range(4)]
        xn1 = xn1r[sbk % 2]
        for t in range(4):
            b = blocks[t]
            emit_ln(x_ring[b % NXR], st1[t % 2], xnst[t % 2], sqdmp[t % 2])
            dst3 = xn1.ap[:].rearrange("p (dc t c) -> p dc t c", dc=NC_, t=4, c=128)[:, :, t, :]
            emit_transpose8(xnst[t % 2], xn1, dst3, "vector")
        emit_proj_fm(swk, xn1, k_ring[sbk % 2])
        emit_v_tm(xn1, sbk, blocks)
        emit_proj_fm(swq, xn1, q_ring[sbk % 2], scale=0.125)
        if debug and sbk == 0:
            for src_b, dst in ((xn1r[0], dbg_xn1), (k_ring[0], dbg_k),
                               (q_ring[0], dbg_q)):
                mdbg = p.emit("sync", "dma_start", out=dst[:], in_=src_b.ap[:],
                              waits=RW([src_b]), inc="dx", inc_n=16)
                setr([src_b], mdbg)
            for i in range(2):
                mdbg = p.emit("sync", "dma_start", out=dbg_v[:, i*1024:(i+1)*1024],
                              in_=v_ring[i].ap[:], waits=RW([v_ring[i]]),
                              inc="dx", inc_n=16)
                setr([v_ring[i]], mdbg)
        qbs = [0, 1, 2] if sbk == 0 else [4 * sbk - 1, 4 * sbk, 4 * sbk + 1, 4 * sbk + 2]
        for qb in qbs:
            emit_attention(qb)
            emit_post(qb)
        if sbk + 1 < NSB:
            for t in range(4):
                b = 4 * (sbk + 1) + t
                xr = x_ring[b % NXR]
                m = p.emit("sync", "dma_start", out=xr.ap[:],
                           in_=x_h[b * 128:(b + 1) * 128, :],
                           waits=RW((), [xr]), inc=p.mksem(f"dxx{b % NXR}"), inc_n=16)
                setw([xr], m)
    emit_attention(NB - 1)
    emit_post(NB - 1)
    if debug:
        m = p.emit("sync", "dma_start", out=comb_h[:], in_=comb.ap[:],
                   waits=RW([comb]), inc="dx", inc_n=16)
        setr([comb], m)

    sync_sems = [s for s in p.cnt if s.startswith(("dxx", "dx1")) or
                 s in ("dx", "dxn", "did", "dre", "dsl", "dwr")]
    own = {"sync": sync_sems, "gpsimd": ["dg", "dwst", "po"], "vector": ["ve"],
           "scalar": ["se"], "tensor": ["pe"]}
    for eng in Prog.ENGS:
        p.wait_only(eng, [(s, p.cnt[s]) for s in own[eng] if s in p.cnt])
        p.emit(eng, "sem_inc", p.sems["pb"], 1)
        p.cnt["pb"] += 1
    barrier = ("pb", p.cnt["pb"])
    for eng in Prog.ENGS:
        p.wait_only(eng, [barrier])

    psA.close()
    esA.close()

    esB = ExitStack()
    sxn2 = [sb(f"sxn2_{dc}", [128, TG], BF16, esB) for dc in range(NC_)]
    oacc = [sb(f"oacc{t}", [128, DM], F32, esB) for t in range(TG // 128)]
    swgb = [sb(f"swgb{i}", [128, NC_ * 1024], BF16, esB) for i in range(2)]
    swub = [sb(f"swub{i}", [128, NC_ * 1024], BF16, esB) for i in range(2)]
    swdb = [sb(f"swdb{i}", [128, NC_ * 1024], BF16, esB) for i in range(2)]
    hbuf = [[sb(f"h{s}_{i}", [128, 512], BF16, esB) for i in range(8)] for s in range(2)]
    ssil = [sb(f"ssil{i}", [128, 512], F32, esB) for i in range(2)]

    psB = ExitStack()
    pg = [psum(f"pg{i}", [128, 512], F32, psB) for i in range(2)]
    pu = [psum(f"pu{i}", [128, 512], F32, psB) for i in range(2)]
    peo = [psum(f"peo{i}", [128, 1024], F32, psB) for i in range(2)]

    chunk_idx = 0
    for g in range(NTG):
        for dc in range(NC_):
            m = p.emit("sync", "dma_start",
                       out=sxn2[dc].ap[:].rearrange("p (t c) -> p t c", c=128),
                       in_=xn2_h[g * (TG // 128):(g + 1) * (TG // 128), dc].rearrange("t p c -> p t c"),
                       waits=RW((), [sxn2[dc]]), inc=p.mksem(f"dsx{dc}"), inc_n=16)
            setw([sxn2[dc]], m)
        for tt in range(TG // 128):
            row0 = (g * (TG // 128) + tt) * 128
            m = p.emit("sync", "dma_start", out=oacc[tt].ap[:],
                       in_=x1_h[row0:row0 + 128, :],
                       waits=RW((), [oacc[tt]]), inc=p.mksem(f"doa{tt}"), inc_n=16)
            setw([oacc[tt]], m)
        for e in range(NE):
            for q in range(NQ):
                par = chunk_idx % 2
                wgb, wub, wdb = swgb[par], swub[par], swdb[par]
                mwg = p.emit("gpsimd", "dma_start",
                             out=wgb.ap[:].rearrange("p (dc c) -> p dc c", c=1024),
                             in_=wg_h[e][:, q * 1024:(q + 1) * 1024].rearrange(
                                 "(dc p) c -> p dc c", p=128),
                             waits=RW((), [wgb]), inc=p.mksem(f"dwg{par}"), inc_n=16)
                setw([wgb], mwg)
                mwu = p.emit("gpsimd", "dma_start",
                             out=wub.ap[:].rearrange("p (dc c) -> p dc c", c=1024),
                             in_=wu_h[e][:, q * 1024:(q + 1) * 1024].rearrange(
                                 "(dc p) c -> p dc c", p=128),
                             waits=RW((), [wub]), inc=p.mksem(f"dwu{par}"), inc_n=16)
                setw([wub], mwu)
                mwd = p.emit("gpsimd", "dma_start",
                             out=wdb.ap[:].rearrange("p (hc c) -> p hc c", c=1024),
                             in_=wd_h[e][q * 1024:(q + 1) * 1024, :].rearrange(
                                 "(hc p) c -> p hc c", p=128),
                             waits=RW((), [wdb]), inc=p.mksem(f"dwd{par}"), inc_n=16)
                setw([wdb], mwd)
                for ts in range(TSB):
                    hset = hbuf[ts % 2]
                    for hc in range(8):
                        pgb, pub = pg[hc % 2], pu[hc % 2]
                        m = None
                        for dc in range(NC_):
                            w = RW([wgb, sxn2[dc]], [pgb]) if dc == 0 else RW([sxn2[dc]])
                            m = p.emit("tensor", "matmul", pgb.ap[:, 0:512],
                                       wgb.ap[:, dc * 1024 + hc * 128:dc * 1024 + (hc + 1) * 128],
                                       sxn2[dc].ap[:, ts * 512:(ts + 1) * 512],
                                       start=(dc == 0), stop=(dc == NC_ - 1),
                                       waits=w, inc="pe" if dc == NC_ - 1 else None)
                        setw([pgb], m)
                        setr([wgb], m)
                        for dc in range(NC_):
                            setr([sxn2[dc]], m)
                        m = None
                        for dc in range(NC_):
                            w = RW([wub], [pub]) if dc == 0 else ()
                            m = p.emit("tensor", "matmul", pub.ap[:, 0:512],
                                       wub.ap[:, dc * 1024 + hc * 128:dc * 1024 + (hc + 1) * 128],
                                       sxn2[dc].ap[:, ts * 512:(ts + 1) * 512],
                                       start=(dc == 0), stop=(dc == NC_ - 1),
                                       waits=w, inc="pe" if dc == NC_ - 1 else None)
                        setw([pub], m)
                        setr([wub], m)
                        msil = p.emit("scalar", "activation", ssil[hc % 2].ap[:],
                                      pgb.ap[:, 0:512], AF.Silu,
                                      waits=RW([pgb], [ssil[hc % 2]]), inc="se")
                        setw([ssil[hc % 2]], msil)
                        setr([pgb], msil)
                        mh = p.emit("vector", "tensor_mul", hset[hc].ap[:],
                                    ssil[hc % 2].ap[:], pub.ap[:, 0:512],
                                    waits=RW([ssil[hc % 2], pub], [hset[hc]]), inc="ve")
                        setw([hset[hc]], mh)
                        setr([ssil[hc % 2], pub], mh)
                    for tl in range(4):
                        tt = ts * 4 + tl
                        pe_ = peo[tt % 2]
                        m = None
                        nmm = 0
                        for hc in range(8):
                            for half in range(2):
                                nmm += 1
                                w = RW([hset[hc], wdb], [pe_]) if hc == 0 else RW([hset[hc]])
                                m = p.emit("tensor", "matmul",
                                           pe_.ap[:, half * 512:(half + 1) * 512],
                                           hset[hc].ap[:, tl * 128:(tl + 1) * 128],
                                           wdb.ap[:, hc * 1024 + half * 512:hc * 1024 + (half + 1) * 512],
                                           start=(hc == 0), stop=(hc == 7),
                                           waits=w, inc="pe" if nmm == 16 else None)
                        setw([pe_], m)
                        setr([wdb], m)
                        for hc in range(8):
                            setr([hset[hc]], m)
                        cap = comb.ap[:, (g * (TG // 128) + tt) * NE + e:
                                      (g * (TG // 128) + tt) * NE + e + 1]
                        mev = p.emit("vector", "scalar_tensor_tensor",
                                     oacc[tt].ap[:], pe_.ap[:, 0:1024], cap,
                                     oacc[tt].ap[:], ALU.mult, ALU.add,
                                     waits=RW([pe_, comb, oacc[tt]], [oacc[tt]]),
                                     inc="ve")
                        setw([oacc[tt]], mev)
                        setr([pe_, comb], mev)
                chunk_idx += 1
        for tt in range(TG // 128):
            row0 = (g * (TG // 128) + tt) * 128
            m = p.emit("sync", "dma_start", out=out_h[row0:row0 + 128, :],
                       in_=oacc[tt].ap[:], waits=RW([oacc[tt]]),
                       inc=p.mksem(f"doa{tt}"), inc_n=16)
            setr([oacc[tt]], m)

    p.wait_only("sync", [(f"doa{tt}", p.cnt[f"doa{tt}"]) for tt in range(TG // 128)])

    psB.close()
    esB.close()

    with nc.Block() as block:
        @block.sync
        def _(eng):
            p.replay("sync", eng)

        @block.gpsimd
        def _(eng):
            p.replay("gpsimd", eng)

        @block.vector
        def _(eng):
            p.replay("vector", eng)

        @block.scalar
        def _(eng):
            p.replay("scalar", eng)

        @block.tensor
        def _(eng):
            p.replay("tensor", eng)

    es.close()
    return nc


def _host_constants():
    ql = np.arange(128)[:, None]
    kl = np.arange(384)[None, :]
    band = (kl >= ql) & (kl <= ql + 256)
    mk = lambda b: np.where(b, np.float32(0.0), np.float32(NEG)).astype(np.float32)
    bm_f = mk(band & (kl >= 128))
    bm_l = mk(band & (kl < 256))
    rel = (128 + ql - kl).astype(np.float32)
    relm = np.where(band, rel, np.float32(NEG)).astype(np.float32)
    iden = np.eye(128, dtype=ml_dtypes.bfloat16)
    return relm, bm_f, bm_l, iden


def kernel(L0_attn_x, L0_attn_Wq, L0_attn_Wk, L0_attn_Wv, L0_attn_Wo,
           L0_attn_slopes, L0_ffn_Wr, L0_ffn_Wg, L0_ffn_Wu, L0_ffn_Wd):
    x = np.ascontiguousarray(np.asarray(L0_attn_x, dtype=np.float32))
    relm, bm_f, bm_l, iden = _host_constants()
    common = {
        "wq": np.ascontiguousarray(np.asarray(L0_attn_Wq, dtype=np.float32)),
        "wk": np.ascontiguousarray(np.asarray(L0_attn_Wk, dtype=np.float32)),
        "wv": np.ascontiguousarray(np.asarray(L0_attn_Wv, dtype=np.float32)),
        "wo": np.ascontiguousarray(np.asarray(L0_attn_Wo, dtype=np.float32)),
        "slopes": np.ascontiguousarray(np.asarray(L0_attn_slopes, dtype=np.float32)),
        "wr": np.ascontiguousarray(np.asarray(L0_ffn_Wr, dtype=np.float32)),
        "wg": np.ascontiguousarray(np.asarray(L0_ffn_Wg, dtype=np.float32)),
        "wu": np.ascontiguousarray(np.asarray(L0_ffn_Wu, dtype=np.float32)),
        "wd": np.ascontiguousarray(np.asarray(L0_ffn_Wd, dtype=np.float32)),
        "relm": relm,
        "bm_f": bm_f,
        "bm_l": bm_l,
        "iden": iden,
    }
    nc = build_nc()
    in_maps = [dict(common, x=np.ascontiguousarray(x[i])) for i in range(8)]
    res = run_bass_kernel_spmd(nc, in_maps, core_ids=list(range(8)))
    return np.stack([res.results[i]["out"] for i in range(8)], axis=0)


# version 15
# speedup vs baseline: 1.1267x; 1.0175x over previous
import numpy as np
import ml_dtypes
from contextlib import ExitStack

import concourse.bass as bass
import concourse.mybir as mybir
from concourse.bass_utils import run_bass_kernel_spmd

F32 = mybir.dt.float32
BF16 = mybir.dt.bfloat16
AF = mybir.ActivationFunctionType
ALU = mybir.AluOpType
AX = mybir.AxisListType

S, DM, NH, DK = 4096, 1024, 16, 64
NB = S // 128
NSB = NB // 4
NC_ = DM // 128
NE, HID = 8, 4096
DEPTH_SCALE = float(1.0 / np.sqrt(12.0))
NEG = -1.0e9
EPS = 1e-5

TG = 1024
NTG = S // TG
NQ = 4
TSB = TG // 512


class Buf:
    __slots__ = ("ap", "w", "r")

    def __init__(self, ap):
        self.ap = ap
        self.w = None
        self.r = {}


class Prog:
    ENGS = ("sync", "gpsimd", "vector", "scalar", "tensor")

    def __init__(self, nc, es):
        self.nc = nc
        self.es = es
        self.q = {e: [] for e in self.ENGS}
        self.sems = {}
        self.cnt = {}
        for name in ("pe", "ve", "se", "dx", "dg", "do", "pb"):
            self.mksem(name)

    def mksem(self, name):
        if name not in self.sems:
            self.sems[name] = self.es.enter_context(self.nc.semaphore(name))
            self.cnt[name] = 0
        return name

    def emit(self, eng, fn, *a, waits=(), inc=None, inc_n=1, **kw):
        ws = [w for w in waits if w is not None]
        incs = []
        ms = None
        if inc is not None:
            self.cnt[inc] += inc_n
            ms = (inc, self.cnt[inc])
            incs.append((inc, inc_n))
        self.q[eng].append([fn, a, kw, ws, incs])
        return ms

    def wait_only(self, eng, waits):
        ws = [w for w in waits if w is not None]
        if ws:
            self.q[eng].append([None, (), {}, ws, []])

    def add_inc_to_last(self, eng, sem, n):
        assert self.q[eng], eng
        self.q[eng][-1][4].append((sem, n))
        self.cnt[sem] += n
        return (sem, self.cnt[sem])

    def replay(self, eng_name, eng):
        seen = {}
        for fn, a, kw, ws, incs in self.q[eng_name]:
            for (s, v) in ws:
                if seen.get(s, -1) >= v:
                    continue
                seen[s] = v
                eng.wait_ge(self.sems[s], v)
            if fn is None:
                continue
            inst = getattr(eng, fn)(*a, **kw)
            for (s, n) in incs:
                inst.then_inc(self.sems[s], n)


def RW(reads=(), writes=()):
    ws = []
    for b in reads:
        if b.w:
            ws.append(b.w)
    for b in writes:
        if b.w:
            ws.append(b.w)
        ws.extend(b.r.items())
    return ws


def setw(bufs, ms):
    for b in bufs:
        b.w = ms
        b.r = {}


def setr(bufs, ms):
    if ms is None:
        return
    for b in bufs:
        old = b.r.get(ms[0], -1)
        if ms[1] > old:
            b.r[ms[0]] = ms[1]


def weave(*gens):
    alive = [g for g in gens if g is not None]
    while alive:
        for g in list(alive):
            try:
                next(g)
            except StopIteration:
                alive.remove(g)


def drain(g):
    for _ in g:
        pass


def build_nc(debug=False):
    nc = bass.Bass("TRN2", target_bir_lowering=False)

    x_h = nc.dram_tensor("x", [S, DM], F32, kind="ExternalInput").ap()
    out_h = nc.dram_tensor("out", [S, DM], F32, kind="ExternalOutput").ap()
    wq_h = nc.dram_tensor("wq", [DM, DM], F32, kind="ExternalInput").ap()
    wk_h = nc.dram_tensor("wk", [DM, DM], F32, kind="ExternalInput").ap()
    wv_h = nc.dram_tensor("wv", [DM, DM], F32, kind="ExternalInput").ap()
    wo_h = nc.dram_tensor("wo", [DM, DM], F32, kind="ExternalInput").ap()
    sl_h = nc.dram_tensor("slopes", [NH], F32, kind="ExternalInput").ap()
    wr_h = nc.dram_tensor("wr", [DM, NE], F32, kind="ExternalInput").ap()
    wg_h = nc.dram_tensor("wg", [NE, DM, HID], F32, kind="ExternalInput").ap()
    wu_h = nc.dram_tensor("wu", [NE, DM, HID], F32, kind="ExternalInput").ap()
    wd_h = nc.dram_tensor("wd", [NE, HID, DM], F32, kind="ExternalInput").ap()
    relm_h = nc.dram_tensor("relm", [128, 384], F32, kind="ExternalInput").ap()
    bmf_h = nc.dram_tensor("bm_f", [128, 384], F32, kind="ExternalInput").ap()
    bml_h = nc.dram_tensor("bm_l", [128, 384], F32, kind="ExternalInput").ap()
    iden_h = nc.dram_tensor("iden", [128, 128], BF16, kind="ExternalInput").ap()

    scr_kind = "ExternalOutput" if debug else "Internal"
    x1_h = nc.dram_tensor("x1_scr", [S, DM], F32, kind=scr_kind).ap()
    xn2_h = nc.dram_tensor("xn2_scr", [NB, NC_, 128, 128], BF16, kind=scr_kind).ap()
    comb_h = (nc.dram_tensor("comb_scr", [128, NB * NE], F32, kind="ExternalOutput").ap()
              if debug else None)
    if debug:
        dbg_xn1 = nc.dram_tensor("dbg_xn1", [128, 4096], BF16, kind="ExternalOutput").ap()
        dbg_k = nc.dram_tensor("dbg_k", [128, 4096], BF16, kind="ExternalOutput").ap()
        dbg_q = nc.dram_tensor("dbg_q", [128, 4096], BF16, kind="ExternalOutput").ap()
        dbg_v = nc.dram_tensor("dbg_v", [128, 2048], BF16, kind="ExternalOutput").ap()
        dbg_s2 = nc.dram_tensor("dbg_s2", [128, 384], F32, kind="ExternalOutput").ap()
        dbg_pA = nc.dram_tensor("dbg_pA", [128, 384], BF16, kind="ExternalOutput").ap()
        dbg_A = nc.dram_tensor("dbg_A", [128, 384], BF16, kind="ExternalOutput").ap()
        dbg_st = nc.dram_tensor("dbg_st", [128, 8], F32, kind="ExternalOutput").ap()

    es = ExitStack()
    p = Prog(nc, es)

    def sb(name, shape, dt=F32, stack=es):
        return Buf(stack.enter_context(nc.sbuf_tensor(name, shape, dt)))

    def psum(name, shape, dt=F32, stack=es):
        return Buf(stack.enter_context(nc.psum_tensor(name, shape, dt)))

    iden = sb("s_iden", [128, 128], BF16)
    bm_ = {"f": sb("s_bmf", [128, 384]), "l": sb("s_bml", [128, 384])}
    relmb = sb("s_relm", [128, 384])
    hbias = sb("s_hbias", [128, NH * 384])
    slbc = sb("slbc", [128, NH])
    comb = sb("comb", [128, NB * NE])
    wrhi = sb("wrhi", [128, NC_ * NE], BF16)
    wrlo = sb("wrlo", [128, NC_ * NE], BF16)

    esA = ExitStack()
    swq = sb("swq", [128, NC_ * DM], BF16, esA)
    swk = sb("swk", [128, NC_ * DM], BF16, esA)
    swv = sb("swv", [128, NC_ * DM], BF16, esA)
    swo = sb("swo", [128, NC_ * DM], BF16, esA)
    swrf = sb("swrf", [128, NC_ * NE], F32, esA)
    NXR = 5
    x_ring = [sb(f"xr{i}", [128, DM], F32, esA) for i in range(NXR)]
    xnst = [sb("xnst0", [128, DM], BF16, esA)] * 2
    sqdmp = xnst
    xn1r = [sb("xn1r0", [128, NC_ * 512], BF16, esA)] * 2
    k_ring = [sb(f"kr{i}", [128, NC_ * 512], BF16, esA) for i in range(2)]
    q_ring = [sb(f"qr{i}", [128, NC_ * 512], BF16, esA) for i in range(2)]
    v_ring = [sb(f"vr{i}", [128, DM], BF16, esA) for i in range(6)]
    spA = [sb(f"pA{i}", [128, 768], BF16, esA) for i in range(2)]
    s2m = [sb(f"s2m{i}", [128, 768], F32, esA) for i in range(2)]
    sA = [sb("A0", [128, 768], BF16, esA)] * 2
    sAT = [sb("AT0", [128, 768], BF16, esA)] * 2
    attnf = [sb(f"attnf{i}", [128, DM], BF16, esA) for i in range(2)]
    x1b = [sb(f"x1b{i}", [128, DM], F32, esA) for i in range(2)]
    xn2f32 = [sb("xn2f0", [128, DM], F32, esA)] * 2
    xn2hit = [sb("xn2ht0", [128, DM], BF16, esA)] * 2
    xn2lot = [sb("xn2lt0", [128, DM], BF16, esA)] * 2
    xn2hif = [sb("xn2hf0", [128, DM], BF16, esA)] * 2
    xn2lof = [sb("xn2lf0", [128, DM], BF16, esA)] * 2

    def statset(tag):
        return {k: sb(f"st_{tag}_{k}", [128, 1], F32, esA)
                for k in ("sum", "sq", "mean", "mm2", "veps", "var", "lnv", "rstd", "nmr")}

    st1 = [statset(f"a{i}") for i in range(2)]
    st2 = [statset(f"b{i}") for i in range(2)]
    slog = [sb(f"rlog{i}", [128, NE], F32, esA) for i in range(2)]
    sind1 = [sb(f"rind1{i}", [128, NE], F32, esA) for i in range(2)]
    slk = [sb(f"rlk{i}", [128, NE], F32, esA) for i in range(2)]
    sind2 = [sb(f"rind2{i}", [128, NE], F32, esA) for i in range(2)]
    sm1 = [sb(f"sxm1{i}", [128, 1], F32, esA) for i in range(2)]
    sm2 = [sb(f"sxm2{i}", [128, 2], F32, esA) for i in range(2)]
    sd_ = [sb(f"sxd{i}", [128, 2], F32, esA) for i in range(2)]
    rm1 = [sb(f"rm1{i}", [128, 1], F32, esA) for i in range(2)]
    rm2 = [sb(f"rm2{i}", [128, 1], F32, esA) for i in range(2)]
    rdd = [sb(f"rd{i}", [128, 1], F32, esA) for i in range(2)]
    se_ = [sb(f"re{i}", [128, 1], F32, esA) for i in range(2)]
    st_ = [sb(f"rt{i}", [128, 1], F32, esA) for i in range(2)]
    sw1 = [sb(f"rw1{i}", [128, 1], F32, esA) for i in range(2)]
    sw1s = [sb(f"rw1s{i}", [128, 1], F32, esA) for i in range(2)]
    sw2s = [sb(f"rw2s{i}", [128, 1], F32, esA) for i in range(2)]
    sc1_ = [sb(f"rc1{i}", [128, NE], F32, esA) for i in range(2)]

    psA = ExitStack()
    pp = [psum(f"pp{i}", [128, 512], F32, psA) for i in range(2)]
    ps_sc = [psum(f"psc{i}", [128, 1024], F32, psA) for i in range(2)]
    ps_tr = psum("ptr", [128, 1024], BF16, psA)
    pav_t = psA.enter_context(nc.psum_tensor("pav", [128, 128], F32))
    pav_lo = Buf(pav_t)
    pav_hi = Buf(pav_t)
    pav_all = (pav_lo, pav_hi)

    ms = p.emit("sync", "dma_start", out=iden.ap[:], in_=iden_h[:], inc=p.mksem("did"), inc_n=16)
    setw([iden], ms)
    for key, hbm in (("f", bmf_h), ("l", bml_h)):
        ms = p.emit("sync", "dma_start", out=bm_[key].ap[:], in_=hbm[:],
                    inc=p.mksem(f"dbm{key}"), inc_n=16)
        setw([bm_[key]], ms)
    ms = p.emit("sync", "dma_start", out=relmb.ap[:], in_=relm_h[:],
                inc=p.mksem("dre"), inc_n=16)
    setw([relmb], ms)
    ms = p.emit("sync", "dma_start", out=slbc.ap[:],
                in_=sl_h.unsqueeze(0).broadcast_to([128, NH]), inc=p.mksem("dsl"), inc_n=16)
    setw([slbc], ms)
    wstage = k_ring[0]
    for h in range(NH):
        ms = p.emit("scalar", "activation",
                    hbias.ap[:, h * 384:(h + 1) * 384], relmb.ap[:],
                    AF.Identity, scale=slbc.ap[:, h:h + 1],
                    waits=RW([relmb, slbc], [hbias]), inc="se")
        setw([hbias], ms)
    setr([relmb, slbc], ms)
    for w_hbm, w_sb in ((wq_h, swq), (wk_h, swk), (wv_h, swv), (wo_h, swo)):
        for half in range(2):
            ms = p.emit("gpsimd", "dma_start",
                        out=wstage.ap[:].rearrange("p (oc c) -> p oc c", c=DM),
                        in_=w_hbm[half * 512:(half + 1) * 512, :].rearrange(
                            "(oc p) c -> p oc c", p=128),
                        waits=RW((), [wstage]), inc=p.mksem("dwst"), inc_n=16)
            setw([wstage], ms)
            for dc in range(NC_):
                m = None
                for ocl in range(4):
                    w = RW([wstage, iden], [ps_tr]) if ocl == 0 else ()
                    m = p.emit("tensor", "transpose",
                               ps_tr.ap[:, ocl * 128:(ocl + 1) * 128],
                               wstage.ap[:, ocl * DM + dc * 128:ocl * DM + (dc + 1) * 128],
                               iden.ap[:], waits=w, inc="pe" if ocl == 3 else None)
                setw([ps_tr], m)
                setr([wstage, iden], m)
                m2 = p.emit("vector", "tensor_copy",
                            w_sb.ap[:, dc * DM + half * 512:dc * DM + (half + 1) * 512],
                            ps_tr.ap[:, 0:512],
                            waits=RW([ps_tr], [w_sb]), inc="ve")
                setw([w_sb], m2)
                setr([ps_tr], m2)
    ms = p.emit("sync", "dma_start",
                out=swrf.ap[:].rearrange("p (dc c) -> p dc c", c=NE),
                in_=wr_h.rearrange("(dc p) c -> p dc c", p=128),
                inc=p.mksem("dwr"), inc_n=16)
    setw([swrf], ms)
    ms = p.emit("vector", "tensor_copy", wrhi.ap[:], swrf.ap[:],
                waits=RW([swrf], [wrhi]), inc="ve")
    setw([wrhi], ms)
    setr([swrf], ms)
    ms = p.emit("vector", "tensor_sub", wrlo.ap[:], swrf.ap[:], wrhi.ap[:],
                waits=RW([swrf, wrhi], [wrlo]), inc="ve")
    setw([wrlo], ms)
    setr([swrf, wrhi], ms)
    for t in range(4):
        ms = p.emit("sync", "dma_start", out=x_ring[t].ap[:],
                    in_=x_h[t * 128:(t + 1) * 128, :],
                    inc=p.mksem(f"dxx{t % NXR}"), inc_n=16)
        setw([x_ring[t]], ms)

    def emit_ln(xbuf, stats, out_bf, sqd, out_f32=None):
        m = p.emit("vector", "reduce_sum", stats["sum"].ap[:], xbuf.ap[:], axis=AX.X,
                   waits=RW([xbuf], [stats["sum"]]), inc="ve")
        setw([stats["sum"]], m); setr([xbuf], m)
        yield
        m = p.emit("scalar", "activation", sqd.ap[:], xbuf.ap[:], AF.Square,
                   accum_out=stats["sq"].ap[:],
                   waits=RW([xbuf], [sqd, stats["sq"]]), inc="se")
        setw([sqd, stats["sq"]], m); setr([xbuf], m)
        yield
        m = p.emit("vector", "tensor_scalar_mul", stats["mean"].ap[:],
                   stats["sum"].ap[:], 1.0 / DM,
                   waits=RW([stats["sum"]], [stats["mean"]]), inc="ve")
        setw([stats["mean"]], m); setr([stats["sum"]], m)
        m = p.emit("vector", "tensor_mul", stats["mm2"].ap[:], stats["mean"].ap[:],
                   stats["mean"].ap[:], waits=RW([stats["mean"]], [stats["mm2"]]), inc="ve")
        setw([stats["mm2"]], m); setr([stats["mean"]], m)
        m = p.emit("vector", "tensor_scalar", stats["veps"].ap[:],
                   stats["sq"].ap[:], 1.0 / DM, EPS, ALU.mult, ALU.add,
                   waits=RW([stats["sq"]], [stats["veps"]]), inc="ve")
        setw([stats["veps"]], m); setr([stats["sq"]], m)
        m = p.emit("vector", "tensor_sub", stats["var"].ap[:],
                   stats["veps"].ap[:], stats["mm2"].ap[:],
                   waits=RW([stats["veps"], stats["mm2"]], [stats["var"]]), inc="ve")
        setw([stats["var"]], m); setr([stats["veps"], stats["mm2"]], m)
        m = p.emit("scalar", "activation", stats["lnv"].ap[:], stats["var"].ap[:],
                   AF.Ln, waits=RW([stats["var"]], [stats["lnv"]]), inc="se")
        setw([stats["lnv"]], m); setr([stats["var"]], m)
        m = p.emit("scalar", "activation", stats["rstd"].ap[:], stats["lnv"].ap[:],
                   AF.Exp, scale=-0.5, waits=RW([stats["lnv"]], [stats["rstd"]]), inc="se")
        setw([stats["rstd"]], m); setr([stats["lnv"]], m)
        yield
        m = p.emit("vector", "scalar_tensor_tensor", stats["nmr"].ap[:],
                   stats["mean"].ap[:], -1.0, stats["rstd"].ap[:], ALU.mult, ALU.mult,
                   waits=RW([stats["mean"], stats["rstd"]], [stats["nmr"]]), inc="ve")
        setw([stats["nmr"]], m); setr([stats["mean"], stats["rstd"]], m)
        m = p.emit("scalar", "activation", out_bf.ap[:], xbuf.ap[:], AF.Identity,
                   bias=stats["nmr"].ap[:], scale=stats["rstd"].ap[:],
                   waits=RW([xbuf, stats["nmr"], stats["rstd"]], [out_bf]), inc="se")
        setw([out_bf], m); setr([xbuf, stats["nmr"], stats["rstd"]], m)
        yield
        if out_f32 is not None:
            m = p.emit("scalar", "activation", out_f32.ap[:], xbuf.ap[:], AF.Identity,
                       bias=stats["nmr"].ap[:], scale=stats["rstd"].ap[:],
                       waits=RW([xbuf, stats["nmr"], stats["rstd"]], [out_f32]), inc="se")
            setw([out_f32], m); setr([xbuf, stats["nmr"], stats["rstd"]], m)

    def emit_transpose8(src_tm, dst_fm, dst_ap3, evac_eng):
        m = None
        for dc in range(NC_):
            w = RW([src_tm, iden], [ps_tr]) if dc == 0 else ()
            m = p.emit("tensor", "transpose", ps_tr.ap[:, dc * 128:(dc + 1) * 128],
                       src_tm.ap[:, dc * 128:(dc + 1) * 128], iden.ap[:],
                       waits=w, inc="pe" if dc == NC_ - 1 else None)
        setw([ps_tr], m)
        setr([src_tm, iden], m)
        sem = "ve" if evac_eng == "vector" else "se"
        fn = "tensor_copy" if evac_eng == "vector" else "copy"
        m2 = p.emit(evac_eng, fn, dst_ap3,
                    ps_tr.ap[:].rearrange("p (dc c) -> p dc c", c=128),
                    waits=RW([ps_tr], [dst_fm]), inc=sem)
        setw([dst_fm], m2)
        setr([ps_tr], m2)
        yield

    def emit_proj_fm(w_sb, xn_slot, dst_slot, scale=None):
        for dout_c in range(NC_):
            pb = pp[dout_c % 2]
            m = None
            for dci in range(NC_):
                w = RW([w_sb, xn_slot], [pb]) if dci == 0 else ()
                m = p.emit("tensor", "matmul", pb.ap[:, 0:512],
                           w_sb.ap[:, dci * DM + dout_c * 128:dci * DM + (dout_c + 1) * 128],
                           xn_slot.ap[:, dci * 512:(dci + 1) * 512],
                           start=(dci == 0), stop=(dci == NC_ - 1),
                           waits=w, inc="pe" if dci == NC_ - 1 else None)
            setw([pb], m)
            setr([w_sb, xn_slot], m)
            dst = dst_slot.ap[:, dout_c * 512:(dout_c + 1) * 512]
            if scale is None:
                m2 = p.emit("vector", "tensor_copy", dst, pb.ap[:, 0:512],
                            waits=RW([pb], [dst_slot]), inc="ve")
            else:
                m2 = p.emit("scalar", "activation", dst, pb.ap[:, 0:512], AF.Copy,
                            scale=scale, waits=RW([pb], [dst_slot]), inc="se")
            setw([dst_slot], m2)
            setr([pb], m2)
            yield

    def emit_v_tm(xn_slot, sbk, blocks):
        for t in range(4):
            b = blocks[t]
            vslot = v_ring[b % 6]
            for half in range(2):
                pb = pp[half]
                m = None
                for dci in range(NC_):
                    w = RW([swv, xn_slot], [pb]) if dci == 0 else ()
                    m = p.emit("tensor", "matmul", pb.ap[:, 0:512],
                               xn_slot.ap[:, dci * 512 + t * 128:dci * 512 + (t + 1) * 128],
                               swv.ap[:, dci * DM + half * 512:dci * DM + (half + 1) * 512],
                               start=(dci == 0), stop=(dci == NC_ - 1),
                               waits=w, inc="pe" if dci == NC_ - 1 else None)
                setw([pb], m)
                setr([swv, xn_slot], m)
                m2 = p.emit("scalar", "copy", vslot.ap[:, half * 512:(half + 1) * 512],
                            pb.ap[:, 0:512], waits=RW([pb], [vslot]), inc="se")
                setw([vslot], m2)
                setr([pb], m2)
                yield

    def emit_attention(qb):
        par_q = (qb // 4) % 2
        kbs = [kb for kb in (qb - 1, qb, qb + 1) if 0 <= kb < NB]
        j0 = kbs[0] - (qb - 1)
        nk = len(kbs)
        edge = None if 0 < qb < NB - 1 else ("f" if qb == 0 else "l")
        qslot = q_ring[par_q]
        kslots = [k_ring[(kb // 4) % 2] for kb in kbs]
        for pr in range(NH // 2):
            par = pr % 2
            dc = pr
            psb = ps_sc[par]
            s2 = s2m[par]
            mz = None
            if edge is not None:
                e0, e1 = (0, 128) if edge == "f" else (256, 384)
                mz = p.emit("vector", "memset", psb.ap[:, e0:e1], 0.0,
                            waits=RW((), [psb]), inc="ve")
                mz = p.emit("vector", "memset", psb.ap[:, 512 + e0:512 + e1], 0.0,
                            inc="ve")
                setw([psb], mz)
            m = None
            first = True
            for hh in range(2):
                h = 2 * pr + hh
                row = (h % 2) * 64
                for j, kb in enumerate(kbs):
                    ksl = kslots[j]
                    w = RW([qslot, ksl], [psb]) if first else RW([ksl])
                    first = False
                    m = p.emit("tensor", "matmul",
                               psb.ap[:, hh * 512 + (j0 + j) * 128:hh * 512 + (j0 + j + 1) * 128],
                               qslot.ap[row:row + 64, dc * 512 + (qb % 4) * 128:dc * 512 + (qb % 4 + 1) * 128],
                               ksl.ap[row:row + 64, dc * 512 + (kb % 4) * 128:dc * 512 + (kb % 4 + 1) * 128],
                               start=True, stop=True,
                               inc="pe" if (hh == 1 and j == nk - 1) else None,
                               waits=w)
            setw([psb], m)
            setr([qslot], m)
            for ksl in kslots:
                setr([ksl], m)
            hb3 = hbias.ap[:, 2 * pr * 384:(2 * pr + 2) * 384].rearrange(
                "p (hh c) -> p hh c", c=384)
            ps3 = psb.ap[:].rearrange("p (hh c) -> p hh c", c=512)[:, :, 0:384]
            mma = p.emit("vector", "scalar_tensor_tensor",
                         s2.ap[:].rearrange("p (hh c) -> p hh c", c=384),
                         ps3, 1.0, hb3, ALU.mult, ALU.add,
                         waits=RW([psb, hbias], [s2]), inc="ve")
            setw([s2], mma)
            setr([psb, hbias], mma)
            if edge is not None:
                bm3 = bm_[edge].ap[:].unsqueeze(1).broadcast_to([128, 2, 384])
                mma = p.emit("vector", "tensor_add",
                             s2.ap[:].rearrange("p (hh c) -> p hh c", c=384),
                             s2.ap[:].rearrange("p (hh c) -> p hh c", c=384), bm3,
                             waits=RW([bm_[edge]], [s2]), inc="ve")
                setw([s2], mma)
                setr([bm_[edge]], mma)
            mrm = p.emit("vector", "reduce_max", sm1[par].ap[:], s2.ap[:],
                         axis=AX.X, negate=True,
                         waits=RW([s2], [sm1[par]]), inc="ve")
            setw([sm1[par]], mrm)
            mex = p.emit("scalar", "activation", spA[par].ap[:], s2.ap[:], AF.Exp,
                         bias=sm1[par].ap[:],
                         waits=RW([s2, sm1[par]], [spA[par]]), inc="se")
            setw([spA[par]], mex)
            setr([s2, sm1[par]], mex)
            mrs = p.emit("vector", "reduce_sum", sm2[par].ap[:],
                         spA[par].ap[:].rearrange("p (hh c) -> p hh c", c=384),
                         axis=AX.X, waits=RW([spA[par]], [sm2[par]]), inc="ve")
            setw([sm2[par]], mrs)
            mri = p.emit("vector", "reciprocal", sd_[par].ap[:], sm2[par].ap[:],
                         waits=RW([sm2[par]], [sd_[par]]), inc="ve")
            setw([sd_[par]], mri)
            setr([sm2[par]], mri)
            rinv3 = sd_[par].ap[:].unsqueeze(2).broadcast_to([128, 2, 384])
            mA = p.emit("vector", "tensor_mul",
                        sA[par].ap[:].rearrange("p (hh c) -> p hh c", c=384),
                        spA[par].ap[:].rearrange("p (hh c) -> p hh c", c=384), rinv3,
                        waits=RW([spA[par], sd_[par]], [sA[par]]), inc="ve")
            setw([sA[par]], mA)
            setr([spA[par], sd_[par]], mA)
            m = None
            ntr = 0
            for hh in range(2):
                for j in range(nk):
                    ntr += 1
                    w = RW([sA[par], iden], [ps_tr]) if ntr == 1 else ()
                    m = p.emit("tensor", "transpose",
                               ps_tr.ap[:, hh * 384 + j * 128:hh * 384 + (j + 1) * 128],
                               sA[par].ap[:, hh * 384 + (j0 + j) * 128:hh * 384 + (j0 + j + 1) * 128],
                               iden.ap[:], waits=w,
                               inc="pe" if ntr == 2 * nk else None)
            setw([ps_tr], m)
            setr([sA[par], iden], m)
            mat = p.emit("scalar", "copy", sAT[par].ap[:, 0:768], ps_tr.ap[:, 0:768],
                         waits=RW([ps_tr], [sAT[par]]), inc="se")
            setw([sAT[par]], mat)
            setr([ps_tr], mat)
            m = None
            for hh in range(2):
                h = 2 * pr + hh
                row = (h % 2) * 64
                pav_half = pav_all[hh]
                for j, kb in enumerate(kbs):
                    vsl = v_ring[kb % 6]
                    w = RW([sAT[par], vsl], [pav_half]) if j == 0 else RW([vsl])
                    m = p.emit("tensor", "matmul", pav_t[row:row + 64, 0:128],
                               vsl.ap[:, h * 64:(h + 1) * 64],
                               sAT[par].ap[:, hh * 384 + j * 128:hh * 384 + (j + 1) * 128],
                               start=(j == 0), stop=(j == nk - 1),
                               waits=w, inc="pe" if j == nk - 1 else None)
                setw([pav_all[hh]], m)
                for kb in kbs:
                    setr([v_ring[kb % 6]], m)
            setr([sAT[par]], m)
            mev = p.emit("vector", "tensor_copy",
                         attnf[qb % 2].ap[:, dc * 128:(dc + 1) * 128],
                         pav_t[:, 0:128],
                         waits=RW([pav_lo, pav_hi], [attnf[qb % 2]]), inc="ve")
            setw([attnf[qb % 2]], mev)
            setr([pav_lo, pav_hi], mev)
            yield

    def emit_post(qb):
        af = attnf[qb % 2]
        x1 = x1b[qb % 2]
        xs = x_ring[qb % NXR]
        for half in range(2):
            pb = pp[half]
            m = None
            for dc in range(NC_):
                w = RW([af, swo], [pb]) if dc == 0 else ()
                m = p.emit("tensor", "matmul", pb.ap[:, 0:512],
                           af.ap[:, dc * 128:(dc + 1) * 128],
                           swo.ap[:, dc * DM + half * 512:dc * DM + (half + 1) * 512],
                           start=(dc == 0), stop=(dc == NC_ - 1),
                           waits=w, inc="pe" if dc == NC_ - 1 else None)
            setw([pb], m)
            setr([af, swo], m)
            m2 = p.emit("vector", "tensor_add", x1.ap[:, half * 512:(half + 1) * 512],
                        pb.ap[:, 0:512], xs.ap[:, half * 512:(half + 1) * 512],
                        waits=RW([pb, xs], [x1]), inc="ve")
            setw([x1], m2)
            setr([pb, xs], m2)
            yield
        mdx = p.emit("sync", "dma_start", out=x1_h[qb * 128:(qb + 1) * 128, :],
                     in_=x1.ap[:], waits=RW([x1]), inc=p.mksem(f"dx1_{qb % 2}"),
                     inc_n=16)
        setr([x1], mdx)
        yield
        yield from emit_ln(x1, st2[qb % 2], xn2hit[qb % 2], sqdmp[qb % 2], out_f32=xn2f32[qb % 2])
        mlo = p.emit("vector", "tensor_sub", xn2lot[qb % 2].ap[:],
                     xn2f32[qb % 2].ap[:], xn2hit[qb % 2].ap[:],
                     waits=RW([xn2f32[qb % 2], xn2hit[qb % 2]], [xn2lot[qb % 2]]),
                     inc="ve")
        setw([xn2lot[qb % 2]], mlo)
        setr([xn2f32[qb % 2], xn2hit[qb % 2]], mlo)
        yield
        hi = xn2hif[qb % 2]
        lo = xn2lof[qb % 2]
        yield from emit_transpose8(xn2hit[qb % 2], hi,
                        hi.ap[:].rearrange("p (dc c) -> p dc c", c=128), "vector")
        yield from emit_transpose8(xn2lot[qb % 2], lo,
                        lo.ap[:].rearrange("p (dc c) -> p dc c", c=128), "scalar")
        mdn = p.emit("sync", "dma_start",
                     out=xn2_h[qb].rearrange("dc p c -> p dc c"),
                     in_=hi.ap[:].rearrange("p (dc c) -> p dc c", c=128),
                     waits=RW([hi]), inc=p.mksem("dxn"), inc_n=16)
        setr([hi], mdn)
        m = None
        first = True
        trips = ([(hi, wrhi)] * NC_ + [(hi, wrlo)] * NC_ + [(lo, wrhi)] * NC_)
        for i, (xf, wr_s) in enumerate(trips):
            dc = i % NC_
            w = RW([hi, lo, wrhi, wrlo], [pav_lo, pav_hi]) if first else ()
            m = p.emit("tensor", "matmul", pav_t[:, 0:NE],
                       xf.ap[:, dc * 128:(dc + 1) * 128],
                       wr_s.ap[:, dc * NE:(dc + 1) * NE],
                       start=first, stop=(i == len(trips) - 1),
                       waits=w, inc="pe" if i == len(trips) - 1 else None)
            first = False
        setw([pav_lo, pav_hi], m)
        setr([hi, lo, wrhi, wrlo], m)
        par = qb % 2
        mlg = p.emit("vector", "tensor_copy", slog[par].ap[:], pav_t[:, 0:NE],
                     waits=RW([pav_lo, pav_hi], [slog[par]]), inc="ve")
        setw([slog[par]], mlg)
        setr([pav_lo, pav_hi], mlg)
        yield
        mm1 = p.emit("vector", "reduce_max", rm1[par].ap[:], slog[par].ap[:], axis=AX.X,
                     waits=RW([slog[par]], [rm1[par]]), inc="ve")
        setw([rm1[par]], mm1)
        mi1 = p.emit("vector", "tensor_single_scalar", sind1[par].ap[:], slog[par].ap[:],
                     rm1[par].ap[:], ALU.is_ge,
                     waits=RW([slog[par], rm1[par]], [sind1[par]]), inc="ve")
        setw([sind1[par]], mi1)
        setr([rm1[par]], mi1)
        yield
        mlk = p.emit("vector", "scalar_tensor_tensor", slk[par].ap[:], sind1[par].ap[:],
                     -1.0e30, slog[par].ap[:], ALU.mult, ALU.add,
                     waits=RW([sind1[par], slog[par]], [slk[par]]), inc="ve")
        setw([slk[par]], mlk)
        setr([sind1[par], slog[par]], mlk)
        mm2_ = p.emit("vector", "reduce_max", rm2[par].ap[:], slk[par].ap[:], axis=AX.X,
                      waits=RW([slk[par]], [rm2[par]]), inc="ve")
        setw([rm2[par]], mm2_)
        mi2 = p.emit("vector", "tensor_single_scalar", sind2[par].ap[:], slk[par].ap[:],
                     rm2[par].ap[:], ALU.is_ge,
                     waits=RW([slk[par], rm2[par]], [sind2[par]]), inc="ve")
        setw([sind2[par]], mi2)
        setr([slk[par], rm2[par]], mi2)
        yield
        md = p.emit("vector", "tensor_sub", rdd[par].ap[:], rm2[par].ap[:], rm1[par].ap[:],
                    waits=RW([rm2[par], rm1[par]], [rdd[par]]), inc="ve")
        setw([rdd[par]], md)
        setr([rm2[par], rm1[par]], md)
        me = p.emit("scalar", "activation", se_[par].ap[:], rdd[par].ap[:], AF.Exp,
                    waits=RW([rdd[par]], [se_[par]]), inc="se")
        setw([se_[par]], me)
        setr([rdd[par]], me)
        mt = p.emit("vector", "tensor_scalar_add", st_[par].ap[:], se_[par].ap[:], 1.0,
                    waits=RW([se_[par]], [st_[par]]), inc="ve")
        setw([st_[par]], mt)
        mw1 = p.emit("vector", "reciprocal", sw1[par].ap[:], st_[par].ap[:],
                     waits=RW([st_[par]], [sw1[par]]), inc="ve")
        setw([sw1[par]], mw1)
        setr([st_[par]], mw1)
        yield
        mw1s = p.emit("vector", "tensor_scalar_mul", sw1s[par].ap[:], sw1[par].ap[:],
                      DEPTH_SCALE, waits=RW([sw1[par]], [sw1s[par]]), inc="ve")
        setw([sw1s[par]], mw1s)
        setr([sw1[par]], mw1s)
        mw2s = p.emit("vector", "tensor_mul", sw2s[par].ap[:], se_[par].ap[:],
                      sw1s[par].ap[:], waits=RW([se_[par], sw1s[par]], [sw2s[par]]),
                      inc="ve")
        setw([sw2s[par]], mw2s)
        setr([se_[par]], mw2s)
        mc1 = p.emit("vector", "tensor_single_scalar", sc1_[par].ap[:], sind1[par].ap[:],
                     sw1s[par].ap[:], ALU.mult,
                     waits=RW([sind1[par], sw1s[par]], [sc1_[par]]), inc="ve")
        setw([sc1_[par]], mc1)
        setr([sind1[par], sw1s[par]], mc1)
        mcb = p.emit("vector", "scalar_tensor_tensor",
                     comb.ap[:, qb * NE:(qb + 1) * NE], sind2[par].ap[:],
                     sw2s[par].ap[:], sc1_[par].ap[:], ALU.mult, ALU.add,
                     waits=RW([sind2[par], sw2s[par], sc1_[par]], [comb]), inc="ve")
        setw([comb], mcb)
        setr([sind2[par], sw2s[par], sc1_[par]], mcb)

    def sec_gen(sbk):
        blocks = [4 * sbk + t for t in range(4)]
        xn1 = xn1r[sbk % 2]
        for t in range(4):
            b = blocks[t]
            yield from emit_ln(x_ring[b % NXR], st1[t % 2], xnst[t % 2], sqdmp[t % 2])
            dst3 = xn1.ap[:].rearrange("p (dc t c) -> p dc t c", dc=NC_, t=4, c=128)[:, :, t, :]
            yield from emit_transpose8(xnst[t % 2], xn1, dst3, "vector")
        yield from emit_proj_fm(swk, xn1, k_ring[sbk % 2])
        yield from emit_v_tm(xn1, sbk, blocks)
        yield from emit_proj_fm(swq, xn1, q_ring[sbk % 2], scale=0.125)
        if debug and sbk == 0:
            for src_b, dst in ((xn1r[0], dbg_xn1), (k_ring[0], dbg_k),
                               (q_ring[0], dbg_q)):
                mdbg = p.emit("sync", "dma_start", out=dst[:], in_=src_b.ap[:],
                              waits=RW([src_b]), inc="dx", inc_n=16)
                setr([src_b], mdbg)
            for i in range(2):
                mdbg = p.emit("sync", "dma_start", out=dbg_v[:, i*1024:(i+1)*1024],
                              in_=v_ring[i].ap[:], waits=RW([v_ring[i]]),
                              inc="dx", inc_n=16)
                setr([v_ring[i]], mdbg)

    def emit_prefetch(sbk):
        for t in range(4):
            b = 4 * (sbk + 1) + t
            xr = x_ring[b % NXR]
            m = p.emit("sync", "dma_start", out=xr.ap[:],
                       in_=x_h[b * 128:(b + 1) * 128, :],
                       waits=RW((), [xr]), inc=p.mksem(f"dxx{b % NXR}"), inc_n=16)
            setw([xr], m)

    pending_post = None
    drain(sec_gen(0))
    for sbk in range(NSB):
        qbs = [0, 1, 2] if sbk == 0 else [4 * sbk - 1, 4 * sbk, 4 * sbk + 1, 4 * sbk + 2]
        for qb in qbs:
            weave(emit_attention(qb), pending_post)
            pending_post = emit_post(qb)
        if sbk + 1 < NSB:
            for _ in range(3):
                next(pending_post, None)
            emit_prefetch(sbk)
            weave(sec_gen(sbk + 1), pending_post)
            pending_post = None
    weave(emit_attention(NB - 1), pending_post)
    drain(emit_post(NB - 1))
    if debug:
        m = p.emit("sync", "dma_start", out=comb_h[:], in_=comb.ap[:],
                   waits=RW([comb]), inc="dx", inc_n=16)
        setr([comb], m)

    sync_sems = [s for s in p.cnt if s.startswith(("dxx", "dx1")) or
                 s in ("dx", "dxn", "did", "dre", "dsl", "dwr")]
    own = {"sync": sync_sems, "gpsimd": ["dg", "dwst"], "vector": ["ve"],
           "scalar": ["se"], "tensor": ["pe"]}
    for eng in Prog.ENGS:
        p.wait_only(eng, [(s, p.cnt[s]) for s in own[eng] if s in p.cnt])
        p.emit(eng, "sem_inc", p.sems["pb"], 1)
        p.cnt["pb"] += 1
    barrier = ("pb", p.cnt["pb"])
    for eng in Prog.ENGS:
        p.wait_only(eng, [barrier])

    psA.close()
    esA.close()

    esB = ExitStack()
    sxn2 = [sb(f"sxn2_{dc}", [128, TG], BF16, esB) for dc in range(NC_)]
    oacc = [sb(f"oacc{t}", [128, DM], F32, esB) for t in range(TG // 128)]
    swgb = [sb(f"swgb{i}", [128, NC_ * 1024], BF16, esB) for i in range(2)]
    swub = [sb(f"swub{i}", [128, NC_ * 1024], BF16, esB) for i in range(2)]
    swdb = [sb(f"swdb{i}", [128, NC_ * 1024], BF16, esB) for i in range(2)]
    hbuf = [[sb(f"h{s}_{i}", [128, 512], BF16, esB) for i in range(8)] for s in range(2)]
    ssil = [sb(f"ssil{i}", [128, 512], F32, esB) for i in range(2)]

    psB = ExitStack()
    pg = [psum(f"pg{i}", [128, 512], F32, psB) for i in range(2)]
    pu = [psum(f"pu{i}", [128, 512], F32, psB) for i in range(2)]
    peo = [psum(f"peo{i}", [128, 1024], F32, psB) for i in range(2)]

    chunk_idx = 0
    for g in range(NTG):
        for dc in range(NC_):
            m = p.emit("sync", "dma_start",
                       out=sxn2[dc].ap[:].rearrange("p (t c) -> p t c", c=128),
                       in_=xn2_h[g * (TG // 128):(g + 1) * (TG // 128), dc].rearrange("t p c -> p t c"),
                       waits=RW((), [sxn2[dc]]), inc=p.mksem(f"dsx{dc}"), inc_n=16)
            setw([sxn2[dc]], m)
        for tt in range(TG // 128):
            row0 = (g * (TG // 128) + tt) * 128
            m = p.emit("sync", "dma_start", out=oacc[tt].ap[:],
                       in_=x1_h[row0:row0 + 128, :],
                       waits=RW((), [oacc[tt]]), inc=p.mksem(f"doa{tt}"), inc_n=16)
            setw([oacc[tt]], m)
        for e in range(NE):
            for q in range(NQ):
                par = chunk_idx % 2
                wgb, wub, wdb = swgb[par], swub[par], swdb[par]
                mwg = p.emit("gpsimd", "dma_start",
                             out=wgb.ap[:].rearrange("p (dc c) -> p dc c", c=1024),
                             in_=wg_h[e][:, q * 1024:(q + 1) * 1024].rearrange(
                                 "(dc p) c -> p dc c", p=128),
                             waits=RW((), [wgb]), inc=p.mksem(f"dwg{par}"), inc_n=16)
                setw([wgb], mwg)
                mwu = p.emit("gpsimd", "dma_start",
                             out=wub.ap[:].rearrange("p (dc c) -> p dc c", c=1024),
                             in_=wu_h[e][:, q * 1024:(q + 1) * 1024].rearrange(
                                 "(dc p) c -> p dc c", p=128),
                             waits=RW((), [wub]), inc=p.mksem(f"dwu{par}"), inc_n=16)
                setw([wub], mwu)
                mwd = p.emit("gpsimd", "dma_start",
                             out=wdb.ap[:].rearrange("p (hc c) -> p hc c", c=1024),
                             in_=wd_h[e][q * 1024:(q + 1) * 1024, :].rearrange(
                                 "(hc p) c -> p hc c", p=128),
                             waits=RW((), [wdb]), inc=p.mksem(f"dwd{par}"), inc_n=16)
                setw([wdb], mwd)
                for ts in range(TSB):
                    hset = hbuf[ts % 2]
                    for hc in range(8):
                        pgb, pub = pg[hc % 2], pu[hc % 2]
                        m = None
                        for dc in range(NC_):
                            w = RW([wgb, sxn2[dc]], [pgb]) if dc == 0 else RW([sxn2[dc]])
                            m = p.emit("tensor", "matmul", pgb.ap[:, 0:512],
                                       wgb.ap[:, dc * 1024 + hc * 128:dc * 1024 + (hc + 1) * 128],
                                       sxn2[dc].ap[:, ts * 512:(ts + 1) * 512],
                                       start=(dc == 0), stop=(dc == NC_ - 1),
                                       waits=w, inc="pe" if dc == NC_ - 1 else None)
                        setw([pgb], m)
                        setr([wgb], m)
                        for dc in range(NC_):
                            setr([sxn2[dc]], m)
                        m = None
                        for dc in range(NC_):
                            w = RW([wub], [pub]) if dc == 0 else ()
                            m = p.emit("tensor", "matmul", pub.ap[:, 0:512],
                                       wub.ap[:, dc * 1024 + hc * 128:dc * 1024 + (hc + 1) * 128],
                                       sxn2[dc].ap[:, ts * 512:(ts + 1) * 512],
                                       start=(dc == 0), stop=(dc == NC_ - 1),
                                       waits=w, inc="pe" if dc == NC_ - 1 else None)
                        setw([pub], m)
                        setr([wub], m)
                        msil = p.emit("scalar", "activation", ssil[hc % 2].ap[:],
                                      pgb.ap[:, 0:512], AF.Silu,
                                      waits=RW([pgb], [ssil[hc % 2]]), inc="se")
                        setw([ssil[hc % 2]], msil)
                        setr([pgb], msil)
                        mh = p.emit("vector", "tensor_mul", hset[hc].ap[:],
                                    ssil[hc % 2].ap[:], pub.ap[:, 0:512],
                                    waits=RW([ssil[hc % 2], pub], [hset[hc]]), inc="ve")
                        setw([hset[hc]], mh)
                        setr([ssil[hc % 2], pub], mh)
                    for tl in range(4):
                        tt = ts * 4 + tl
                        pe_ = peo[tt % 2]
                        m = None
                        nmm = 0
                        for hc in range(8):
                            for half in range(2):
                                nmm += 1
                                w = RW([hset[hc], wdb], [pe_]) if hc == 0 else RW([hset[hc]])
                                m = p.emit("tensor", "matmul",
                                           pe_.ap[:, half * 512:(half + 1) * 512],
                                           hset[hc].ap[:, tl * 128:(tl + 1) * 128],
                                           wdb.ap[:, hc * 1024 + half * 512:hc * 1024 + (half + 1) * 512],
                                           start=(hc == 0), stop=(hc == 7),
                                           waits=w, inc="pe" if nmm == 16 else None)
                        setw([pe_], m)
                        setr([wdb], m)
                        for hc in range(8):
                            setr([hset[hc]], m)
                        cap = comb.ap[:, (g * (TG // 128) + tt) * NE + e:
                                      (g * (TG // 128) + tt) * NE + e + 1]
                        mev = p.emit("vector", "scalar_tensor_tensor",
                                     oacc[tt].ap[:], pe_.ap[:, 0:1024], cap,
                                     oacc[tt].ap[:], ALU.mult, ALU.add,
                                     waits=RW([pe_, comb, oacc[tt]], [oacc[tt]]),
                                     inc="ve")
                        setw([oacc[tt]], mev)
                        setr([pe_, comb], mev)
                chunk_idx += 1
        for tt in range(TG // 128):
            row0 = (g * (TG // 128) + tt) * 128
            m = p.emit("sync", "dma_start", out=out_h[row0:row0 + 128, :],
                       in_=oacc[tt].ap[:], waits=RW([oacc[tt]]),
                       inc=p.mksem(f"doa{tt}"), inc_n=16)
            setr([oacc[tt]], m)

    p.wait_only("sync", [(f"doa{tt}", p.cnt[f"doa{tt}"]) for tt in range(TG // 128)])

    psB.close()
    esB.close()

    with nc.Block() as block:
        @block.sync
        def _(eng):
            p.replay("sync", eng)

        @block.gpsimd
        def _(eng):
            p.replay("gpsimd", eng)

        @block.vector
        def _(eng):
            p.replay("vector", eng)

        @block.scalar
        def _(eng):
            p.replay("scalar", eng)

        @block.tensor
        def _(eng):
            p.replay("tensor", eng)

    es.close()
    return nc


def _host_constants():
    ql = np.arange(128)[:, None]
    kl = np.arange(384)[None, :]
    band = (kl >= ql) & (kl <= ql + 256)
    mk = lambda b: np.where(b, np.float32(0.0), np.float32(NEG)).astype(np.float32)
    bm_f = mk(band & (kl >= 128))
    bm_l = mk(band & (kl < 256))
    rel = (128 + ql - kl).astype(np.float32)
    relm = np.where(band, rel, np.float32(NEG)).astype(np.float32)
    iden = np.eye(128, dtype=ml_dtypes.bfloat16)
    return relm, bm_f, bm_l, iden


def kernel(L0_attn_x, L0_attn_Wq, L0_attn_Wk, L0_attn_Wv, L0_attn_Wo,
           L0_attn_slopes, L0_ffn_Wr, L0_ffn_Wg, L0_ffn_Wu, L0_ffn_Wd):
    x = np.ascontiguousarray(np.asarray(L0_attn_x, dtype=np.float32))
    relm, bm_f, bm_l, iden = _host_constants()
    common = {
        "wq": np.ascontiguousarray(np.asarray(L0_attn_Wq, dtype=np.float32)),
        "wk": np.ascontiguousarray(np.asarray(L0_attn_Wk, dtype=np.float32)),
        "wv": np.ascontiguousarray(np.asarray(L0_attn_Wv, dtype=np.float32)),
        "wo": np.ascontiguousarray(np.asarray(L0_attn_Wo, dtype=np.float32)),
        "slopes": np.ascontiguousarray(np.asarray(L0_attn_slopes, dtype=np.float32)),
        "wr": np.ascontiguousarray(np.asarray(L0_ffn_Wr, dtype=np.float32)),
        "wg": np.ascontiguousarray(np.asarray(L0_ffn_Wg, dtype=np.float32)),
        "wu": np.ascontiguousarray(np.asarray(L0_ffn_Wu, dtype=np.float32)),
        "wd": np.ascontiguousarray(np.asarray(L0_ffn_Wd, dtype=np.float32)),
        "relm": relm,
        "bm_f": bm_f,
        "bm_l": bm_l,
        "iden": iden,
    }
    nc = build_nc()
    in_maps = [dict(common, x=np.ascontiguousarray(x[i])) for i in range(8)]
    res = run_bass_kernel_spmd(nc, in_maps, core_ids=list(range(8)))
    return np.stack([res.results[i]["out"] for i in range(8)], axis=0)


# version 16
# speedup vs baseline: 1.1613x; 1.0307x over previous
import numpy as np
import ml_dtypes
from contextlib import ExitStack

import concourse.bass as bass
import concourse.mybir as mybir
from concourse.bass_utils import run_bass_kernel_spmd

F32 = mybir.dt.float32
BF16 = mybir.dt.bfloat16
AF = mybir.ActivationFunctionType
ALU = mybir.AluOpType
AX = mybir.AxisListType

S, DM, NH, DK = 4096, 1024, 16, 64
NB = S // 128
NSB = NB // 4
NC_ = DM // 128
NE, HID = 8, 4096
DEPTH_SCALE = float(1.0 / np.sqrt(12.0))
NEG = -1.0e9
EPS = 1e-5

TG = 1024
NTG = S // TG
NQ = 4
TSB = TG // 512


class Buf:
    __slots__ = ("ap", "w", "r")

    def __init__(self, ap):
        self.ap = ap
        self.w = None
        self.r = {}


class Prog:
    ENGS = ("sync", "gpsimd", "vector", "scalar", "tensor")

    def __init__(self, nc, es):
        self.nc = nc
        self.es = es
        self.q = {e: [] for e in self.ENGS}
        self.sems = {}
        self.cnt = {}
        for name in ("pe", "ve", "se", "dx", "dg", "do", "pb"):
            self.mksem(name)

    def mksem(self, name):
        if name not in self.sems:
            self.sems[name] = self.es.enter_context(self.nc.semaphore(name))
            self.cnt[name] = 0
        return name

    def emit(self, eng, fn, *a, waits=(), inc=None, inc_n=1, **kw):
        ws = [w for w in waits if w is not None]
        incs = []
        ms = None
        if inc is not None:
            self.cnt[inc] += inc_n
            ms = (inc, self.cnt[inc])
            incs.append((inc, inc_n))
        self.q[eng].append([fn, a, kw, ws, incs])
        return ms

    def wait_only(self, eng, waits):
        ws = [w for w in waits if w is not None]
        if ws:
            self.q[eng].append([None, (), {}, ws, []])

    def add_inc_to_last(self, eng, sem, n):
        assert self.q[eng], eng
        self.q[eng][-1][4].append((sem, n))
        self.cnt[sem] += n
        return (sem, self.cnt[sem])

    def replay(self, eng_name, eng):
        seen = {}
        for fn, a, kw, ws, incs in self.q[eng_name]:
            for (s, v) in ws:
                if seen.get(s, -1) >= v:
                    continue
                seen[s] = v
                eng.wait_ge(self.sems[s], v)
            if fn is None:
                continue
            inst = getattr(eng, fn)(*a, **kw)
            for (s, n) in incs:
                inst.then_inc(self.sems[s], n)


def RW(reads=(), writes=()):
    ws = []
    for b in reads:
        if b.w:
            ws.append(b.w)
    for b in writes:
        if b.w:
            ws.append(b.w)
        ws.extend(b.r.items())
    return ws


def setw(bufs, ms):
    for b in bufs:
        b.w = ms
        b.r = {}


def setr(bufs, ms):
    if ms is None:
        return
    for b in bufs:
        old = b.r.get(ms[0], -1)
        if ms[1] > old:
            b.r[ms[0]] = ms[1]


def weave(*gens):
    alive = [g for g in gens if g is not None]
    while alive:
        for g in list(alive):
            try:
                next(g)
            except StopIteration:
                alive.remove(g)


def drain(g):
    for _ in g:
        pass


def build_nc(debug=False):
    nc = bass.Bass("TRN2", target_bir_lowering=False)

    x_h = nc.dram_tensor("x", [S, DM], F32, kind="ExternalInput").ap()
    out_h = nc.dram_tensor("out", [S, DM], F32, kind="ExternalOutput").ap()
    wq_h = nc.dram_tensor("wq", [DM, DM], F32, kind="ExternalInput").ap()
    wk_h = nc.dram_tensor("wk", [DM, DM], F32, kind="ExternalInput").ap()
    wv_h = nc.dram_tensor("wv", [DM, DM], F32, kind="ExternalInput").ap()
    wo_h = nc.dram_tensor("wo", [DM, DM], F32, kind="ExternalInput").ap()
    sl_h = nc.dram_tensor("slopes", [NH], F32, kind="ExternalInput").ap()
    wr_h = nc.dram_tensor("wr", [DM, NE], F32, kind="ExternalInput").ap()
    wg_h = nc.dram_tensor("wg", [NE, DM, HID], F32, kind="ExternalInput").ap()
    wu_h = nc.dram_tensor("wu", [NE, DM, HID], F32, kind="ExternalInput").ap()
    wd_h = nc.dram_tensor("wd", [NE, HID, DM], F32, kind="ExternalInput").ap()
    relm_h = nc.dram_tensor("relm", [128, 384], F32, kind="ExternalInput").ap()
    bmf_h = nc.dram_tensor("bm_f", [128, 384], F32, kind="ExternalInput").ap()
    bml_h = nc.dram_tensor("bm_l", [128, 384], F32, kind="ExternalInput").ap()
    iden_h = nc.dram_tensor("iden", [128, 128], BF16, kind="ExternalInput").ap()

    scr_kind = "ExternalOutput" if debug else "Internal"
    x1_h = nc.dram_tensor("x1_scr", [S, DM], F32, kind=scr_kind).ap()
    xn2_h = nc.dram_tensor("xn2_scr", [NB, NC_, 128, 128], BF16, kind=scr_kind).ap()
    comb_h = (nc.dram_tensor("comb_scr", [128, NB * NE], F32, kind="ExternalOutput").ap()
              if debug else None)
    if debug:
        dbg_xn1 = nc.dram_tensor("dbg_xn1", [128, 4096], BF16, kind="ExternalOutput").ap()
        dbg_k = nc.dram_tensor("dbg_k", [128, 4096], BF16, kind="ExternalOutput").ap()
        dbg_q = nc.dram_tensor("dbg_q", [128, 4096], BF16, kind="ExternalOutput").ap()
        dbg_v = nc.dram_tensor("dbg_v", [128, 2048], BF16, kind="ExternalOutput").ap()
        dbg_s2 = nc.dram_tensor("dbg_s2", [128, 384], F32, kind="ExternalOutput").ap()
        dbg_pA = nc.dram_tensor("dbg_pA", [128, 384], BF16, kind="ExternalOutput").ap()
        dbg_A = nc.dram_tensor("dbg_A", [128, 384], BF16, kind="ExternalOutput").ap()
        dbg_st = nc.dram_tensor("dbg_st", [128, 8], F32, kind="ExternalOutput").ap()

    es = ExitStack()
    p = Prog(nc, es)

    def sb(name, shape, dt=F32, stack=es):
        return Buf(stack.enter_context(nc.sbuf_tensor(name, shape, dt)))

    def psum(name, shape, dt=F32, stack=es):
        return Buf(stack.enter_context(nc.psum_tensor(name, shape, dt)))

    iden = sb("s_iden", [128, 128], BF16)
    bm_ = {"f": sb("s_bmf", [128, 384]), "l": sb("s_bml", [128, 384])}
    relmb = sb("s_relm", [128, 384])
    hbias = sb("s_hbias", [128, NH * 384])
    slbc = sb("slbc", [128, NH])
    comb = sb("comb", [128, NB * NE])
    wrhi = sb("wrhi", [128, NC_ * NE], BF16)
    wrlo = sb("wrlo", [128, NC_ * NE], BF16)

    esA = ExitStack()
    swq = sb("swq", [128, NC_ * DM], BF16, esA)
    swk = sb("swk", [128, NC_ * DM], BF16, esA)
    swv = sb("swv", [128, NC_ * DM], BF16, esA)
    swo = sb("swo", [128, NC_ * DM], BF16, esA)
    swrf = sb("swrf", [128, NC_ * NE], F32, esA)
    NXR = 5
    x_ring = [sb(f"xr{i}", [128, DM], F32, esA) for i in range(NXR)]
    xnst = [sb("xnst0", [128, DM], BF16, esA)] * 2
    sqdmp = xnst
    xn1r = [sb("xn1r0", [128, NC_ * 512], BF16, esA)] * 2
    k_ring = [sb(f"kr{i}", [128, NC_ * 512], BF16, esA) for i in range(2)]
    q_ring = [sb(f"qr{i}", [128, NC_ * 512], BF16, esA) for i in range(2)]
    v_ring = [sb(f"vr{i}", [128, DM], BF16, esA) for i in range(6)]
    spA = [sb(f"pA{i}", [128, 768], BF16, esA) for i in range(2)]
    s2m = [sb(f"s2m{i}", [128, 768], F32, esA) for i in range(2)]
    sA = [sb("A0", [128, 768], BF16, esA)] * 2
    sAT = [sb("AT0", [128, 768], BF16, esA)] * 2
    attnf = [sb(f"attnf{i}", [128, DM], BF16, esA) for i in range(2)]
    x1b = [sb(f"x1b{i}", [128, DM], F32, esA) for i in range(2)]
    xn2f32 = [sb("xn2f0", [128, DM], F32, esA)] * 2
    xn2hit = [sb("xn2ht0", [128, DM], BF16, esA)] * 2
    xn2lot = [sb("xn2lt0", [128, DM], BF16, esA)] * 2
    xn2hif = [sb("xn2hf0", [128, DM], BF16, esA)] * 2
    xn2lof = [sb("xn2lf0", [128, DM], BF16, esA)] * 2

    def statset(tag):
        return {k: sb(f"st_{tag}_{k}", [128, 1], F32, esA)
                for k in ("sum", "sq", "mean", "mm2", "veps", "var", "lnv", "rstd", "nmr")}

    st1 = [statset(f"a{i}") for i in range(2)]
    st2 = [statset(f"b{i}") for i in range(2)]
    slog = [sb(f"rlog{i}", [128, NE], F32, esA) for i in range(2)]
    sind1 = [sb(f"rind1{i}", [128, NE], F32, esA) for i in range(2)]
    slk = [sb(f"rlk{i}", [128, NE], F32, esA) for i in range(2)]
    sind2 = [sb(f"rind2{i}", [128, NE], F32, esA) for i in range(2)]
    sm1 = [sb(f"sxm1{i}", [128, 1], F32, esA) for i in range(2)]
    sm2 = [sb(f"sxm2{i}", [128, 2], F32, esA) for i in range(2)]
    sd_ = [sb(f"sxd{i}", [128, 2], F32, esA) for i in range(2)]
    rm1 = [sb(f"rm1{i}", [128, 1], F32, esA) for i in range(2)]
    rm2 = [sb(f"rm2{i}", [128, 1], F32, esA) for i in range(2)]
    rdd = [sb(f"rd{i}", [128, 1], F32, esA) for i in range(2)]
    se_ = [sb(f"re{i}", [128, 1], F32, esA) for i in range(2)]
    st_ = [sb(f"rt{i}", [128, 1], F32, esA) for i in range(2)]
    sw1 = [sb(f"rw1{i}", [128, 1], F32, esA) for i in range(2)]
    sw1s = [sb(f"rw1s{i}", [128, 1], F32, esA) for i in range(2)]
    sw2s = [sb(f"rw2s{i}", [128, 1], F32, esA) for i in range(2)]
    sc1_ = [sb(f"rc1{i}", [128, NE], F32, esA) for i in range(2)]

    psA = ExitStack()
    pp = [psum(f"pp{i}", [128, 512], F32, psA) for i in range(2)]
    ps_sc = [psum(f"psc{i}", [128, 1024], F32, psA) for i in range(2)]
    ps_tr = psum("ptr", [128, 1024], BF16, psA)
    pav_t = psA.enter_context(nc.psum_tensor("pav", [128, 128], F32))
    pav_lo = Buf(pav_t)
    pav_hi = Buf(pav_t)
    pav_all = (pav_lo, pav_hi)

    ms = p.emit("sync", "dma_start", out=iden.ap[:], in_=iden_h[:], inc=p.mksem("did"), inc_n=16)
    setw([iden], ms)
    for key, hbm in (("f", bmf_h), ("l", bml_h)):
        ms = p.emit("sync", "dma_start", out=bm_[key].ap[:], in_=hbm[:],
                    inc=p.mksem(f"dbm{key}"), inc_n=16)
        setw([bm_[key]], ms)
    ms = p.emit("sync", "dma_start", out=relmb.ap[:], in_=relm_h[:],
                inc=p.mksem("dre"), inc_n=16)
    setw([relmb], ms)
    ms = p.emit("sync", "dma_start", out=slbc.ap[:],
                in_=sl_h.unsqueeze(0).broadcast_to([128, NH]), inc=p.mksem("dsl"), inc_n=16)
    setw([slbc], ms)
    wstage = k_ring[0]
    for h in range(NH):
        ms = p.emit("scalar", "activation",
                    hbias.ap[:, h * 384:(h + 1) * 384], relmb.ap[:],
                    AF.Identity, scale=slbc.ap[:, h:h + 1],
                    waits=RW([relmb, slbc], [hbias]), inc="se")
        setw([hbias], ms)
    setr([relmb, slbc], ms)
    for w_hbm, w_sb in ((wq_h, swq), (wk_h, swk), (wv_h, swv), (wo_h, swo)):
        for half in range(2):
            ms = p.emit("gpsimd", "dma_start",
                        out=wstage.ap[:].rearrange("p (oc c) -> p oc c", c=DM),
                        in_=w_hbm[half * 512:(half + 1) * 512, :].rearrange(
                            "(oc p) c -> p oc c", p=128),
                        waits=RW((), [wstage]), inc=p.mksem("dwst"), inc_n=16)
            setw([wstage], ms)
            for dc in range(NC_):
                m = None
                for ocl in range(4):
                    w = RW([wstage, iden], [ps_tr]) if ocl == 0 else ()
                    m = p.emit("tensor", "transpose",
                               ps_tr.ap[:, ocl * 128:(ocl + 1) * 128],
                               wstage.ap[:, ocl * DM + dc * 128:ocl * DM + (dc + 1) * 128],
                               iden.ap[:], waits=w, inc="pe" if ocl == 3 else None)
                setw([ps_tr], m)
                setr([wstage, iden], m)
                m2 = p.emit("vector", "tensor_copy",
                            w_sb.ap[:, dc * DM + half * 512:dc * DM + (half + 1) * 512],
                            ps_tr.ap[:, 0:512],
                            waits=RW([ps_tr], [w_sb]), inc="ve")
                setw([w_sb], m2)
                setr([ps_tr], m2)
    ms = p.emit("sync", "dma_start",
                out=swrf.ap[:].rearrange("p (dc c) -> p dc c", c=NE),
                in_=wr_h.rearrange("(dc p) c -> p dc c", p=128),
                inc=p.mksem("dwr"), inc_n=16)
    setw([swrf], ms)
    ms = p.emit("vector", "tensor_copy", wrhi.ap[:], swrf.ap[:],
                waits=RW([swrf], [wrhi]), inc="ve")
    setw([wrhi], ms)
    setr([swrf], ms)
    ms = p.emit("vector", "tensor_sub", wrlo.ap[:], swrf.ap[:], wrhi.ap[:],
                waits=RW([swrf, wrhi], [wrlo]), inc="ve")
    setw([wrlo], ms)
    setr([swrf, wrhi], ms)
    for t in range(4):
        ms = p.emit("sync", "dma_start", out=x_ring[t].ap[:],
                    in_=x_h[t * 128:(t + 1) * 128, :],
                    inc=p.mksem(f"dxx{t % NXR}"), inc_n=16)
        setw([x_ring[t]], ms)

    def emit_ln(xbuf, stats, out_bf, sqd, out_f32=None):
        m = p.emit("vector", "reduce_sum", stats["sum"].ap[:], xbuf.ap[:], axis=AX.X,
                   waits=RW([xbuf], [stats["sum"]]), inc="ve")
        setw([stats["sum"]], m); setr([xbuf], m)
        yield
        m = p.emit("scalar", "activation", sqd.ap[:], xbuf.ap[:], AF.Square,
                   accum_out=stats["sq"].ap[:],
                   waits=RW([xbuf], [sqd, stats["sq"]]), inc="se")
        setw([sqd, stats["sq"]], m); setr([xbuf], m)
        yield
        m = p.emit("vector", "tensor_scalar_mul", stats["mean"].ap[:],
                   stats["sum"].ap[:], 1.0 / DM,
                   waits=RW([stats["sum"]], [stats["mean"]]), inc="ve")
        setw([stats["mean"]], m); setr([stats["sum"]], m)
        yield
        m = p.emit("vector", "tensor_mul", stats["mm2"].ap[:], stats["mean"].ap[:],
                   stats["mean"].ap[:], waits=RW([stats["mean"]], [stats["mm2"]]), inc="ve")
        setw([stats["mm2"]], m); setr([stats["mean"]], m)
        m = p.emit("vector", "tensor_scalar", stats["veps"].ap[:],
                   stats["sq"].ap[:], 1.0 / DM, EPS, ALU.mult, ALU.add,
                   waits=RW([stats["sq"]], [stats["veps"]]), inc="ve")
        setw([stats["veps"]], m); setr([stats["sq"]], m)
        yield
        m = p.emit("vector", "tensor_sub", stats["var"].ap[:],
                   stats["veps"].ap[:], stats["mm2"].ap[:],
                   waits=RW([stats["veps"], stats["mm2"]], [stats["var"]]), inc="ve")
        setw([stats["var"]], m); setr([stats["veps"], stats["mm2"]], m)
        yield
        m = p.emit("scalar", "activation", stats["lnv"].ap[:], stats["var"].ap[:],
                   AF.Ln, waits=RW([stats["var"]], [stats["lnv"]]), inc="se")
        setw([stats["lnv"]], m); setr([stats["var"]], m)
        yield
        m = p.emit("scalar", "activation", stats["rstd"].ap[:], stats["lnv"].ap[:],
                   AF.Exp, scale=-0.5, waits=RW([stats["lnv"]], [stats["rstd"]]), inc="se")
        setw([stats["rstd"]], m); setr([stats["lnv"]], m)
        yield
        m = p.emit("vector", "scalar_tensor_tensor", stats["nmr"].ap[:],
                   stats["mean"].ap[:], -1.0, stats["rstd"].ap[:], ALU.mult, ALU.mult,
                   waits=RW([stats["mean"], stats["rstd"]], [stats["nmr"]]), inc="ve")
        setw([stats["nmr"]], m); setr([stats["mean"], stats["rstd"]], m)
        yield
        m = p.emit("scalar", "activation", out_bf.ap[:], xbuf.ap[:], AF.Identity,
                   bias=stats["nmr"].ap[:], scale=stats["rstd"].ap[:],
                   waits=RW([xbuf, stats["nmr"], stats["rstd"]], [out_bf]), inc="se")
        setw([out_bf], m); setr([xbuf, stats["nmr"], stats["rstd"]], m)
        yield
        if out_f32 is not None:
            m = p.emit("scalar", "activation", out_f32.ap[:], xbuf.ap[:], AF.Identity,
                       bias=stats["nmr"].ap[:], scale=stats["rstd"].ap[:],
                       waits=RW([xbuf, stats["nmr"], stats["rstd"]], [out_f32]), inc="se")
            setw([out_f32], m); setr([xbuf, stats["nmr"], stats["rstd"]], m)

    def emit_transpose8(src_tm, dst_fm, dst_ap3, evac_eng):
        m = None
        for dc in range(NC_):
            w = RW([src_tm, iden], [ps_tr]) if dc == 0 else ()
            m = p.emit("tensor", "transpose", ps_tr.ap[:, dc * 128:(dc + 1) * 128],
                       src_tm.ap[:, dc * 128:(dc + 1) * 128], iden.ap[:],
                       waits=w, inc="pe" if dc == NC_ - 1 else None)
        setw([ps_tr], m)
        setr([src_tm, iden], m)
        sem = "ve" if evac_eng == "vector" else "se"
        fn = "tensor_copy" if evac_eng == "vector" else "copy"
        m2 = p.emit(evac_eng, fn, dst_ap3,
                    ps_tr.ap[:].rearrange("p (dc c) -> p dc c", c=128),
                    waits=RW([ps_tr], [dst_fm]), inc=sem)
        setw([dst_fm], m2)
        setr([ps_tr], m2)
        yield

    def emit_proj_fm(w_sb, xn_slot, dst_slot, scale=None):
        for dout_c in range(NC_):
            pb = pp[dout_c % 2]
            m = None
            for dci in range(NC_):
                w = RW([w_sb, xn_slot], [pb]) if dci == 0 else ()
                m = p.emit("tensor", "matmul", pb.ap[:, 0:512],
                           w_sb.ap[:, dci * DM + dout_c * 128:dci * DM + (dout_c + 1) * 128],
                           xn_slot.ap[:, dci * 512:(dci + 1) * 512],
                           start=(dci == 0), stop=(dci == NC_ - 1),
                           waits=w, inc="pe" if dci == NC_ - 1 else None)
            setw([pb], m)
            setr([w_sb, xn_slot], m)
            dst = dst_slot.ap[:, dout_c * 512:(dout_c + 1) * 512]
            if scale is None:
                m2 = p.emit("vector", "tensor_copy", dst, pb.ap[:, 0:512],
                            waits=RW([pb], [dst_slot]), inc="ve")
            else:
                m2 = p.emit("scalar", "activation", dst, pb.ap[:, 0:512], AF.Copy,
                            scale=scale, waits=RW([pb], [dst_slot]), inc="se")
            setw([dst_slot], m2)
            setr([pb], m2)
            yield

    def emit_v_tm(xn_slot, sbk, blocks):
        for t in range(4):
            b = blocks[t]
            vslot = v_ring[b % 6]
            for half in range(2):
                pb = pp[half]
                m = None
                for dci in range(NC_):
                    w = RW([swv, xn_slot], [pb]) if dci == 0 else ()
                    m = p.emit("tensor", "matmul", pb.ap[:, 0:512],
                               xn_slot.ap[:, dci * 512 + t * 128:dci * 512 + (t + 1) * 128],
                               swv.ap[:, dci * DM + half * 512:dci * DM + (half + 1) * 512],
                               start=(dci == 0), stop=(dci == NC_ - 1),
                               waits=w, inc="pe" if dci == NC_ - 1 else None)
                setw([pb], m)
                setr([swv, xn_slot], m)
                m2 = p.emit("scalar", "copy", vslot.ap[:, half * 512:(half + 1) * 512],
                            pb.ap[:, 0:512], waits=RW([pb], [vslot]), inc="se")
                setw([vslot], m2)
                setr([pb], m2)
                yield

    def emit_attention(qb):
        par_q = (qb // 4) % 2
        kbs = [kb for kb in (qb - 1, qb, qb + 1) if 0 <= kb < NB]
        j0 = kbs[0] - (qb - 1)
        nk = len(kbs)
        edge = None if 0 < qb < NB - 1 else ("f" if qb == 0 else "l")
        qslot = q_ring[par_q]
        kslots = [k_ring[(kb // 4) % 2] for kb in kbs]
        for pr in range(NH // 2):
            par = pr % 2
            dc = pr
            psb = ps_sc[par]
            s2 = s2m[par]
            mz = None
            if edge is not None:
                e0, e1 = (0, 128) if edge == "f" else (256, 384)
                mz = p.emit("vector", "memset", psb.ap[:, e0:e1], 0.0,
                            waits=RW((), [psb]), inc="ve")
                mz = p.emit("vector", "memset", psb.ap[:, 512 + e0:512 + e1], 0.0,
                            inc="ve")
                setw([psb], mz)
            m = None
            first = True
            for hh in range(2):
                h = 2 * pr + hh
                row = (h % 2) * 64
                for j, kb in enumerate(kbs):
                    ksl = kslots[j]
                    w = RW([qslot, ksl], [psb]) if first else RW([ksl])
                    first = False
                    m = p.emit("tensor", "matmul",
                               psb.ap[:, hh * 512 + (j0 + j) * 128:hh * 512 + (j0 + j + 1) * 128],
                               qslot.ap[row:row + 64, dc * 512 + (qb % 4) * 128:dc * 512 + (qb % 4 + 1) * 128],
                               ksl.ap[row:row + 64, dc * 512 + (kb % 4) * 128:dc * 512 + (kb % 4 + 1) * 128],
                               start=True, stop=True,
                               inc="pe" if (hh == 1 and j == nk - 1) else None,
                               waits=w)
            setw([psb], m)
            setr([qslot], m)
            for ksl in kslots:
                setr([ksl], m)
            hb3 = hbias.ap[:, 2 * pr * 384:(2 * pr + 2) * 384].rearrange(
                "p (hh c) -> p hh c", c=384)
            ps3 = psb.ap[:].rearrange("p (hh c) -> p hh c", c=512)[:, :, 0:384]
            mma = p.emit("vector", "scalar_tensor_tensor",
                         s2.ap[:].rearrange("p (hh c) -> p hh c", c=384),
                         ps3, 1.0, hb3, ALU.mult, ALU.add,
                         waits=RW([psb, hbias], [s2]), inc="ve")
            setw([s2], mma)
            setr([psb, hbias], mma)
            yield
            if edge is not None:
                bm3 = bm_[edge].ap[:].unsqueeze(1).broadcast_to([128, 2, 384])
                mma = p.emit("vector", "tensor_add",
                             s2.ap[:].rearrange("p (hh c) -> p hh c", c=384),
                             s2.ap[:].rearrange("p (hh c) -> p hh c", c=384), bm3,
                             waits=RW([bm_[edge]], [s2]), inc="ve")
                setw([s2], mma)
                setr([bm_[edge]], mma)
            mrm = p.emit("vector", "reduce_max", sm1[par].ap[:], s2.ap[:],
                         axis=AX.X, negate=True,
                         waits=RW([s2], [sm1[par]]), inc="ve")
            setw([sm1[par]], mrm)
            yield
            mex = p.emit("scalar", "activation", spA[par].ap[:], s2.ap[:], AF.Exp,
                         bias=sm1[par].ap[:],
                         waits=RW([s2, sm1[par]], [spA[par]]), inc="se")
            setw([spA[par]], mex)
            setr([s2, sm1[par]], mex)
            mrs = p.emit("vector", "reduce_sum", sm2[par].ap[:],
                         spA[par].ap[:].rearrange("p (hh c) -> p hh c", c=384),
                         axis=AX.X, waits=RW([spA[par]], [sm2[par]]), inc="ve")
            setw([sm2[par]], mrs)
            mri = p.emit("vector", "reciprocal", sd_[par].ap[:], sm2[par].ap[:],
                         waits=RW([sm2[par]], [sd_[par]]), inc="ve")
            setw([sd_[par]], mri)
            setr([sm2[par]], mri)
            yield
            rinv3 = sd_[par].ap[:].unsqueeze(2).broadcast_to([128, 2, 384])
            mA = p.emit("vector", "tensor_mul",
                        sA[par].ap[:].rearrange("p (hh c) -> p hh c", c=384),
                        spA[par].ap[:].rearrange("p (hh c) -> p hh c", c=384), rinv3,
                        waits=RW([spA[par], sd_[par]], [sA[par]]), inc="ve")
            setw([sA[par]], mA)
            setr([spA[par], sd_[par]], mA)
            yield
            m = None
            ntr = 0
            for hh in range(2):
                for j in range(nk):
                    ntr += 1
                    w = RW([sA[par], iden], [ps_tr]) if ntr == 1 else ()
                    m = p.emit("tensor", "transpose",
                               ps_tr.ap[:, hh * 384 + j * 128:hh * 384 + (j + 1) * 128],
                               sA[par].ap[:, hh * 384 + (j0 + j) * 128:hh * 384 + (j0 + j + 1) * 128],
                               iden.ap[:], waits=w,
                               inc="pe" if ntr == 2 * nk else None)
            setw([ps_tr], m)
            setr([sA[par], iden], m)
            mat = p.emit("scalar", "copy", sAT[par].ap[:, 0:768], ps_tr.ap[:, 0:768],
                         waits=RW([ps_tr], [sAT[par]]), inc="se")
            setw([sAT[par]], mat)
            setr([ps_tr], mat)
            yield
            m = None
            for hh in range(2):
                h = 2 * pr + hh
                row = (h % 2) * 64
                pav_half = pav_all[hh]
                for j, kb in enumerate(kbs):
                    vsl = v_ring[kb % 6]
                    w = RW([sAT[par], vsl], [pav_half]) if j == 0 else RW([vsl])
                    m = p.emit("tensor", "matmul", pav_t[row:row + 64, 0:128],
                               vsl.ap[:, h * 64:(h + 1) * 64],
                               sAT[par].ap[:, hh * 384 + j * 128:hh * 384 + (j + 1) * 128],
                               start=(j == 0), stop=(j == nk - 1),
                               waits=w, inc="pe" if j == nk - 1 else None)
                setw([pav_all[hh]], m)
                for kb in kbs:
                    setr([v_ring[kb % 6]], m)
            setr([sAT[par]], m)
            mev = p.emit("vector", "tensor_copy",
                         attnf[qb % 2].ap[:, dc * 128:(dc + 1) * 128],
                         pav_t[:, 0:128],
                         waits=RW([pav_lo, pav_hi], [attnf[qb % 2]]), inc="ve")
            setw([attnf[qb % 2]], mev)
            setr([pav_lo, pav_hi], mev)
            yield

    def emit_post(qb):
        af = attnf[qb % 2]
        x1 = x1b[qb % 2]
        xs = x_ring[qb % NXR]
        for half in range(2):
            pb = pp[half]
            m = None
            for dc in range(NC_):
                w = RW([af, swo], [pb]) if dc == 0 else ()
                m = p.emit("tensor", "matmul", pb.ap[:, 0:512],
                           af.ap[:, dc * 128:(dc + 1) * 128],
                           swo.ap[:, dc * DM + half * 512:dc * DM + (half + 1) * 512],
                           start=(dc == 0), stop=(dc == NC_ - 1),
                           waits=w, inc="pe" if dc == NC_ - 1 else None)
            setw([pb], m)
            setr([af, swo], m)
            m2 = p.emit("vector", "tensor_add", x1.ap[:, half * 512:(half + 1) * 512],
                        pb.ap[:, 0:512], xs.ap[:, half * 512:(half + 1) * 512],
                        waits=RW([pb, xs], [x1]), inc="ve")
            setw([x1], m2)
            setr([pb, xs], m2)
            yield
        mdx = p.emit("sync", "dma_start", out=x1_h[qb * 128:(qb + 1) * 128, :],
                     in_=x1.ap[:], waits=RW([x1]), inc=p.mksem(f"dx1_{qb % 2}"),
                     inc_n=16)
        setr([x1], mdx)
        yield
        yield from emit_ln(x1, st2[qb % 2], xn2hit[qb % 2], sqdmp[qb % 2], out_f32=xn2f32[qb % 2])
        mlo = p.emit("vector", "tensor_sub", xn2lot[qb % 2].ap[:],
                     xn2f32[qb % 2].ap[:], xn2hit[qb % 2].ap[:],
                     waits=RW([xn2f32[qb % 2], xn2hit[qb % 2]], [xn2lot[qb % 2]]),
                     inc="ve")
        setw([xn2lot[qb % 2]], mlo)
        setr([xn2f32[qb % 2], xn2hit[qb % 2]], mlo)
        yield
        hi = xn2hif[qb % 2]
        lo = xn2lof[qb % 2]
        yield from emit_transpose8(xn2hit[qb % 2], hi,
                        hi.ap[:].rearrange("p (dc c) -> p dc c", c=128), "vector")
        yield from emit_transpose8(xn2lot[qb % 2], lo,
                        lo.ap[:].rearrange("p (dc c) -> p dc c", c=128), "scalar")
        mdn = p.emit("sync", "dma_start",
                     out=xn2_h[qb].rearrange("dc p c -> p dc c"),
                     in_=hi.ap[:].rearrange("p (dc c) -> p dc c", c=128),
                     waits=RW([hi]), inc=p.mksem("dxn"), inc_n=16)
        setr([hi], mdn)
        m = None
        first = True
        trips = ([(hi, wrhi)] * NC_ + [(hi, wrlo)] * NC_ + [(lo, wrhi)] * NC_)
        for i, (xf, wr_s) in enumerate(trips):
            dc = i % NC_
            w = RW([hi, lo, wrhi, wrlo], [pav_lo, pav_hi]) if first else ()
            m = p.emit("tensor", "matmul", pav_t[:, 0:NE],
                       xf.ap[:, dc * 128:(dc + 1) * 128],
                       wr_s.ap[:, dc * NE:(dc + 1) * NE],
                       start=first, stop=(i == len(trips) - 1),
                       waits=w, inc="pe" if i == len(trips) - 1 else None)
            first = False
        setw([pav_lo, pav_hi], m)
        setr([hi, lo, wrhi, wrlo], m)
        par = qb % 2
        mlg = p.emit("vector", "tensor_copy", slog[par].ap[:], pav_t[:, 0:NE],
                     waits=RW([pav_lo, pav_hi], [slog[par]]), inc="ve")
        setw([slog[par]], mlg)
        setr([pav_lo, pav_hi], mlg)
        yield
        mm1 = p.emit("vector", "reduce_max", rm1[par].ap[:], slog[par].ap[:], axis=AX.X,
                     waits=RW([slog[par]], [rm1[par]]), inc="ve")
        setw([rm1[par]], mm1)
        yield
        mi1 = p.emit("vector", "tensor_single_scalar", sind1[par].ap[:], slog[par].ap[:],
                     rm1[par].ap[:], ALU.is_ge,
                     waits=RW([slog[par], rm1[par]], [sind1[par]]), inc="ve")
        setw([sind1[par]], mi1)
        setr([rm1[par]], mi1)
        yield
        mlk = p.emit("vector", "scalar_tensor_tensor", slk[par].ap[:], sind1[par].ap[:],
                     -1.0e30, slog[par].ap[:], ALU.mult, ALU.add,
                     waits=RW([sind1[par], slog[par]], [slk[par]]), inc="ve")
        setw([slk[par]], mlk)
        setr([sind1[par], slog[par]], mlk)
        yield
        mm2_ = p.emit("vector", "reduce_max", rm2[par].ap[:], slk[par].ap[:], axis=AX.X,
                      waits=RW([slk[par]], [rm2[par]]), inc="ve")
        setw([rm2[par]], mm2_)
        yield
        mi2 = p.emit("vector", "tensor_single_scalar", sind2[par].ap[:], slk[par].ap[:],
                     rm2[par].ap[:], ALU.is_ge,
                     waits=RW([slk[par], rm2[par]], [sind2[par]]), inc="ve")
        setw([sind2[par]], mi2)
        setr([slk[par], rm2[par]], mi2)
        yield
        md = p.emit("vector", "tensor_sub", rdd[par].ap[:], rm2[par].ap[:], rm1[par].ap[:],
                    waits=RW([rm2[par], rm1[par]], [rdd[par]]), inc="ve")
        setw([rdd[par]], md)
        setr([rm2[par], rm1[par]], md)
        yield
        me = p.emit("scalar", "activation", se_[par].ap[:], rdd[par].ap[:], AF.Exp,
                    waits=RW([rdd[par]], [se_[par]]), inc="se")
        setw([se_[par]], me)
        setr([rdd[par]], me)
        yield
        mt = p.emit("vector", "tensor_scalar_add", st_[par].ap[:], se_[par].ap[:], 1.0,
                    waits=RW([se_[par]], [st_[par]]), inc="ve")
        setw([st_[par]], mt)
        yield
        mw1 = p.emit("vector", "reciprocal", sw1[par].ap[:], st_[par].ap[:],
                     waits=RW([st_[par]], [sw1[par]]), inc="ve")
        setw([sw1[par]], mw1)
        setr([st_[par]], mw1)
        yield
        mw1s = p.emit("vector", "tensor_scalar_mul", sw1s[par].ap[:], sw1[par].ap[:],
                      DEPTH_SCALE, waits=RW([sw1[par]], [sw1s[par]]), inc="ve")
        setw([sw1s[par]], mw1s)
        setr([sw1[par]], mw1s)
        yield
        mw2s = p.emit("vector", "tensor_mul", sw2s[par].ap[:], se_[par].ap[:],
                      sw1s[par].ap[:], waits=RW([se_[par], sw1s[par]], [sw2s[par]]),
                      inc="ve")
        setw([sw2s[par]], mw2s)
        setr([se_[par]], mw2s)
        yield
        mc1 = p.emit("vector", "tensor_single_scalar", sc1_[par].ap[:], sind1[par].ap[:],
                     sw1s[par].ap[:], ALU.mult,
                     waits=RW([sind1[par], sw1s[par]], [sc1_[par]]), inc="ve")
        setw([sc1_[par]], mc1)
        setr([sind1[par], sw1s[par]], mc1)
        yield
        mcb = p.emit("vector", "scalar_tensor_tensor",
                     comb.ap[:, qb * NE:(qb + 1) * NE], sind2[par].ap[:],
                     sw2s[par].ap[:], sc1_[par].ap[:], ALU.mult, ALU.add,
                     waits=RW([sind2[par], sw2s[par], sc1_[par]], [comb]), inc="ve")
        setw([comb], mcb)
        setr([sind2[par], sw2s[par], sc1_[par]], mcb)

    def sec_gen(sbk):
        blocks = [4 * sbk + t for t in range(4)]
        xn1 = xn1r[sbk % 2]
        for t in range(4):
            b = blocks[t]
            yield from emit_ln(x_ring[b % NXR], st1[t % 2], xnst[t % 2], sqdmp[t % 2])
            dst3 = xn1.ap[:].rearrange("p (dc t c) -> p dc t c", dc=NC_, t=4, c=128)[:, :, t, :]
            yield from emit_transpose8(xnst[t % 2], xn1, dst3, "vector")
        yield from emit_proj_fm(swk, xn1, k_ring[sbk % 2])
        yield from emit_v_tm(xn1, sbk, blocks)
        yield from emit_proj_fm(swq, xn1, q_ring[sbk % 2], scale=0.125)
        if debug and sbk == 0:
            for src_b, dst in ((xn1r[0], dbg_xn1), (k_ring[0], dbg_k),
                               (q_ring[0], dbg_q)):
                mdbg = p.emit("sync", "dma_start", out=dst[:], in_=src_b.ap[:],
                              waits=RW([src_b]), inc="dx", inc_n=16)
                setr([src_b], mdbg)
            for i in range(2):
                mdbg = p.emit("sync", "dma_start", out=dbg_v[:, i*1024:(i+1)*1024],
                              in_=v_ring[i].ap[:], waits=RW([v_ring[i]]),
                              inc="dx", inc_n=16)
                setr([v_ring[i]], mdbg)

    def emit_prefetch(sbk):
        for t in range(4):
            b = 4 * (sbk + 1) + t
            xr = x_ring[b % NXR]
            m = p.emit("sync", "dma_start", out=xr.ap[:],
                       in_=x_h[b * 128:(b + 1) * 128, :],
                       waits=RW((), [xr]), inc=p.mksem(f"dxx{b % NXR}"), inc_n=16)
            setw([xr], m)

    pending_post = None
    drain(sec_gen(0))
    for sbk in range(NSB):
        qbs = [0, 1, 2] if sbk == 0 else [4 * sbk - 1, 4 * sbk, 4 * sbk + 1, 4 * sbk + 2]
        for qb in qbs:
            weave(emit_attention(qb), pending_post)
            pending_post = emit_post(qb)
        if sbk + 1 < NSB:
            for _ in range(3):
                next(pending_post, None)
            emit_prefetch(sbk)
            weave(sec_gen(sbk + 1), pending_post)
            pending_post = None
    weave(emit_attention(NB - 1), pending_post)
    drain(emit_post(NB - 1))
    if debug:
        m = p.emit("sync", "dma_start", out=comb_h[:], in_=comb.ap[:],
                   waits=RW([comb]), inc="dx", inc_n=16)
        setr([comb], m)

    sync_sems = [s for s in p.cnt if s.startswith(("dxx", "dx1")) or
                 s in ("dx", "dxn", "did", "dre", "dsl", "dwr")]
    own = {"sync": sync_sems, "gpsimd": ["dg", "dwst"], "vector": ["ve"],
           "scalar": ["se"], "tensor": ["pe"]}
    for eng in Prog.ENGS:
        p.wait_only(eng, [(s, p.cnt[s]) for s in own[eng] if s in p.cnt])
        p.emit(eng, "sem_inc", p.sems["pb"], 1)
        p.cnt["pb"] += 1
    barrier = ("pb", p.cnt["pb"])
    for eng in Prog.ENGS:
        p.wait_only(eng, [barrier])

    psA.close()
    esA.close()

    esB = ExitStack()
    sxn2 = [sb(f"sxn2_{dc}", [128, TG], BF16, esB) for dc in range(NC_)]
    oacc = [sb(f"oacc{t}", [128, DM], F32, esB) for t in range(TG // 128)]
    swgb = [sb(f"swgb{i}", [128, NC_ * 1024], BF16, esB) for i in range(2)]
    swub = [sb(f"swub{i}", [128, NC_ * 1024], BF16, esB) for i in range(2)]
    swdb = [sb(f"swdb{i}", [128, NC_ * 1024], BF16, esB) for i in range(2)]
    hbuf = [[sb(f"h{s}_{i}", [128, 512], BF16, esB) for i in range(8)] for s in range(2)]
    ssil = [sb(f"ssil{i}", [128, 512], F32, esB) for i in range(2)]

    psB = ExitStack()
    pg = [psum(f"pg{i}", [128, 512], F32, psB) for i in range(2)]
    pu = [psum(f"pu{i}", [128, 512], F32, psB) for i in range(2)]
    peo = [psum(f"peo{i}", [128, 1024], F32, psB) for i in range(2)]

    chunk_idx = 0
    for g in range(NTG):
        for dc in range(NC_):
            m = p.emit("sync", "dma_start",
                       out=sxn2[dc].ap[:].rearrange("p (t c) -> p t c", c=128),
                       in_=xn2_h[g * (TG // 128):(g + 1) * (TG // 128), dc].rearrange("t p c -> p t c"),
                       waits=RW((), [sxn2[dc]]), inc=p.mksem(f"dsx{dc}"), inc_n=16)
            setw([sxn2[dc]], m)
        for tt in range(TG // 128):
            row0 = (g * (TG // 128) + tt) * 128
            m = p.emit("sync", "dma_start", out=oacc[tt].ap[:],
                       in_=x1_h[row0:row0 + 128, :],
                       waits=RW((), [oacc[tt]]), inc=p.mksem(f"doa{tt}"), inc_n=16)
            setw([oacc[tt]], m)
        for e in range(NE):
            for q in range(NQ):
                par = chunk_idx % 2
                wgb, wub, wdb = swgb[par], swub[par], swdb[par]
                mwg = p.emit("gpsimd", "dma_start",
                             out=wgb.ap[:].rearrange("p (dc c) -> p dc c", c=1024),
                             in_=wg_h[e][:, q * 1024:(q + 1) * 1024].rearrange(
                                 "(dc p) c -> p dc c", p=128),
                             waits=RW((), [wgb]), inc=p.mksem(f"dwg{par}"), inc_n=16)
                setw([wgb], mwg)
                mwu = p.emit("gpsimd", "dma_start",
                             out=wub.ap[:].rearrange("p (dc c) -> p dc c", c=1024),
                             in_=wu_h[e][:, q * 1024:(q + 1) * 1024].rearrange(
                                 "(dc p) c -> p dc c", p=128),
                             waits=RW((), [wub]), inc=p.mksem(f"dwu{par}"), inc_n=16)
                setw([wub], mwu)
                mwd = p.emit("gpsimd", "dma_start",
                             out=wdb.ap[:].rearrange("p (hc c) -> p hc c", c=1024),
                             in_=wd_h[e][q * 1024:(q + 1) * 1024, :].rearrange(
                                 "(hc p) c -> p hc c", p=128),
                             waits=RW((), [wdb]), inc=p.mksem(f"dwd{par}"), inc_n=16)
                setw([wdb], mwd)
                for ts in range(TSB):
                    hset = hbuf[ts % 2]
                    for hc in range(8):
                        pgb, pub = pg[hc % 2], pu[hc % 2]
                        m = None
                        for dc in range(NC_):
                            w = RW([wgb, sxn2[dc]], [pgb]) if dc == 0 else RW([sxn2[dc]])
                            m = p.emit("tensor", "matmul", pgb.ap[:, 0:512],
                                       wgb.ap[:, dc * 1024 + hc * 128:dc * 1024 + (hc + 1) * 128],
                                       sxn2[dc].ap[:, ts * 512:(ts + 1) * 512],
                                       start=(dc == 0), stop=(dc == NC_ - 1),
                                       waits=w, inc="pe" if dc == NC_ - 1 else None)
                        setw([pgb], m)
                        setr([wgb], m)
                        for dc in range(NC_):
                            setr([sxn2[dc]], m)
                        m = None
                        for dc in range(NC_):
                            w = RW([wub], [pub]) if dc == 0 else ()
                            m = p.emit("tensor", "matmul", pub.ap[:, 0:512],
                                       wub.ap[:, dc * 1024 + hc * 128:dc * 1024 + (hc + 1) * 128],
                                       sxn2[dc].ap[:, ts * 512:(ts + 1) * 512],
                                       start=(dc == 0), stop=(dc == NC_ - 1),
                                       waits=w, inc="pe" if dc == NC_ - 1 else None)
                        setw([pub], m)
                        setr([wub], m)
                        msil = p.emit("scalar", "activation", ssil[hc % 2].ap[:],
                                      pgb.ap[:, 0:512], AF.Silu,
                                      waits=RW([pgb], [ssil[hc % 2]]), inc="se")
                        setw([ssil[hc % 2]], msil)
                        setr([pgb], msil)
                        mh = p.emit("vector", "tensor_mul", hset[hc].ap[:],
                                    ssil[hc % 2].ap[:], pub.ap[:, 0:512],
                                    waits=RW([ssil[hc % 2], pub], [hset[hc]]), inc="ve")
                        setw([hset[hc]], mh)
                        setr([ssil[hc % 2], pub], mh)
                    for tl in range(4):
                        tt = ts * 4 + tl
                        pe_ = peo[tt % 2]
                        m = None
                        nmm = 0
                        for hc in range(8):
                            for half in range(2):
                                nmm += 1
                                w = RW([hset[hc], wdb], [pe_]) if hc == 0 else RW([hset[hc]])
                                m = p.emit("tensor", "matmul",
                                           pe_.ap[:, half * 512:(half + 1) * 512],
                                           hset[hc].ap[:, tl * 128:(tl + 1) * 128],
                                           wdb.ap[:, hc * 1024 + half * 512:hc * 1024 + (half + 1) * 512],
                                           start=(hc == 0), stop=(hc == 7),
                                           waits=w, inc="pe" if nmm == 16 else None)
                        setw([pe_], m)
                        setr([wdb], m)
                        for hc in range(8):
                            setr([hset[hc]], m)
                        cap = comb.ap[:, (g * (TG // 128) + tt) * NE + e:
                                      (g * (TG // 128) + tt) * NE + e + 1]
                        mev = p.emit("vector", "scalar_tensor_tensor",
                                     oacc[tt].ap[:], pe_.ap[:, 0:1024], cap,
                                     oacc[tt].ap[:], ALU.mult, ALU.add,
                                     waits=RW([pe_, comb, oacc[tt]], [oacc[tt]]),
                                     inc="ve")
                        setw([oacc[tt]], mev)
                        setr([pe_, comb], mev)
                chunk_idx += 1
        for tt in range(TG // 128):
            row0 = (g * (TG // 128) + tt) * 128
            m = p.emit("sync", "dma_start", out=out_h[row0:row0 + 128, :],
                       in_=oacc[tt].ap[:], waits=RW([oacc[tt]]),
                       inc=p.mksem(f"doa{tt}"), inc_n=16)
            setr([oacc[tt]], m)

    p.wait_only("sync", [(f"doa{tt}", p.cnt[f"doa{tt}"]) for tt in range(TG // 128)])

    psB.close()
    esB.close()

    with nc.Block() as block:
        @block.sync
        def _(eng):
            p.replay("sync", eng)

        @block.gpsimd
        def _(eng):
            p.replay("gpsimd", eng)

        @block.vector
        def _(eng):
            p.replay("vector", eng)

        @block.scalar
        def _(eng):
            p.replay("scalar", eng)

        @block.tensor
        def _(eng):
            p.replay("tensor", eng)

    es.close()
    return nc


def _host_constants():
    ql = np.arange(128)[:, None]
    kl = np.arange(384)[None, :]
    band = (kl >= ql) & (kl <= ql + 256)
    mk = lambda b: np.where(b, np.float32(0.0), np.float32(NEG)).astype(np.float32)
    bm_f = mk(band & (kl >= 128))
    bm_l = mk(band & (kl < 256))
    rel = (128 + ql - kl).astype(np.float32)
    relm = np.where(band, rel, np.float32(NEG)).astype(np.float32)
    iden = np.eye(128, dtype=ml_dtypes.bfloat16)
    return relm, bm_f, bm_l, iden


def kernel(L0_attn_x, L0_attn_Wq, L0_attn_Wk, L0_attn_Wv, L0_attn_Wo,
           L0_attn_slopes, L0_ffn_Wr, L0_ffn_Wg, L0_ffn_Wu, L0_ffn_Wd):
    x = np.ascontiguousarray(np.asarray(L0_attn_x, dtype=np.float32))
    relm, bm_f, bm_l, iden = _host_constants()
    common = {
        "wq": np.ascontiguousarray(np.asarray(L0_attn_Wq, dtype=np.float32)),
        "wk": np.ascontiguousarray(np.asarray(L0_attn_Wk, dtype=np.float32)),
        "wv": np.ascontiguousarray(np.asarray(L0_attn_Wv, dtype=np.float32)),
        "wo": np.ascontiguousarray(np.asarray(L0_attn_Wo, dtype=np.float32)),
        "slopes": np.ascontiguousarray(np.asarray(L0_attn_slopes, dtype=np.float32)),
        "wr": np.ascontiguousarray(np.asarray(L0_ffn_Wr, dtype=np.float32)),
        "wg": np.ascontiguousarray(np.asarray(L0_ffn_Wg, dtype=np.float32)),
        "wu": np.ascontiguousarray(np.asarray(L0_ffn_Wu, dtype=np.float32)),
        "wd": np.ascontiguousarray(np.asarray(L0_ffn_Wd, dtype=np.float32)),
        "relm": relm,
        "bm_f": bm_f,
        "bm_l": bm_l,
        "iden": iden,
    }
    nc = build_nc()
    in_maps = [dict(common, x=np.ascontiguousarray(x[i])) for i in range(8)]
    res = run_bass_kernel_spmd(nc, in_maps, core_ids=list(range(8)))
    return np.stack([res.results[i]["out"] for i in range(8)], axis=0)


# version 18 (final)
# speedup vs baseline: 1.2089x; 1.0409x over previous
import numpy as np
import ml_dtypes
from contextlib import ExitStack

import concourse.bass as bass
import concourse.mybir as mybir
from concourse.bass_utils import run_bass_kernel_spmd

F32 = mybir.dt.float32
BF16 = mybir.dt.bfloat16
AF = mybir.ActivationFunctionType
ALU = mybir.AluOpType
AX = mybir.AxisListType

S, DM, NH, DK = 4096, 1024, 16, 64
NB = S // 128
NSB = NB // 4
NC_ = DM // 128
NE, HID = 8, 4096
DEPTH_SCALE = float(1.0 / np.sqrt(12.0))
NEG = -1.0e9
EPS = 1e-5

TG = 1024
NTG = S // TG
NQ = 4
TSB = TG // 512


class Buf:
    __slots__ = ("ap", "w", "r")

    def __init__(self, ap):
        self.ap = ap
        self.w = None
        self.r = {}


class Prog:
    ENGS = ("sync", "gpsimd", "vector", "scalar", "tensor")

    def __init__(self, nc, es):
        self.nc = nc
        self.es = es
        self.q = {e: [] for e in self.ENGS}
        self.sems = {}
        self.cnt = {}
        for name in ("pe", "ve", "se", "dx", "dg", "do", "pb"):
            self.mksem(name)

    def mksem(self, name):
        if name not in self.sems:
            self.sems[name] = self.es.enter_context(self.nc.semaphore(name))
            self.cnt[name] = 0
        return name

    def emit(self, eng, fn, *a, waits=(), inc=None, inc_n=1, **kw):
        ws = [w for w in waits if w is not None]
        incs = []
        ms = None
        if inc is not None:
            self.cnt[inc] += inc_n
            ms = (inc, self.cnt[inc])
            incs.append((inc, inc_n))
        self.q[eng].append([fn, a, kw, ws, incs])
        return ms

    def wait_only(self, eng, waits):
        ws = [w for w in waits if w is not None]
        if ws:
            self.q[eng].append([None, (), {}, ws, []])

    def add_inc_to_last(self, eng, sem, n):
        assert self.q[eng], eng
        self.q[eng][-1][4].append((sem, n))
        self.cnt[sem] += n
        return (sem, self.cnt[sem])

    def replay(self, eng_name, eng):
        seen = {}
        for fn, a, kw, ws, incs in self.q[eng_name]:
            for (s, v) in ws:
                if seen.get(s, -1) >= v:
                    continue
                seen[s] = v
                eng.wait_ge(self.sems[s], v)
            if fn is None:
                continue
            inst = getattr(eng, fn)(*a, **kw)
            for (s, n) in incs:
                inst.then_inc(self.sems[s], n)


def RW(reads=(), writes=()):
    ws = []
    for b in reads:
        if b.w:
            ws.append(b.w)
    for b in writes:
        if b.w:
            ws.append(b.w)
        ws.extend(b.r.items())
    return ws


def setw(bufs, ms):
    for b in bufs:
        b.w = ms
        b.r = {}


def setr(bufs, ms):
    if ms is None:
        return
    for b in bufs:
        old = b.r.get(ms[0], -1)
        if ms[1] > old:
            b.r[ms[0]] = ms[1]


def weave(*gens):
    alive = [g for g in gens if g is not None]
    while alive:
        for g in list(alive):
            try:
                next(g)
            except StopIteration:
                alive.remove(g)


def drain(g):
    for _ in g:
        pass


def build_nc(debug=False):
    nc = bass.Bass("TRN2", target_bir_lowering=False)

    x_h = nc.dram_tensor("x", [S, DM], F32, kind="ExternalInput").ap()
    out_h = nc.dram_tensor("out", [S, DM], F32, kind="ExternalOutput").ap()
    wq_h = nc.dram_tensor("wq", [DM, DM], F32, kind="ExternalInput").ap()
    wk_h = nc.dram_tensor("wk", [DM, DM], F32, kind="ExternalInput").ap()
    wv_h = nc.dram_tensor("wv", [DM, DM], F32, kind="ExternalInput").ap()
    wo_h = nc.dram_tensor("wo", [DM, DM], F32, kind="ExternalInput").ap()
    sl_h = nc.dram_tensor("slopes", [NH], F32, kind="ExternalInput").ap()
    wr_h = nc.dram_tensor("wr", [DM, NE], F32, kind="ExternalInput").ap()
    wg_h = nc.dram_tensor("wg", [NE, DM, HID], F32, kind="ExternalInput").ap()
    wu_h = nc.dram_tensor("wu", [NE, DM, HID], F32, kind="ExternalInput").ap()
    wd_h = nc.dram_tensor("wd", [NE, HID, DM], F32, kind="ExternalInput").ap()
    relm_h = nc.dram_tensor("relm", [128, 384], F32, kind="ExternalInput").ap()
    bmf_h = nc.dram_tensor("bm_f", [128, 384], F32, kind="ExternalInput").ap()
    bml_h = nc.dram_tensor("bm_l", [128, 384], F32, kind="ExternalInput").ap()
    iden_h = nc.dram_tensor("iden", [128, 128], BF16, kind="ExternalInput").ap()

    scr_kind = "ExternalOutput" if debug else "Internal"
    x1_h = nc.dram_tensor("x1_scr", [S, DM], F32, kind=scr_kind).ap()
    xn2_h = nc.dram_tensor("xn2_scr", [NB, NC_, 128, 128], BF16, kind=scr_kind).ap()
    comb_h = (nc.dram_tensor("comb_scr", [128, NB * NE], F32, kind="ExternalOutput").ap()
              if debug else None)
    if debug:
        dbg_xn1 = nc.dram_tensor("dbg_xn1", [128, 4096], BF16, kind="ExternalOutput").ap()
        dbg_k = nc.dram_tensor("dbg_k", [128, 4096], BF16, kind="ExternalOutput").ap()
        dbg_q = nc.dram_tensor("dbg_q", [128, 4096], BF16, kind="ExternalOutput").ap()
        dbg_v = nc.dram_tensor("dbg_v", [128, 2048], BF16, kind="ExternalOutput").ap()
        dbg_s2 = nc.dram_tensor("dbg_s2", [128, 384], F32, kind="ExternalOutput").ap()
        dbg_pA = nc.dram_tensor("dbg_pA", [128, 384], BF16, kind="ExternalOutput").ap()
        dbg_A = nc.dram_tensor("dbg_A", [128, 384], BF16, kind="ExternalOutput").ap()
        dbg_st = nc.dram_tensor("dbg_st", [128, 8], F32, kind="ExternalOutput").ap()

    es = ExitStack()
    p = Prog(nc, es)

    def sb(name, shape, dt=F32, stack=es):
        return Buf(stack.enter_context(nc.sbuf_tensor(name, shape, dt)))

    def psum(name, shape, dt=F32, stack=es):
        return Buf(stack.enter_context(nc.psum_tensor(name, shape, dt)))

    iden = sb("s_iden", [128, 128], BF16)
    bm_ = {"f": sb("s_bmf", [128, 384]), "l": sb("s_bml", [128, 384])}
    relmb = sb("s_relm", [128, 384])
    hbias = sb("s_hbias", [128, NH * 384])
    slbc = sb("slbc", [128, NH])
    comb = sb("comb", [128, NB * NE])
    wrhi = sb("wrhi", [128, NC_ * NE], BF16)
    wrlo = sb("wrlo", [128, NC_ * NE], BF16)

    esA = ExitStack()
    swq = sb("swq", [128, NC_ * DM], BF16, esA)
    swk = sb("swk", [128, NC_ * DM], BF16, esA)
    swv = sb("swv", [128, NC_ * DM], BF16, esA)
    swo = sb("swo", [128, NC_ * DM], BF16, esA)
    swrf = sb("swrf", [128, NC_ * NE], F32, esA)
    NXR = 5
    x_ring = [sb(f"xr{i}", [128, DM], F32, esA) for i in range(NXR)]
    xnst = [sb("xnst0", [128, DM], BF16, esA)] * 2
    sqdmp = xnst
    xn1r = [sb("xn1r0", [128, NC_ * 512], BF16, esA)] * 2
    k_ring = [sb(f"kr{i}", [128, NC_ * 512], BF16, esA) for i in range(2)]
    q_ring = [sb(f"qr{i}", [128, NC_ * 512], BF16, esA) for i in range(2)]
    v_ring = [sb(f"vr{i}", [128, DM], BF16, esA) for i in range(6)]
    spA = [sb(f"pA{i}", [128, 768], BF16, esA) for i in range(2)]
    s2m = [sb(f"s2m{i}", [128, 768], F32, esA) for i in range(2)]
    sA = [sb("A0", [128, 768], BF16, esA)] * 2
    sAT = [sb("AT0", [128, 768], BF16, esA)] * 2
    attnf = [sb(f"attnf{i}", [128, DM], BF16, esA) for i in range(2)]
    x1b = [sb(f"x1b{i}", [128, DM], F32, esA) for i in range(2)]
    xn2f32 = [sb("xn2f0", [128, DM], F32, esA)] * 2
    xn2hit = [sb("xn2ht0", [128, DM], BF16, esA)] * 2
    xn2lot = [sb("xn2lt0", [128, DM], BF16, esA)] * 2
    xn2hif = [sb("xn2hf0", [128, DM], BF16, esA)] * 2
    xn2lof = [sb("xn2lf0", [128, DM], BF16, esA)] * 2

    def statset(tag):
        return {k: sb(f"st_{tag}_{k}", [128, 1], F32, esA)
                for k in ("sum", "sq", "mean", "mm2", "veps", "var", "lnv", "rstd", "nmr")}

    st1 = [statset(f"a{i}") for i in range(2)]
    st2 = [statset(f"b{i}") for i in range(2)]
    slog = [sb(f"rlog{i}", [128, NE], F32, esA) for i in range(2)]
    sind1 = [sb(f"rind1{i}", [128, NE], F32, esA) for i in range(2)]
    slk = [sb(f"rlk{i}", [128, NE], F32, esA) for i in range(2)]
    sind2 = [sb(f"rind2{i}", [128, NE], F32, esA) for i in range(2)]
    sm1 = [sb(f"sxm1{i}", [128, 1], F32, esA) for i in range(2)]
    sm2 = [sb(f"sxm2{i}", [128, 2], F32, esA) for i in range(2)]
    sd_ = [sb(f"sxd{i}", [128, 2], F32, esA) for i in range(2)]
    rm1 = [sb(f"rm1{i}", [128, 1], F32, esA) for i in range(2)]
    rm2 = [sb(f"rm2{i}", [128, 1], F32, esA) for i in range(2)]
    rdd = [sb(f"rd{i}", [128, 1], F32, esA) for i in range(2)]
    se_ = [sb(f"re{i}", [128, 1], F32, esA) for i in range(2)]
    st_ = [sb(f"rt{i}", [128, 1], F32, esA) for i in range(2)]
    sw1 = [sb(f"rw1{i}", [128, 1], F32, esA) for i in range(2)]
    sw1s = [sb(f"rw1s{i}", [128, 1], F32, esA) for i in range(2)]
    sw2s = [sb(f"rw2s{i}", [128, 1], F32, esA) for i in range(2)]
    sc1_ = [sb(f"rc1{i}", [128, NE], F32, esA) for i in range(2)]

    psA = ExitStack()
    pp = [psum(f"pp{i}", [128, 512], F32, psA) for i in range(2)]
    ps_sc = [psum(f"psc{i}", [128, 1024], F32, psA) for i in range(2)]
    ps_tr = psum("ptr", [128, 1024], BF16, psA)
    pav_t = psA.enter_context(nc.psum_tensor("pav", [128, 128], F32))
    pav_lo = Buf(pav_t)
    pav_hi = Buf(pav_t)
    pav_all = (pav_lo, pav_hi)

    ms = p.emit("sync", "dma_start", out=iden.ap[:], in_=iden_h[:], inc=p.mksem("did"), inc_n=16)
    setw([iden], ms)
    for key, hbm in (("f", bmf_h), ("l", bml_h)):
        ms = p.emit("sync", "dma_start", out=bm_[key].ap[:], in_=hbm[:],
                    inc=p.mksem(f"dbm{key}"), inc_n=16)
        setw([bm_[key]], ms)
    ms = p.emit("sync", "dma_start", out=relmb.ap[:], in_=relm_h[:],
                inc=p.mksem("dre"), inc_n=16)
    setw([relmb], ms)
    ms = p.emit("sync", "dma_start", out=slbc.ap[:],
                in_=sl_h.unsqueeze(0).broadcast_to([128, NH]), inc=p.mksem("dsl"), inc_n=16)
    setw([slbc], ms)
    wstage = k_ring[0]
    for h in range(NH):
        ms = p.emit("scalar", "activation",
                    hbias.ap[:, h * 384:(h + 1) * 384], relmb.ap[:],
                    AF.Identity, scale=slbc.ap[:, h:h + 1],
                    waits=RW([relmb, slbc], [hbias]), inc="se")
        setw([hbias], ms)
    setr([relmb, slbc], ms)
    for w_hbm, w_sb in ((wq_h, swq), (wk_h, swk), (wv_h, swv), (wo_h, swo)):
        for half in range(2):
            ms = p.emit("gpsimd", "dma_start",
                        out=wstage.ap[:].rearrange("p (oc c) -> p oc c", c=DM),
                        in_=w_hbm[half * 512:(half + 1) * 512, :].rearrange(
                            "(oc p) c -> p oc c", p=128),
                        waits=RW((), [wstage]), inc=p.mksem("dwst"), inc_n=16)
            setw([wstage], ms)
            for dc in range(NC_):
                m = None
                for ocl in range(4):
                    w = RW([wstage, iden], [ps_tr]) if ocl == 0 else ()
                    m = p.emit("tensor", "transpose",
                               ps_tr.ap[:, ocl * 128:(ocl + 1) * 128],
                               wstage.ap[:, ocl * DM + dc * 128:ocl * DM + (dc + 1) * 128],
                               iden.ap[:], waits=w, inc="pe" if ocl == 3 else None)
                setw([ps_tr], m)
                setr([wstage, iden], m)
                m2 = p.emit("vector", "tensor_copy",
                            w_sb.ap[:, dc * DM + half * 512:dc * DM + (half + 1) * 512],
                            ps_tr.ap[:, 0:512],
                            waits=RW([ps_tr], [w_sb]), inc="ve")
                setw([w_sb], m2)
                setr([ps_tr], m2)
    ms = p.emit("sync", "dma_start",
                out=swrf.ap[:].rearrange("p (dc c) -> p dc c", c=NE),
                in_=wr_h.rearrange("(dc p) c -> p dc c", p=128),
                inc=p.mksem("dwr"), inc_n=16)
    setw([swrf], ms)
    ms = p.emit("vector", "tensor_copy", wrhi.ap[:], swrf.ap[:],
                waits=RW([swrf], [wrhi]), inc="ve")
    setw([wrhi], ms)
    setr([swrf], ms)
    ms = p.emit("vector", "tensor_sub", wrlo.ap[:], swrf.ap[:], wrhi.ap[:],
                waits=RW([swrf, wrhi], [wrlo]), inc="ve")
    setw([wrlo], ms)
    setr([swrf, wrhi], ms)
    for t in range(4):
        ms = p.emit("sync", "dma_start", out=x_ring[t].ap[:],
                    in_=x_h[t * 128:(t + 1) * 128, :],
                    inc=p.mksem(f"dxx{t % NXR}"), inc_n=16)
        setw([x_ring[t]], ms)

    def emit_ln(xbuf, stats, out_bf, sqd, out_f32=None):
        m = p.emit("vector", "reduce_sum", stats["sum"].ap[:], xbuf.ap[:], axis=AX.X,
                   waits=RW([xbuf], [stats["sum"]]), inc="ve")
        setw([stats["sum"]], m); setr([xbuf], m)
        yield
        m = p.emit("scalar", "activation", sqd.ap[:], xbuf.ap[:], AF.Square,
                   accum_out=stats["sq"].ap[:],
                   waits=RW([xbuf], [sqd, stats["sq"]]), inc="se")
        setw([sqd, stats["sq"]], m); setr([xbuf], m)
        yield
        m = p.emit("vector", "tensor_scalar_mul", stats["mean"].ap[:],
                   stats["sum"].ap[:], 1.0 / DM,
                   waits=RW([stats["sum"]], [stats["mean"]]), inc="ve")
        setw([stats["mean"]], m); setr([stats["sum"]], m)
        yield
        m = p.emit("vector", "tensor_mul", stats["mm2"].ap[:], stats["mean"].ap[:],
                   stats["mean"].ap[:], waits=RW([stats["mean"]], [stats["mm2"]]), inc="ve")
        setw([stats["mm2"]], m); setr([stats["mean"]], m)
        m = p.emit("vector", "tensor_scalar", stats["veps"].ap[:],
                   stats["sq"].ap[:], 1.0 / DM, EPS, ALU.mult, ALU.add,
                   waits=RW([stats["sq"]], [stats["veps"]]), inc="ve")
        setw([stats["veps"]], m); setr([stats["sq"]], m)
        yield
        m = p.emit("vector", "tensor_sub", stats["var"].ap[:],
                   stats["veps"].ap[:], stats["mm2"].ap[:],
                   waits=RW([stats["veps"], stats["mm2"]], [stats["var"]]), inc="ve")
        setw([stats["var"]], m); setr([stats["veps"], stats["mm2"]], m)
        yield
        m = p.emit("scalar", "activation", stats["lnv"].ap[:], stats["var"].ap[:],
                   AF.Ln, waits=RW([stats["var"]], [stats["lnv"]]), inc="se")
        setw([stats["lnv"]], m); setr([stats["var"]], m)
        yield
        m = p.emit("scalar", "activation", stats["rstd"].ap[:], stats["lnv"].ap[:],
                   AF.Exp, scale=-0.5, waits=RW([stats["lnv"]], [stats["rstd"]]), inc="se")
        setw([stats["rstd"]], m); setr([stats["lnv"]], m)
        yield
        m = p.emit("vector", "scalar_tensor_tensor", stats["nmr"].ap[:],
                   stats["mean"].ap[:], -1.0, stats["rstd"].ap[:], ALU.mult, ALU.mult,
                   waits=RW([stats["mean"], stats["rstd"]], [stats["nmr"]]), inc="ve")
        setw([stats["nmr"]], m); setr([stats["mean"], stats["rstd"]], m)
        yield
        m = p.emit("scalar", "activation", out_bf.ap[:], xbuf.ap[:], AF.Identity,
                   bias=stats["nmr"].ap[:], scale=stats["rstd"].ap[:],
                   waits=RW([xbuf, stats["nmr"], stats["rstd"]], [out_bf]), inc="se")
        setw([out_bf], m); setr([xbuf, stats["nmr"], stats["rstd"]], m)
        yield
        if out_f32 is not None:
            m = p.emit("scalar", "activation", out_f32.ap[:], xbuf.ap[:], AF.Identity,
                       bias=stats["nmr"].ap[:], scale=stats["rstd"].ap[:],
                       waits=RW([xbuf, stats["nmr"], stats["rstd"]], [out_f32]), inc="se")
            setw([out_f32], m); setr([xbuf, stats["nmr"], stats["rstd"]], m)

    def emit_transpose8(src_tm, dst_fm, dst_ap3, evac_eng):
        m = None
        for dc in range(NC_):
            w = RW([src_tm, iden], [ps_tr]) if dc == 0 else ()
            m = p.emit("tensor", "transpose", ps_tr.ap[:, dc * 128:(dc + 1) * 128],
                       src_tm.ap[:, dc * 128:(dc + 1) * 128], iden.ap[:],
                       waits=w, inc="pe" if dc == NC_ - 1 else None)
        setw([ps_tr], m)
        setr([src_tm, iden], m)
        sem = "ve" if evac_eng == "vector" else "se"
        fn = "tensor_copy" if evac_eng == "vector" else "copy"
        m2 = p.emit(evac_eng, fn, dst_ap3,
                    ps_tr.ap[:].rearrange("p (dc c) -> p dc c", c=128),
                    waits=RW([ps_tr], [dst_fm]), inc=sem)
        setw([dst_fm], m2)
        setr([ps_tr], m2)
        yield

    def emit_proj_fm(w_sb, xn_slot, dst_slot, scale=None):
        for dout_c in range(NC_):
            pb = pp[dout_c % 2]
            m = None
            for dci in range(NC_):
                w = RW([w_sb, xn_slot], [pb]) if dci == 0 else ()
                m = p.emit("tensor", "matmul", pb.ap[:, 0:512],
                           w_sb.ap[:, dci * DM + dout_c * 128:dci * DM + (dout_c + 1) * 128],
                           xn_slot.ap[:, dci * 512:(dci + 1) * 512],
                           start=(dci == 0), stop=(dci == NC_ - 1),
                           waits=w, inc="pe" if dci == NC_ - 1 else None)
            setw([pb], m)
            setr([w_sb, xn_slot], m)
            dst = dst_slot.ap[:, dout_c * 512:(dout_c + 1) * 512]
            if scale is None:
                m2 = p.emit("vector", "tensor_copy", dst, pb.ap[:, 0:512],
                            waits=RW([pb], [dst_slot]), inc="ve")
            else:
                m2 = p.emit("scalar", "activation", dst, pb.ap[:, 0:512], AF.Copy,
                            scale=scale, waits=RW([pb], [dst_slot]), inc="se")
            setw([dst_slot], m2)
            setr([pb], m2)
            yield

    def emit_v_tm(xn_slot, sbk, blocks):
        for t in range(4):
            b = blocks[t]
            vslot = v_ring[b % 6]
            for half in range(2):
                pb = pp[half]
                m = None
                for dci in range(NC_):
                    w = RW([swv, xn_slot], [pb]) if dci == 0 else ()
                    m = p.emit("tensor", "matmul", pb.ap[:, 0:512],
                               xn_slot.ap[:, dci * 512 + t * 128:dci * 512 + (t + 1) * 128],
                               swv.ap[:, dci * DM + half * 512:dci * DM + (half + 1) * 512],
                               start=(dci == 0), stop=(dci == NC_ - 1),
                               waits=w, inc="pe" if dci == NC_ - 1 else None)
                setw([pb], m)
                setr([swv, xn_slot], m)
                m2 = p.emit("scalar", "copy", vslot.ap[:, half * 512:(half + 1) * 512],
                            pb.ap[:, 0:512], waits=RW([pb], [vslot]), inc="se")
                setw([vslot], m2)
                setr([pb], m2)
                yield

    def emit_attention(qb):
        par_q = (qb // 4) % 2
        kbs = [kb for kb in (qb - 1, qb, qb + 1) if 0 <= kb < NB]
        j0 = kbs[0] - (qb - 1)
        nk = len(kbs)
        edge = None if 0 < qb < NB - 1 else ("f" if qb == 0 else "l")
        qslot = q_ring[par_q]
        kslots = [k_ring[(kb // 4) % 2] for kb in kbs]

        def scores_part(pr):
            par = pr % 2
            dc = pr
            psb = ps_sc[par]
            s2 = s2m[par]
            mz = None
            if edge is not None:
                e0, e1 = (0, 128) if edge == "f" else (256, 384)
                mz = p.emit("vector", "memset", psb.ap[:, e0:e1], 0.0,
                            waits=RW((), [psb]), inc="ve")
                mz = p.emit("vector", "memset", psb.ap[:, 512 + e0:512 + e1], 0.0,
                            inc="ve")
                setw([psb], mz)
            m = None
            first = True
            for hh in range(2):
                h = 2 * pr + hh
                row = (h % 2) * 64
                for j, kb in enumerate(kbs):
                    ksl = kslots[j]
                    w = RW([qslot, ksl], [psb]) if first else RW([ksl])
                    first = False
                    m = p.emit("tensor", "matmul",
                               psb.ap[:, hh * 512 + (j0 + j) * 128:hh * 512 + (j0 + j + 1) * 128],
                               qslot.ap[row:row + 64, dc * 512 + (qb % 4) * 128:dc * 512 + (qb % 4 + 1) * 128],
                               ksl.ap[row:row + 64, dc * 512 + (kb % 4) * 128:dc * 512 + (kb % 4 + 1) * 128],
                               start=True, stop=True,
                               inc="pe" if (hh == 1 and j == nk - 1) else None,
                               waits=w)
            setw([psb], m)
            setr([qslot], m)
            for ksl in kslots:
                setr([ksl], m)
            hb3 = hbias.ap[:, 2 * pr * 384:(2 * pr + 2) * 384].rearrange(
                "p (hh c) -> p hh c", c=384)
            ps3 = psb.ap[:].rearrange("p (hh c) -> p hh c", c=512)[:, :, 0:384]
            mma = p.emit("vector", "scalar_tensor_tensor",
                         s2.ap[:].rearrange("p (hh c) -> p hh c", c=384),
                         ps3, 1.0, hb3, ALU.mult, ALU.add,
                         waits=RW([psb, hbias], [s2]), inc="ve")
            setw([s2], mma)
            setr([psb, hbias], mma)
            yield
            if edge is not None:
                bm3 = bm_[edge].ap[:].unsqueeze(1).broadcast_to([128, 2, 384])
                mma = p.emit("vector", "tensor_add",
                             s2.ap[:].rearrange("p (hh c) -> p hh c", c=384),
                             s2.ap[:].rearrange("p (hh c) -> p hh c", c=384), bm3,
                             waits=RW([bm_[edge]], [s2]), inc="ve")
                setw([s2], mma)
                setr([bm_[edge]], mma)
            mrm = p.emit("vector", "reduce_max", sm1[par].ap[:], s2.ap[:],
                         axis=AX.X, negate=True,
                         waits=RW([s2], [sm1[par]]), inc="ve")
            setw([sm1[par]], mrm)
            yield
            mex = p.emit("scalar", "activation", spA[par].ap[:], s2.ap[:], AF.Exp,
                         bias=sm1[par].ap[:],
                         waits=RW([s2, sm1[par]], [spA[par]]), inc="se")
            setw([spA[par]], mex)
            setr([s2, sm1[par]], mex)
            mrs = p.emit("vector", "reduce_sum", sm2[par].ap[:],
                         spA[par].ap[:].rearrange("p (hh c) -> p hh c", c=384),
                         axis=AX.X, waits=RW([spA[par]], [sm2[par]]), inc="ve")
            setw([sm2[par]], mrs)
            mri = p.emit("vector", "reciprocal", sd_[par].ap[:], sm2[par].ap[:],
                         waits=RW([sm2[par]], [sd_[par]]), inc="ve")
            setw([sd_[par]], mri)
            setr([sm2[par]], mri)
            yield
            rinv3 = sd_[par].ap[:].unsqueeze(2).broadcast_to([128, 2, 384])
            mA = p.emit("vector", "tensor_mul",
                        sA[par].ap[:].rearrange("p (hh c) -> p hh c", c=384),
                        spA[par].ap[:].rearrange("p (hh c) -> p hh c", c=384), rinv3,
                        waits=RW([spA[par], sd_[par]], [sA[par]]), inc="ve")
            setw([sA[par]], mA)
            setr([spA[par], sd_[par]], mA)

        def av_part(pr):
            par = pr % 2
            dc = pr
            m = None
            ntr = 0
            for hh in range(2):
                for j in range(nk):
                    ntr += 1
                    w = RW([sA[par], iden], [ps_tr]) if ntr == 1 else ()
                    m = p.emit("tensor", "transpose",
                               ps_tr.ap[:, hh * 384 + j * 128:hh * 384 + (j + 1) * 128],
                               sA[par].ap[:, hh * 384 + (j0 + j) * 128:hh * 384 + (j0 + j + 1) * 128],
                               iden.ap[:], waits=w,
                               inc="pe" if ntr == 2 * nk else None)
            setw([ps_tr], m)
            setr([sA[par], iden], m)
            mat = p.emit("scalar", "copy", sAT[par].ap[:, 0:768], ps_tr.ap[:, 0:768],
                         waits=RW([ps_tr], [sAT[par]]), inc="se")
            setw([sAT[par]], mat)
            setr([ps_tr], mat)
            yield
            m = None
            for hh in range(2):
                h = 2 * pr + hh
                row = (h % 2) * 64
                pav_half = pav_all[hh]
                for j, kb in enumerate(kbs):
                    vsl = v_ring[kb % 6]
                    w = RW([sAT[par], vsl], [pav_half]) if j == 0 else RW([vsl])
                    m = p.emit("tensor", "matmul", pav_t[row:row + 64, 0:128],
                               vsl.ap[:, h * 64:(h + 1) * 64],
                               sAT[par].ap[:, hh * 384 + j * 128:hh * 384 + (j + 1) * 128],
                               start=(j == 0), stop=(j == nk - 1),
                               waits=w, inc="pe" if j == nk - 1 else None)
                setw([pav_all[hh]], m)
                for kb in kbs:
                    setr([v_ring[kb % 6]], m)
            setr([sAT[par]], m)
            mev = p.emit("vector", "tensor_copy",
                         attnf[qb % 2].ap[:, dc * 128:(dc + 1) * 128],
                         pav_t[:, 0:128],
                         waits=RW([pav_lo, pav_hi], [attnf[qb % 2]]), inc="ve")
            setw([attnf[qb % 2]], mev)
            setr([pav_lo, pav_hi], mev)
            yield

        for _ in scores_part(0):
            yield
        for pr in range(NH // 2):
            nxt = scores_part(pr + 1) if pr + 1 < NH // 2 else None
            cur = av_part(pr)
            alive = [g for g in (nxt, cur) if g is not None]
            while alive:
                for g in list(alive):
                    try:
                        next(g)
                    except StopIteration:
                        alive.remove(g)
                yield

    def emit_post(qb):
        af = attnf[qb % 2]
        x1 = x1b[qb % 2]
        xs = x_ring[qb % NXR]
        for half in range(2):
            pb = pp[half]
            m = None
            for dc in range(NC_):
                w = RW([af, swo], [pb]) if dc == 0 else ()
                m = p.emit("tensor", "matmul", pb.ap[:, 0:512],
                           af.ap[:, dc * 128:(dc + 1) * 128],
                           swo.ap[:, dc * DM + half * 512:dc * DM + (half + 1) * 512],
                           start=(dc == 0), stop=(dc == NC_ - 1),
                           waits=w, inc="pe" if dc == NC_ - 1 else None)
            setw([pb], m)
            setr([af, swo], m)
            m2 = p.emit("vector", "tensor_add", x1.ap[:, half * 512:(half + 1) * 512],
                        pb.ap[:, 0:512], xs.ap[:, half * 512:(half + 1) * 512],
                        waits=RW([pb, xs], [x1]), inc="ve")
            setw([x1], m2)
            setr([pb, xs], m2)
            yield
        mdx = p.emit("sync", "dma_start", out=x1_h[qb * 128:(qb + 1) * 128, :],
                     in_=x1.ap[:], waits=RW([x1]), inc=p.mksem(f"dx1_{qb % 2}"),
                     inc_n=16)
        setr([x1], mdx)
        yield
        yield from emit_ln(x1, st2[qb % 2], xn2hit[qb % 2], sqdmp[qb % 2], out_f32=xn2f32[qb % 2])
        mlo = p.emit("vector", "tensor_sub", xn2lot[qb % 2].ap[:],
                     xn2f32[qb % 2].ap[:], xn2hit[qb % 2].ap[:],
                     waits=RW([xn2f32[qb % 2], xn2hit[qb % 2]], [xn2lot[qb % 2]]),
                     inc="ve")
        setw([xn2lot[qb % 2]], mlo)
        setr([xn2f32[qb % 2], xn2hit[qb % 2]], mlo)
        yield
        hi = xn2hif[qb % 2]
        lo = xn2lof[qb % 2]
        yield from emit_transpose8(xn2hit[qb % 2], hi,
                        hi.ap[:].rearrange("p (dc c) -> p dc c", c=128), "vector")
        yield from emit_transpose8(xn2lot[qb % 2], lo,
                        lo.ap[:].rearrange("p (dc c) -> p dc c", c=128), "scalar")
        mdn = p.emit("sync", "dma_start",
                     out=xn2_h[qb].rearrange("dc p c -> p dc c"),
                     in_=hi.ap[:].rearrange("p (dc c) -> p dc c", c=128),
                     waits=RW([hi]), inc=p.mksem("dxn"), inc_n=16)
        setr([hi], mdn)
        m = None
        first = True
        trips = ([(hi, wrhi)] * NC_ + [(hi, wrlo)] * NC_ + [(lo, wrhi)] * NC_)
        for i, (xf, wr_s) in enumerate(trips):
            dc = i % NC_
            w = RW([hi, lo, wrhi, wrlo], [pav_lo, pav_hi]) if first else ()
            m = p.emit("tensor", "matmul", pav_t[:, 0:NE],
                       xf.ap[:, dc * 128:(dc + 1) * 128],
                       wr_s.ap[:, dc * NE:(dc + 1) * NE],
                       start=first, stop=(i == len(trips) - 1),
                       waits=w, inc="pe" if i == len(trips) - 1 else None)
            first = False
        setw([pav_lo, pav_hi], m)
        setr([hi, lo, wrhi, wrlo], m)
        par = qb % 2
        mlg = p.emit("vector", "tensor_copy", slog[par].ap[:], pav_t[:, 0:NE],
                     waits=RW([pav_lo, pav_hi], [slog[par]]), inc="ve")
        setw([slog[par]], mlg)
        setr([pav_lo, pav_hi], mlg)
        yield
        mm1 = p.emit("vector", "reduce_max", rm1[par].ap[:], slog[par].ap[:], axis=AX.X,
                     waits=RW([slog[par]], [rm1[par]]), inc="ve")
        setw([rm1[par]], mm1)
        yield
        mi1 = p.emit("vector", "tensor_single_scalar", sind1[par].ap[:], slog[par].ap[:],
                     rm1[par].ap[:], ALU.is_ge,
                     waits=RW([slog[par], rm1[par]], [sind1[par]]), inc="ve")
        setw([sind1[par]], mi1)
        setr([rm1[par]], mi1)
        yield
        mlk = p.emit("vector", "scalar_tensor_tensor", slk[par].ap[:], sind1[par].ap[:],
                     -1.0e30, slog[par].ap[:], ALU.mult, ALU.add,
                     waits=RW([sind1[par], slog[par]], [slk[par]]), inc="ve")
        setw([slk[par]], mlk)
        setr([sind1[par], slog[par]], mlk)
        yield
        mm2_ = p.emit("vector", "reduce_max", rm2[par].ap[:], slk[par].ap[:], axis=AX.X,
                      waits=RW([slk[par]], [rm2[par]]), inc="ve")
        setw([rm2[par]], mm2_)
        yield
        mi2 = p.emit("vector", "tensor_single_scalar", sind2[par].ap[:], slk[par].ap[:],
                     rm2[par].ap[:], ALU.is_ge,
                     waits=RW([slk[par], rm2[par]], [sind2[par]]), inc="ve")
        setw([sind2[par]], mi2)
        setr([slk[par], rm2[par]], mi2)
        yield
        md = p.emit("vector", "tensor_sub", rdd[par].ap[:], rm2[par].ap[:], rm1[par].ap[:],
                    waits=RW([rm2[par], rm1[par]], [rdd[par]]), inc="ve")
        setw([rdd[par]], md)
        setr([rm2[par], rm1[par]], md)
        yield
        me = p.emit("scalar", "activation", se_[par].ap[:], rdd[par].ap[:], AF.Exp,
                    waits=RW([rdd[par]], [se_[par]]), inc="se")
        setw([se_[par]], me)
        setr([rdd[par]], me)
        yield
        mt = p.emit("vector", "tensor_scalar_add", st_[par].ap[:], se_[par].ap[:], 1.0,
                    waits=RW([se_[par]], [st_[par]]), inc="ve")
        setw([st_[par]], mt)
        yield
        mw1 = p.emit("vector", "reciprocal", sw1[par].ap[:], st_[par].ap[:],
                     waits=RW([st_[par]], [sw1[par]]), inc="ve")
        setw([sw1[par]], mw1)
        setr([st_[par]], mw1)
        yield
        mw1s = p.emit("vector", "tensor_scalar_mul", sw1s[par].ap[:], sw1[par].ap[:],
                      DEPTH_SCALE, waits=RW([sw1[par]], [sw1s[par]]), inc="ve")
        setw([sw1s[par]], mw1s)
        setr([sw1[par]], mw1s)
        yield
        mw2s = p.emit("vector", "tensor_mul", sw2s[par].ap[:], se_[par].ap[:],
                      sw1s[par].ap[:], waits=RW([se_[par], sw1s[par]], [sw2s[par]]),
                      inc="ve")
        setw([sw2s[par]], mw2s)
        setr([se_[par]], mw2s)
        yield
        mc1 = p.emit("vector", "tensor_single_scalar", sc1_[par].ap[:], sind1[par].ap[:],
                     sw1s[par].ap[:], ALU.mult,
                     waits=RW([sind1[par], sw1s[par]], [sc1_[par]]), inc="ve")
        setw([sc1_[par]], mc1)
        setr([sind1[par], sw1s[par]], mc1)
        yield
        mcb = p.emit("vector", "scalar_tensor_tensor",
                     comb.ap[:, qb * NE:(qb + 1) * NE], sind2[par].ap[:],
                     sw2s[par].ap[:], sc1_[par].ap[:], ALU.mult, ALU.add,
                     waits=RW([sind2[par], sw2s[par], sc1_[par]], [comb]), inc="ve")
        setw([comb], mcb)
        setr([sind2[par], sw2s[par], sc1_[par]], mcb)

    def sec_gen(sbk):
        blocks = [4 * sbk + t for t in range(4)]
        xn1 = xn1r[sbk % 2]
        for t in range(4):
            b = blocks[t]
            yield from emit_ln(x_ring[b % NXR], st1[t % 2], xnst[t % 2], sqdmp[t % 2])
            dst3 = xn1.ap[:].rearrange("p (dc t c) -> p dc t c", dc=NC_, t=4, c=128)[:, :, t, :]
            yield from emit_transpose8(xnst[t % 2], xn1, dst3, "vector")
        yield from emit_proj_fm(swk, xn1, k_ring[sbk % 2])
        yield from emit_v_tm(xn1, sbk, blocks)
        yield from emit_proj_fm(swq, xn1, q_ring[sbk % 2], scale=0.125)
        if debug and sbk == 0:
            for src_b, dst in ((xn1r[0], dbg_xn1), (k_ring[0], dbg_k),
                               (q_ring[0], dbg_q)):
                mdbg = p.emit("sync", "dma_start", out=dst[:], in_=src_b.ap[:],
                              waits=RW([src_b]), inc="dx", inc_n=16)
                setr([src_b], mdbg)
            for i in range(2):
                mdbg = p.emit("sync", "dma_start", out=dbg_v[:, i*1024:(i+1)*1024],
                              in_=v_ring[i].ap[:], waits=RW([v_ring[i]]),
                              inc="dx", inc_n=16)
                setr([v_ring[i]], mdbg)

    def emit_prefetch(sbk):
        for t in range(4):
            b = 4 * (sbk + 1) + t
            xr = x_ring[b % NXR]
            m = p.emit("sync", "dma_start", out=xr.ap[:],
                       in_=x_h[b * 128:(b + 1) * 128, :],
                       waits=RW((), [xr]), inc=p.mksem(f"dxx{b % NXR}"), inc_n=16)
            setw([xr], m)

    pending_post = None
    drain(sec_gen(0))
    for sbk in range(NSB):
        qbs = [0, 1, 2] if sbk == 0 else [4 * sbk - 1, 4 * sbk, 4 * sbk + 1, 4 * sbk + 2]
        for qb in qbs:
            weave(emit_attention(qb), pending_post)
            pending_post = emit_post(qb)
        if sbk + 1 < NSB:
            for _ in range(3):
                next(pending_post, None)
            emit_prefetch(sbk)
            weave(sec_gen(sbk + 1), pending_post)
            pending_post = None
    weave(emit_attention(NB - 1), pending_post)
    drain(emit_post(NB - 1))
    if debug:
        m = p.emit("sync", "dma_start", out=comb_h[:], in_=comb.ap[:],
                   waits=RW([comb]), inc="dx", inc_n=16)
        setr([comb], m)

    sync_sems = [s for s in p.cnt if s.startswith(("dxx", "dx1")) or
                 s in ("dx", "dxn", "did", "dre", "dsl", "dwr")]
    own = {"sync": sync_sems, "gpsimd": ["dg", "dwst"], "vector": ["ve"],
           "scalar": ["se"], "tensor": ["pe"]}
    for eng in Prog.ENGS:
        p.wait_only(eng, [(s, p.cnt[s]) for s in own[eng] if s in p.cnt])
        p.emit(eng, "sem_inc", p.sems["pb"], 1)
        p.cnt["pb"] += 1
    barrier = ("pb", p.cnt["pb"])
    for eng in Prog.ENGS:
        p.wait_only(eng, [barrier])

    psA.close()
    esA.close()

    esB = ExitStack()
    sxn2 = [sb(f"sxn2_{dc}", [128, TG], BF16, esB) for dc in range(NC_)]
    oacc = [sb(f"oacc{t}", [128, DM], F32, esB) for t in range(TG // 128)]
    swgb = [sb(f"swgb{i}", [128, NC_ * 1024], BF16, esB) for i in range(2)]
    swub = [sb(f"swub{i}", [128, NC_ * 1024], BF16, esB) for i in range(2)]
    swdb = [sb(f"swdb{i}", [128, NC_ * 1024], BF16, esB) for i in range(2)]
    hbuf = [[sb(f"h{s}_{i}", [128, 512], BF16, esB) for i in range(8)] for s in range(2)]
    ssil = [sb(f"ssil{i}", [128, 512], F32, esB) for i in range(2)]

    psB = ExitStack()
    pg = [psum(f"pg{i}", [128, 512], F32, psB) for i in range(2)]
    pu = [psum(f"pu{i}", [128, 512], F32, psB) for i in range(2)]
    peo = [psum(f"peo{i}", [128, 1024], F32, psB) for i in range(2)]

    chunk_idx = 0
    for g in range(NTG):
        for dc in range(NC_):
            m = p.emit("sync", "dma_start",
                       out=sxn2[dc].ap[:].rearrange("p (t c) -> p t c", c=128),
                       in_=xn2_h[g * (TG // 128):(g + 1) * (TG // 128), dc].rearrange("t p c -> p t c"),
                       waits=RW((), [sxn2[dc]]), inc=p.mksem(f"dsx{dc}"), inc_n=16)
            setw([sxn2[dc]], m)
        for tt in range(TG // 128):
            row0 = (g * (TG // 128) + tt) * 128
            m = p.emit("sync", "dma_start", out=oacc[tt].ap[:],
                       in_=x1_h[row0:row0 + 128, :],
                       waits=RW((), [oacc[tt]]), inc=p.mksem(f"doa{tt}"), inc_n=16)
            setw([oacc[tt]], m)
        for e in range(NE):
            for q in range(NQ):
                par = chunk_idx % 2
                wgb, wub, wdb = swgb[par], swub[par], swdb[par]
                mwg = p.emit("gpsimd", "dma_start",
                             out=wgb.ap[:].rearrange("p (dc c) -> p dc c", c=1024),
                             in_=wg_h[e][:, q * 1024:(q + 1) * 1024].rearrange(
                                 "(dc p) c -> p dc c", p=128),
                             waits=RW((), [wgb]), inc=p.mksem(f"dwg{par}"), inc_n=16)
                setw([wgb], mwg)
                mwu = p.emit("gpsimd", "dma_start",
                             out=wub.ap[:].rearrange("p (dc c) -> p dc c", c=1024),
                             in_=wu_h[e][:, q * 1024:(q + 1) * 1024].rearrange(
                                 "(dc p) c -> p dc c", p=128),
                             waits=RW((), [wub]), inc=p.mksem(f"dwu{par}"), inc_n=16)
                setw([wub], mwu)
                mwd = p.emit("gpsimd", "dma_start",
                             out=wdb.ap[:].rearrange("p (hc c) -> p hc c", c=1024),
                             in_=wd_h[e][q * 1024:(q + 1) * 1024, :].rearrange(
                                 "(hc p) c -> p hc c", p=128),
                             waits=RW((), [wdb]), inc=p.mksem(f"dwd{par}"), inc_n=16)
                setw([wdb], mwd)
                for ts in range(TSB):
                    hset = hbuf[ts % 2]
                    for hc in range(8):
                        pgb, pub = pg[hc % 2], pu[hc % 2]
                        m = None
                        for dc in range(NC_):
                            w = RW([wgb, sxn2[dc]], [pgb]) if dc == 0 else RW([sxn2[dc]])
                            m = p.emit("tensor", "matmul", pgb.ap[:, 0:512],
                                       wgb.ap[:, dc * 1024 + hc * 128:dc * 1024 + (hc + 1) * 128],
                                       sxn2[dc].ap[:, ts * 512:(ts + 1) * 512],
                                       start=(dc == 0), stop=(dc == NC_ - 1),
                                       waits=w, inc="pe" if dc == NC_ - 1 else None)
                        setw([pgb], m)
                        setr([wgb], m)
                        for dc in range(NC_):
                            setr([sxn2[dc]], m)
                        m = None
                        for dc in range(NC_):
                            w = RW([wub], [pub]) if dc == 0 else ()
                            m = p.emit("tensor", "matmul", pub.ap[:, 0:512],
                                       wub.ap[:, dc * 1024 + hc * 128:dc * 1024 + (hc + 1) * 128],
                                       sxn2[dc].ap[:, ts * 512:(ts + 1) * 512],
                                       start=(dc == 0), stop=(dc == NC_ - 1),
                                       waits=w, inc="pe" if dc == NC_ - 1 else None)
                        setw([pub], m)
                        setr([wub], m)
                        msil = p.emit("scalar", "activation", ssil[hc % 2].ap[:],
                                      pgb.ap[:, 0:512], AF.Silu,
                                      waits=RW([pgb], [ssil[hc % 2]]), inc="se")
                        setw([ssil[hc % 2]], msil)
                        setr([pgb], msil)
                        mh = p.emit("vector", "tensor_mul", hset[hc].ap[:],
                                    ssil[hc % 2].ap[:], pub.ap[:, 0:512],
                                    waits=RW([ssil[hc % 2], pub], [hset[hc]]), inc="ve")
                        setw([hset[hc]], mh)
                        setr([ssil[hc % 2], pub], mh)
                    for tl in range(4):
                        tt = ts * 4 + tl
                        pe_ = peo[tt % 2]
                        m = None
                        nmm = 0
                        for hc in range(8):
                            for half in range(2):
                                nmm += 1
                                w = RW([hset[hc], wdb], [pe_]) if hc == 0 else RW([hset[hc]])
                                m = p.emit("tensor", "matmul",
                                           pe_.ap[:, half * 512:(half + 1) * 512],
                                           hset[hc].ap[:, tl * 128:(tl + 1) * 128],
                                           wdb.ap[:, hc * 1024 + half * 512:hc * 1024 + (half + 1) * 512],
                                           start=(hc == 0), stop=(hc == 7),
                                           waits=w, inc="pe" if nmm == 16 else None)
                        setw([pe_], m)
                        setr([wdb], m)
                        for hc in range(8):
                            setr([hset[hc]], m)
                        cap = comb.ap[:, (g * (TG // 128) + tt) * NE + e:
                                      (g * (TG // 128) + tt) * NE + e + 1]
                        mev = p.emit("vector", "scalar_tensor_tensor",
                                     oacc[tt].ap[:], pe_.ap[:, 0:1024], cap,
                                     oacc[tt].ap[:], ALU.mult, ALU.add,
                                     waits=RW([pe_, comb, oacc[tt]], [oacc[tt]]),
                                     inc="ve")
                        setw([oacc[tt]], mev)
                        setr([pe_, comb], mev)
                chunk_idx += 1
        for tt in range(TG // 128):
            row0 = (g * (TG // 128) + tt) * 128
            m = p.emit("sync", "dma_start", out=out_h[row0:row0 + 128, :],
                       in_=oacc[tt].ap[:], waits=RW([oacc[tt]]),
                       inc=p.mksem(f"doa{tt}"), inc_n=16)
            setr([oacc[tt]], m)

    p.wait_only("sync", [(f"doa{tt}", p.cnt[f"doa{tt}"]) for tt in range(TG // 128)])

    psB.close()
    esB.close()

    with nc.Block() as block:
        @block.sync
        def _(eng):
            p.replay("sync", eng)

        @block.gpsimd
        def _(eng):
            p.replay("gpsimd", eng)

        @block.vector
        def _(eng):
            p.replay("vector", eng)

        @block.scalar
        def _(eng):
            p.replay("scalar", eng)

        @block.tensor
        def _(eng):
            p.replay("tensor", eng)

    es.close()
    return nc


def _host_constants():
    ql = np.arange(128)[:, None]
    kl = np.arange(384)[None, :]
    band = (kl >= ql) & (kl <= ql + 256)
    mk = lambda b: np.where(b, np.float32(0.0), np.float32(NEG)).astype(np.float32)
    bm_f = mk(band & (kl >= 128))
    bm_l = mk(band & (kl < 256))
    rel = (128 + ql - kl).astype(np.float32)
    relm = np.where(band, rel, np.float32(NEG)).astype(np.float32)
    iden = np.eye(128, dtype=ml_dtypes.bfloat16)
    return relm, bm_f, bm_l, iden


def kernel(L0_attn_x, L0_attn_Wq, L0_attn_Wk, L0_attn_Wv, L0_attn_Wo,
           L0_attn_slopes, L0_ffn_Wr, L0_ffn_Wg, L0_ffn_Wu, L0_ffn_Wd):
    x = np.ascontiguousarray(np.asarray(L0_attn_x, dtype=np.float32))
    relm, bm_f, bm_l, iden = _host_constants()
    common = {
        "wq": np.ascontiguousarray(np.asarray(L0_attn_Wq, dtype=np.float32)),
        "wk": np.ascontiguousarray(np.asarray(L0_attn_Wk, dtype=np.float32)),
        "wv": np.ascontiguousarray(np.asarray(L0_attn_Wv, dtype=np.float32)),
        "wo": np.ascontiguousarray(np.asarray(L0_attn_Wo, dtype=np.float32)),
        "slopes": np.ascontiguousarray(np.asarray(L0_attn_slopes, dtype=np.float32)),
        "wr": np.ascontiguousarray(np.asarray(L0_ffn_Wr, dtype=np.float32)),
        "wg": np.ascontiguousarray(np.asarray(L0_ffn_Wg, dtype=np.float32)),
        "wu": np.ascontiguousarray(np.asarray(L0_ffn_Wu, dtype=np.float32)),
        "wd": np.ascontiguousarray(np.asarray(L0_ffn_Wd, dtype=np.float32)),
        "relm": relm,
        "bm_f": bm_f,
        "bm_l": bm_l,
        "iden": iden,
    }
    nc = build_nc()
    in_maps = [dict(common, x=np.ascontiguousarray(x[i])) for i in range(8)]
    res = run_bass_kernel_spmd(nc, in_maps, core_ids=list(range(8)))
    return np.stack([res.results[i]["out"] for i in range(8)], axis=0)
